# version 02
# baseline (speedup 1.0000x reference)
import contextlib
import numpy as np

import concourse.bass as bass
import concourse.mybir as mybir
from concourse.bass_utils import run_bass_kernel_spmd

F32 = mybir.dt.float32
BF16 = mybir.dt.bfloat16
AF = mybir.ActivationFunctionType
ALU = mybir.AluOpType
AX = mybir.AxisListType

MAGIC = float(1.5 * 2 ** 23)
T7 = 10.0 / 12.0
D = 1024
E2 = 2048
NST = 128
I3 = 2816
TOK = 4096
TOKF = 8192
ST2 = 8
ST1 = 16
M = 512
KD = D // 128
KI = I3 // 128

ENGS = ("pe", "dve", "act", "pool", "sync")
DMA_ENGS = ("sync", "act", "pool")


class Buf:

    __slots__ = ("wtok", "rtoks")

    def __init__(self):
        self.wtok = None
        self.rtoks = []


class Sched:

    NDMA = 8

    def __init__(self, nc, sems):
        self.nc = nc
        self.sems = sems
        self.steps = {e: [] for e in ENGS}
        self.gcnt = {e: 0 for e in ENGS}
        self.dgidx = {e: 0 for e in DMA_ENGS}
        self.dcnt = {}
        self.hi = {}

    def _wait(self, eng_name, deps):
        out = []
        for tok in deps:
            if tok is None:
                continue
            semname, val = tok
            if val <= 0:
                continue
            key = (eng_name, semname)
            if semname.startswith("g:") and semname[2:] == eng_name:
                continue
            if self.hi.get(key, 0) >= val:
                continue
            self.hi[key] = val
            out.append((self.sems[semname], val))
        return out

    def _wait_nodedup(self, eng_name, deps, strict=False):
        best = {}
        for tok in deps:
            if tok is None:
                continue
            semname, val = tok
            if val <= 0:
                continue
            if not strict and semname.startswith("g:") and semname[2:] == eng_name:
                continue
            if best.get(semname, 0) < val:
                best[semname] = val
        return [(self.sems[s], v) for s, v in best.items()]

    def emit(self, eng_name, fn, deps=(), reads=(), writes=(), dma_n=0, strict=False):
        deps = list(deps)
        for b in reads:
            deps.append(b.wtok)
        for b in writes:
            deps.append(b.wtok)
            deps.extend(b.rtoks)
        if dma_n:
            rot = self.dgidx[eng_name] % Sched.NDMA
            self.dgidx[eng_name] += 1
            semname = f"d:{eng_name}:{rot}"
            prev = self.dcnt.get((eng_name, rot), 0)
            if prev > 0:
                deps.append((semname, prev))
            new_total = prev + 16 * dma_n
            self.dcnt[(eng_name, rot)] = new_total
            waits = self._wait_nodedup(eng_name, deps)
            tok = (semname, new_total)
            sem = self.sems[semname]

            def step(eng, waits=waits, fn=fn, sem=sem):
                for s, v in waits:
                    eng.wait_ge(s, v)
                fn(eng, sem)

        else:
            waits = self._wait_nodedup(eng_name, deps, strict=strict)
            semname = "g:" + eng_name
            self.gcnt[eng_name] += 1
            tok = (semname, self.gcnt[eng_name])
            sem = self.sems[semname]

            def step(eng, waits=waits, fn=fn, sem=sem):
                for s, v in waits:
                    eng.wait_ge(s, v)
                last = fn(eng)
                last.then_inc(sem, 1)

        self.steps[eng_name].append(step)
        for b in reads:
            b.rtoks.append(tok)
        for b in writes:
            b.wtok = tok
            b.rtoks = []
        return tok

    def barrier_tokens(self):
        toks = [("g:" + e, self.gcnt[e]) for e in ENGS]
        toks += [(f"d:{q}:{r}", v) for (q, r), v in self.dcnt.items()]
        return toks


def _emit_quant(S, xn_ap, nelem, nb, out_ap, q, dve="dve", pool="pool",
                 deps=(), reads=(), writes=()):
    x3 = xn_ap.rearrange("p (b w) -> p b w", w=16)
    bmax, rinv, sc12 = q["bmax"], q["rinv"], q["sc12"]
    tF, b1, b2, b3, b4 = q["tF"], q["b1"], q["b2"], q["b3"], q["b4"]

    if nb >= 64:
        def red(v):
            v.reduce_max(bmax[:, :nb], x3, axis=AX.X, apply_absolute_value=True)
            v.tensor_scalar(rinv[:, :nb], bmax[:, :nb], 6e-12, None, ALU.max)
            v.reciprocal(bmax[:, :nb], rinv[:, :nb])
            return v.tensor_scalar(sc12[:, :nb], rinv[:, :nb], 1.0 / 12.0, None, ALU.mult)

        t_red = S.emit(dve, red, deps=deps, reads=list(reads),
                       writes=[q["_bmax"], q["_rinv"], q["_sc12"]])
    else:
        def red_a(v):
            return v.reduce_max(bmax[:, :nb], x3, axis=AX.X, apply_absolute_value=True)
        t_ra = S.emit(dve, red_a, deps=deps, reads=list(reads), writes=[q["_bmax"]])

        def red_b(g):
            return g.tensor_scalar(rinv[:, :nb], bmax[:, :nb], 6e-12, None, ALU.max)
        t_rb = S.emit(pool, red_b, deps=[t_ra], reads=[q["_bmax"]], writes=[q["_rinv"]])

        def red_c(v):
            return v.reciprocal(bmax[:, :nb], rinv[:, :nb])
        t_rc = S.emit(dve, red_c, deps=[t_rb], reads=[q["_rinv"]], writes=[q["_bmax"]])

        def red_d(g):
            return g.tensor_scalar(sc12[:, :nb], rinv[:, :nb], 1.0 / 12.0, None, ALU.mult)
        t_red = S.emit(pool, red_d, deps=[t_rc], reads=[q["_rinv"]], writes=[q["_sc12"]])

    def pmul(g):
        rb = bmax[:, :nb].unsqueeze(2).broadcast_to([128, nb, 16])
        return g.tensor_tensor(q["xnq"][:, :nelem].rearrange("p (b w) -> p b w", w=16),
                               x3, rb, ALU.mult)

    t_xn = S.emit(pool, pmul, deps=[t_red], reads=list(reads) + [q["_bmax"]],
                  writes=[q["_xnq"]])
    xq = q["xnq"][:, :nelem]

    def ladder(v):
        v.tensor_scalar(tF[:, :nelem], xq, 12.0, MAGIC, ALU.mult, ALU.add)
        v.tensor_scalar(b1[:, :nelem], tF[:, :nelem], MAGIC, 4.0, ALU.subtract, ALU.min)
        v.tensor_scalar(tF[:, :nelem], xq, 6.0, MAGIC, ALU.mult, ALU.add)
        v.tensor_scalar(b2[:, :nelem], tF[:, :nelem], MAGIC, 4.0, ALU.subtract, ALU.min)
        v.tensor_scalar(b3[:, :nelem], b1[:, :nelem], -4.0, None, ALU.max)
        v.tensor_scalar(b1[:, :nelem], b2[:, :nelem], 2.0, None, ALU.max)
        v.tensor_scalar(b4[:, :nelem], b2[:, :nelem], -2.0, -4.0, ALU.min, ALU.max)
        v.tensor_tensor(b2[:, :nelem], b1[:, :nelem], b4[:, :nelem], ALU.add)
        v.scalar_tensor_tensor(b1[:, :nelem], b2[:, :nelem], 2.0, b3[:, :nelem],
                               ALU.mult, ALU.add)
        v.tensor_scalar(b3[:, :nelem], xq, T7, 4.0, ALU.is_gt, ALU.mult)
        v.tensor_scalar(b4[:, :nelem], xq, -T7, 4.0, ALU.is_lt, ALU.mult)
        v.tensor_tensor(b2[:, :nelem], b3[:, :nelem], b4[:, :nelem], ALU.subtract)
        return v.tensor_tensor(b3[:, :nelem], b1[:, :nelem], b2[:, :nelem], ALU.add)

    t_lad = S.emit(dve, ladder, deps=[t_xn], reads=[q["_xnq"]],
                   writes=[q["_tF"], q["_b1"], q["_b2"], q["_b3"], q["_b4"]])

    def fmul(g):
        scb = sc12[:, :nb].unsqueeze(2).broadcast_to([128, nb, 16])
        return g.tensor_tensor(out_ap.rearrange("p (b w) -> p b w", w=16),
                               q["b3"][:, :nelem].rearrange("p (b w) -> p b w", w=16),
                               scb, ALU.mult)

    return S.emit(pool, fmul, deps=[t_lad], reads=[q["_b3"], q["_sc12"]],
                  writes=list(writes))


def build_nc(debug=False, st2_limit=ST2, serialize=False):
    nc = bass.Bass("TRN2", target_bir_lowering=False)

    xT = nc.dram_tensor("xT", [D, TOKF], F32, kind="ExternalInput").ap()
    x_tm = nc.dram_tensor("x_tm", [TOK, D], F32, kind="ExternalInput").ap()
    WiuT = nc.dram_tensor("WiuT", [D, D], F32, kind="ExternalInput").ap()
    WigT = nc.dram_tensor("WigT", [D, D], F32, kind="ExternalInput").ap()
    WstT = nc.dram_tensor("WstT", [D, NST], F32, kind="ExternalInput").ap()
    WdtT = nc.dram_tensor("WdtT", [D, NST], F32, kind="ExternalInput").ap()
    WcT = nc.dram_tensor("WcT", [NST, D], F32, kind="ExternalInput").ap()
    WoT = nc.dram_tensor("WoT", [D, D], F32, kind="ExternalInput").ap()
    wg = nc.dram_tensor("wg", [I3, D], F32, kind="ExternalInput").ap()
    wu = nc.dram_tensor("wu", [I3, D], F32, kind="ExternalInput").ap()
    wd = nc.dram_tensor("wd", [D, I3], F32, kind="ExternalInput").ap()
    bdt = nc.dram_tensor("bdt", [NST, 1], F32, kind="ExternalInput").ap()
    alog = nc.dram_tensor("alog", [NST, 1], F32, kind="ExternalInput").ap()
    dvec = nc.dram_tensor("dvec", [1, D], F32, kind="ExternalInput").ap()
    identI = nc.dram_tensor("identI", [128, 128], BF16, kind="ExternalInput").ap()
    cmask = nc.dram_tensor("cmask", [128, ST1], F32, kind="ExternalInput").ap()
    out = nc.dram_tensor("out", [TOK, D], F32, kind="ExternalOutput").ap()
    wqgT = nc.dram_tensor("wqgT", [D, I3], BF16).ap()
    wquT = nc.dram_tensor("wquT", [D, I3], BF16).ap()
    wqdT = nc.dram_tensor("wqdT", [I3, D], BF16).ap()

    dbg = {}
    if debug:
        dbg["l"] = nc.dram_tensor("dbg_l", [NST, TOK], BF16, kind="ExternalOutput").ap()
        dbg["mcols"] = nc.dram_tensor("dbg_mcols", [128, 32], F32, kind="ExternalOutput").ap()
        dbg["icols"] = nc.dram_tensor("dbg_icols", [128, 32], F32, kind="ExternalOutput").ap()
        dbg["wqg"] = nc.dram_tensor("dbg_wqg", [D, I3], BF16, kind="ExternalOutput").ap()
        dbg["s"] = nc.dram_tensor("dbg_s", [TOK, D], F32, kind="ExternalOutput").ap()
        dbg["xn1"] = nc.dram_tensor("dbg_xn1", [128, KD, M], BF16, kind="ExternalOutput").ap()
        dbg["u"] = nc.dram_tensor("dbg_u", [128, KD, M], BF16, kind="ExternalOutput").ap()
        dbg["dec"] = nc.dram_tensor("dbg_dec", [NST, M], F32, kind="ExternalOutput").ap()
        dbg["ef"] = nc.dram_tensor("dbg_ef", [NST, M], F32, kind="ExternalOutput").ap()
        dbg["xn2"] = nc.dram_tensor("dbg_xn2", [128, 4, D], F32, kind="ExternalOutput").ap()
        dbg["dt"] = nc.dram_tensor("dbg_dt", [NST, M], F32, kind="ExternalOutput").ap()
        dbg["bt"] = nc.dram_tensor("dbg_bt", [NST, M], F32, kind="ExternalOutput").ap()
        dbg["rate"] = nc.dram_tensor("dbg_rate", [NST, 1], F32, kind="ExternalOutput").ap()
        dbg["hqfm"] = nc.dram_tensor("dbg_hqfm", [128, KI, M], BF16, kind="ExternalOutput").ap()
        dbg["xqfm"] = nc.dram_tensor("dbg_xqfm", [128, KD, M], BF16, kind="ExternalOutput").ap()
        dbg["sgtm"] = nc.dram_tensor("dbg_sgtm", [128, 4, D], BF16, kind="ExternalOutput").ap()
        dbg["fps"] = nc.dram_tensor("dbg_fps", [128, 512], F32, kind="ExternalOutput").ap()
        dbg["f2"] = nc.dram_tensor("dbg_f2", [128, 512], F32, kind="ExternalOutput").ap()
        dbg["wqd"] = nc.dram_tensor("dbg_wqd", [I3, D], BF16, kind="ExternalOutput").ap()
        dbg["gsc"] = nc.dram_tensor("dbg_gsc", [128, 512], F32, kind="ExternalOutput").ap()
        dbg["outb"] = nc.dram_tensor("dbg_outb", [128, 512], F32, kind="ExternalOutput").ap()
        dbg["mvc"] = nc.dram_tensor("dbg_mvc", [128, 8], F32, kind="ExternalOutput").ap()
        dbg["ipc"] = nc.dram_tensor("dbg_ipc", [128, 64], F32, kind="ExternalOutput").ap()
        dbg["xnf"] = nc.dram_tensor("dbg_xnf", [128, 4, D], F32, kind="ExternalOutput").ap()
        dbg["xq"] = nc.dram_tensor("dbg_xq", [TOK, D], BF16, kind="ExternalOutput").ap()

    semnames = ["g:" + e for e in ENGS] + [
        f"d:{q}:{r}" for q in DMA_ENGS for r in range(Sched.NDMA)]

    with contextlib.ExitStack() as ctx:
        sems = {n: ctx.enter_context(nc.semaphore(n.replace(":", "_"))) for n in semnames}
        S = Sched(nc, sems)
        B = lambda: Buf()

        sb = lambda name, shape, dt: ctx.enter_context(nc.sbuf_tensor(name, shape, dt))
        ps = lambda name, shape: ctx.enter_context(nc.psum_tensor(name, shape, F32))

        l_b = sb("l_b", [NST, TOK], BF16)
        ident_sb = sb("ident_sb", [128, 128], BF16)
        ident1 = sb("ident1", [1, 1], F32)
        ones_col = sb("ones_col", [128, 1], BF16)
        ones_row = sb("ones_row", [1, 128], F32)
        bdt_sb = sb("bdt_sb", [NST, 1], F32)
        rate_sb = sb("rate_sb", [NST, 1], F32)
        dbc = sb("dbc", [128, D], F32)
        cmask_sb = sb("cmask_sb", [128, ST1], F32)
        mcols = sb("mcols", [128, 32], F32)
        icols = sb("icols", [128, 32], F32)
        wing_sb = sb("wing_sb", [128, KD, D], BF16)
        wct_sb = sb("wct_sb", [NST, D], BF16)
        wout_sb = sb("wout_sb", [128, KD, D], BF16)
        QF = 1024
        q = {"bmax": sb("q_bmax", [128, QF // 16], F32),
             "rinv": sb("q_rinv", [128, QF // 16], F32),
             "sc12": sb("q_sc12", [128, QF // 16], F32),
             "xnq": sb("q_xnq", [128, QF], F32),
             "tF": sb("q_tF", [128, QF], F32),
             "b1": sb("q_b1", [128, QF], BF16),
             "b2": sb("q_b2", [128, QF], BF16),
             "b3": sb("q_b3", [128, QF], BF16),
             "b4": sb("q_b4", [128, QF], BF16)}
        for k in ("bmax", "rinv", "sc12", "xnq", "tF", "b1", "b2", "b3", "b4"):
            q["_" + k] = B()

        P = [ps(f"ps{i}", [128, 512]) for i in range(8)]
        PB = [B() for _ in range(8)]

        b_l = B(); b_mcols = B(); b_icols = B(); b_consts = B()
        b_wing = B(); b_wct = B(); b_wout = B()
        b_wqg = B(); b_wqu = B(); b_wqd = B()

        def fn_consts(e, sem):
            e.dma_start(out=ident_sb[:], in_=identI).then_inc(sem, 16)
            e.dma_start(out=bdt_sb[:], in_=bdt).then_inc(sem, 16)
            e.dma_start(out=rate_sb[:], in_=alog).then_inc(sem, 16)
            e.dma_start(out=cmask_sb[:], in_=cmask).then_inc(sem, 16)
            e.dma_start(out=dbc[0:1, :], in_=dvec).then_inc(sem, 16)
        t_c0 = S.emit("sync", fn_consts, dma_n=5, writes=[b_consts])

        def fn_memset(v):
            v.memset(ones_col[:], 1.0)
            v.memset(ones_row[:], 1.0)
            return v.memset(ident1[:], 1.0)
        t_ms = S.emit("dve", fn_memset, writes=[b_consts])

        def fn_table1(a):
            return a.preload_act_table(AF.Ln)
        from concourse.hw_specs import get_activation_tables
        _tabs = get_activation_tables(nc.m.arch)
        _explnset = None
        for _i, (_n, _fs) in enumerate(_tabs.items()):
            if AF.Exp in _fs and AF.Ln in _fs:
                _explnset = _i
                break
        _siluset = None
        for _i, (_n, _fs) in enumerate(_tabs.items()):
            if AF.Silu in _fs:
                _siluset = _i
                break
        _sqrtset = None
        for _i, (_n, _fs) in enumerate(_tabs.items()):
            if AF.Sqrt in _fs:
                _sqrtset = _i
                break

        def fn_tab_expln(a):
            return a.add_instruction(mybir.InstLoadActFuncSet(
                name=nc.get_next_instruction_name(), act_func_set_id=_explnset,
                ins=[], outs=[]))
        t_tab1 = S.emit("act", fn_tab_expln)

        rt1 = sb("rt1", [NST, 1], F32); rt2 = sb("rt2", [NST, 1], F32)
        b_rate = B()

        def fn_rate(a):
            a.activation(rt1[:], rate_sb[:], AF.Abs)
            a.activation(rt1[:], rt1[:], AF.Exp, scale=-1.0)
            a.activation(rt1[:], rt1[:], AF.Ln, bias=1.0)
            return a.activation(rt2[:], rate_sb[:], AF.Relu)
        t_r1 = S.emit("act", fn_rate, deps=[t_c0, t_tab1], reads=[b_consts], writes=[b_rate])

        def fn_rate2(v):
            return v.scalar_tensor_tensor(rate_sb[:], rt1[:], 1e-4, rt2[:],
                                          ALU.add, ALU.add)
        t_rate = S.emit("dve", fn_rate2, deps=[t_r1], writes=[b_rate])

        b_dbc = B()

        def fn_dbc_mm(t):
            t.matmul(P[6][:, 0:512], ones_row[:], dbc[0:1, 0:512], start=True, stop=True)
            return t.matmul(P[7][:, 0:512], ones_row[:], dbc[0:1, 512:1024], start=True, stop=True)
        t_dbm = S.emit("pe", fn_dbc_mm, deps=[t_c0, t_ms], reads=[b_consts],
                       writes=[PB[6], PB[7]])

        def fn_dbc_ev(a):
            a.activation(dbc[:, 0:512], P[6][:, 0:512], AF.Copy)
            return a.activation(dbc[:, 512:1024], P[7][:, 0:512], AF.Copy)
        t_dbc = S.emit("act", fn_dbc_ev, deps=[t_dbm, t_tab1], reads=[PB[6], PB[7]], writes=[b_dbc])

        def fn_w2(e, sem):
            e.dma_start(out=wing_sb[:], in_=WigT.rearrange("(k p) e -> p k e", p=128)).then_inc(sem, 16)
            e.dma_start(out=wct_sb[:], in_=WcT).then_inc(sem, 16)
            e.dma_start(out=wout_sb[:], in_=WoT.rearrange("(k p) e -> p k e", p=128)).then_inc(sem, 16)
        t_w2 = S.emit("pool", fn_w2, dma_n=3, writes=[b_wing, b_wct, b_wout])

        with contextlib.ExitStack() as prep_ctx:
            stg = prep_ctx.enter_context(nc.sbuf_tensor("stg", [128, KI * D], BF16))
            wrow = prep_ctx.enter_context(nc.sbuf_tensor("wrow", [128, I3], F32))
            wqrow = prep_ctx.enter_context(nc.sbuf_tensor("wqrow", [128, I3], BF16))
            b_stg = [B() for _ in range(KI)]
            b_wrow = B(); b_wqrow = B()

            def prep_weight(src, dst, rows, cols, b_dst):
                nrt = rows // 128
                nct = cols // 128
                for rt in range(nrt):
                    def fn_ld(e, sem, rt=rt):
                        e.dma_start(out=wrow[:, 0:cols],
                                    in_=src[rt * 128:(rt + 1) * 128, :]).then_inc(sem, 16)
                    t_ld = S.emit("sync", fn_ld, dma_n=1, writes=[b_wrow])
                    toks = []
                    for off in range(0, cols, QF):
                        w = min(QF, cols - off)
                        tq = _emit_quant(S, wrow[:, off:off + w], w, w // 16,
                                         wqrow[:, off:off + w], q,
                                         deps=[t_ld], reads=[b_wrow], writes=[b_wqrow])
                        toks.append(tq)
                    for cc in range(nct):
                        pslot = 6 + (cc % 2)
                        def fn_tr(t, cc=cc, pslot=pslot):
                            return t.matmul(P[pslot][:, 0:128],
                                            wqrow[:, cc * 128:(cc + 1) * 128],
                                            ident_sb[:], start=True, stop=True)
                        t_tr = S.emit("pe", fn_tr, deps=toks + [t_c0],
                                      reads=[b_wqrow, b_consts], writes=[PB[pslot]])
                        def fn_ev(a, cc=cc, rt=rt, pslot=pslot, rows=rows):
                            return a.activation(stg[:, cc * rows + rt * 128:cc * rows + (rt + 1) * 128],
                                                P[pslot][:, 0:128], AF.Copy)
                        S.emit("act", fn_ev, deps=[t_tr], reads=[PB[pslot]], writes=[b_stg[cc]])
                for cc in range(nct):
                    def fn_st(e, sem, cc=cc, rows=rows):
                        e.dma_start(out=dst[cc * 128:(cc + 1) * 128, :],
                                    in_=stg[:, cc * rows:(cc + 1) * rows]).then_inc(sem, 16)
                    S.emit("sync", fn_st, dma_n=1, reads=[b_stg[cc]], writes=[b_dst])

            prep_weight(wg, wqgT, I3, D, b_wqg)
            prep_weight(wu, wquT, I3, D, b_wqu)
            prep_weight(wd, wqdT, D, I3, b_wqd)

            if debug:
                def fn_dwq(e, sem):
                    e.dma_start(out=dbg["wqg"], in_=wqgT).then_inc(sem, 16)
                S.emit("sync", fn_dwq, dma_n=1, reads=[b_wqg])

        bar0 = S.barrier_tokens()
        p1 = {}
        with contextlib.ExitStack() as p1ctx:
            sb1 = lambda name, shape, dt: p1ctx.enter_context(nc.sbuf_tensor(name, shape, dt))
            wiu_sb = sb1("wiu_sb", [128, KD, D], BF16)
            wst_sb = sb1("wst_sb", [128, KD, NST], BF16)
            wdt_sb = sb1("wdt_sb", [128, KD, NST], BF16)
            b_wiu = B(); b_wst = B(); b_wdt = B()

            def fn_w1(e, sem):
                e.dma_start(out=wiu_sb[:], in_=WiuT.rearrange("(k p) e -> p k e", p=128)).then_inc(sem, 16)
                e.dma_start(out=wst_sb[:], in_=WstT.rearrange("(k p) n -> p k n", p=128)).then_inc(sem, 16)
                e.dma_start(out=wdt_sb[:], in_=WdtT.rearrange("(k p) n -> p k n", p=128)).then_inc(sem, 16)
            t_w1 = S.emit("pool", fn_w1, dma_n=3, deps=bar0,
                          writes=[b_wiu, b_wst, b_wdt])
            xfm = [sb1(f"xfm{i}", [128, KD, M], BF16) for i in range(2)]
            x2fm = sb1("x2fm", [128, KD, M], BF16)
            xnfm = sb1("xnfm", [128, KD, M], BF16)
            ufm = sb1("ufm", [128, KD, M], BF16)
            tscr = sb1("tscr", [128, M], F32)
            mrow = sb1("mrow", [1, M], F32)
            irow = sb1("irow", [1, M], F32)
            mbs = sb1("mbs", [128, M], F32)
            ibs = sb1("ibs", [128, M], F32)
            dtc = sb1("dtc", [128, M], F32)
            dtc2 = sb1("dtc2", [128, M], F32)
            dec = sb1("dec", [128, M], F32)
            ef = sb1("ef", [128, M], F32)
            lscr = sb1("lscr", [128, M], F32)
            carry = sb1("carry", [NST, 1], F32)
            b_xfm = [B(), B()]
            b_x2 = B(); b_xn1 = B(); b_u = B(); b_stat = B(); b_bc = B()
            b_dt = B(); b_dec = B(); b_ef = B(); b_l1 = B(); b_carry = B()
            b_tscr = B()

            def fn_carry0(v):
                return v.memset(carry[:], 0.0)
            t_cy = S.emit("dve", fn_carry0, deps=bar0, writes=[b_carry])

            for s in range(ST1):
                xb = xfm[s % 2]
                bxb = b_xfm[s % 2]

                def fn_lx(e, sem, s=s, xb=xb):
                    e.dma_start(out=xb[:],
                                in_=xT.rearrange("(c p) t -> p c t", p=128)[:, :, s * M:(s + 1) * M]
                                ).then_inc(sem, 16)
                t_lx = S.emit("pool", fn_lx, dma_n=1, deps=bar0, writes=[bxb])

                def fn_sq(a, xb=xb):
                    return a.activation(x2fm[:], xb[:], AF.Square)
                t_sq = S.emit("act", fn_sq, deps=[t_lx, t_tab1], reads=[bxb], writes=[b_x2])

                def fn_cs(t, xb=xb):
                    for c in range(KD):
                        t.matmul(P[4][0:1, :], ones_col[:], xb[:, c, :],
                                 start=(c == 0), stop=(c == KD - 1))
                    for c in range(KD):
                        last = t.matmul(P[5][0:1, :], ones_col[:], x2fm[:, c, :],
                                        start=(c == 0), stop=(c == KD - 1))
                    return last
                t_cs = S.emit("pe", fn_cs, deps=[t_lx, t_sq, t_ms], reads=[bxb, b_x2],
                              writes=[PB[4], PB[5]])

                def fn_stat(v):
                    v.tensor_scalar(mrow[:], P[4][0:1, :], 1.0 / D, None, ALU.mult)
                    v.tensor_scalar(irow[:], P[5][0:1, :], 1.0 / D, None, ALU.mult)
                    v.tensor_tensor(tscr[0:1, :], mrow[:], mrow[:], ALU.mult)
                    v.tensor_tensor(irow[:], irow[:], tscr[0:1, :], ALU.subtract)
                    return v.tensor_scalar(irow[:], irow[:], 1e-5, None, ALU.add)
                t_st = S.emit("dve", fn_stat, deps=[t_cs], reads=[PB[4], PB[5]],
                              writes=[b_stat, b_tscr])

                def fn_inv(a):
                    a.activation(irow[:], irow[:], AF.Ln)
                    return a.activation(irow[:], irow[:], AF.Exp, scale=-0.5)
                t_inv = S.emit("act", fn_inv, deps=[t_st], writes=[b_stat])

                def fn_bc(t):
                    t.matmul(P[4][:, :], ones_row[:], mrow[:], start=True, stop=True)
                    return t.matmul(P[5][:, :], ones_row[:], irow[:], start=True, stop=True)
                t_bc = S.emit("pe", fn_bc, deps=[t_inv, t_ms], reads=[b_stat],
                              writes=[PB[4], PB[5]])

                def fn_bc_ev(a):
                    a.activation(mbs[:], P[4][:, :], AF.Copy)
                    return a.activation(ibs[:], P[5][:, :], AF.Copy)
                t_bce = S.emit("act", fn_bc_ev, deps=[t_bc], reads=[PB[4], PB[5]], writes=[b_bc])

                if s >= 8:
                    S2i = s - 8

                    def fn_strm(t, S2i=S2i):
                        for cc in range(4):
                            t.matmul(P[6][:, cc:cc + 1], mrow[0:1, cc * 128:(cc + 1) * 128],
                                     ident1[:], start=True, stop=True)
                        for cc in range(4):
                            last = t.matmul(P[6][:, 4 + cc:5 + cc],
                                            irow[0:1, cc * 128:(cc + 1) * 128],
                                            ident1[:], start=True, stop=True)
                        return last
                    t_str = S.emit("pe", fn_strm, deps=[t_inv, t_ms], reads=[b_stat],
                                   writes=[PB[6]])

                    def fn_stre(a, S2i=S2i):
                        a.activation(mcols[:, S2i * 4:(S2i + 1) * 4], P[6][:, 0:4], AF.Copy)
                        return a.activation(icols[:, S2i * 4:(S2i + 1) * 4], P[6][:, 4:8], AF.Copy)
                    S.emit("act", fn_stre, deps=[t_str], reads=[PB[6]],
                           writes=[b_mcols, b_icols])

                def fn_xn1(v, xb=xb):
                    for c in range(KD):
                        last = v.tensor_tensor(xnfm[:, c, :], xb[:, c, :], mbs[:], ALU.subtract)
                    return last
                t_x1 = S.emit("dve", fn_xn1, deps=[t_bce], reads=[bxb, b_bc], writes=[b_xn1])

                def fn_xn2(g):
                    for c in range(KD):
                        last = g.tensor_tensor(xnfm[:, c, :], xnfm[:, c, :], ibs[:], ALU.mult)
                    return last
                t_x2 = S.emit("pool", fn_xn2, deps=[t_x1], reads=[b_bc], writes=[b_xn1])

                t_uev = []
                for e8 in range(KD):
                    pslot = e8 % 2
                    def fn_mm(t, e8=e8, pslot=pslot):
                        for k in range(KD):
                            last = t.matmul(P[pslot][:, :],
                                            wiu_sb[:, k, e8 * 128:(e8 + 1) * 128],
                                            xnfm[:, k, :], start=(k == 0), stop=(k == KD - 1))
                        return last
                    t_mm = S.emit("pe", fn_mm, deps=[t_x2, t_w1], reads=[b_xn1, b_wiu],
                                  writes=[PB[pslot]])

                    def fn_ev(a, e8=e8, pslot=pslot):
                        return a.activation(ufm[:, e8, :], P[pslot][:, :], AF.Copy)
                    t_uev.append(S.emit("act", fn_ev, deps=[t_mm], reads=[PB[pslot]],
                                        writes=[b_u]))

                def fn_bt2(t):
                    for k in range(KD):
                        t.matmul(P[2][:, :], wst_sb[:, k, :], ufm[:, k, :],
                                 start=(k == 0), stop=(k == KD - 1))
                    for k in range(KD):
                        lb = t.matmul(P[3][:, :], wdt_sb[:, k, :], ufm[:, k, :],
                                      start=(k == 0), stop=(k == KD - 1))
                    return lb
                t_bt = S.emit("pe", fn_bt2, deps=t_uev + [t_w1], reads=[b_u, b_wst, b_wdt],
                              writes=[PB[2], PB[3]])

                def fn_dta(a):
                    a.activation(dtc[:], P[3][:, :], AF.Abs, bias=bdt_sb[:])
                    a.activation(dtc2[:], P[3][:, :], AF.Relu, bias=bdt_sb[:])
                    a.activation(dtc[:], dtc[:], AF.Exp, scale=-1.0)
                    return a.activation(dtc[:], dtc[:], AF.Ln, bias=1.0)
                t_dta = S.emit("act", fn_dta, deps=[t_bt, t_rate, t_c0], reads=[PB[3], b_consts],
                               writes=[b_dt])

                def fn_dtb(v):
                    v.tensor_tensor(dtc[:], dtc[:], dtc2[:], ALU.add)
                    v.tensor_scalar(dtc[:], dtc[:], 10.0, None, ALU.min)
                    return v.tensor_scalar(dtc[:], dtc[:], rate_sb[:], None, ALU.mult)
                t_dtb = S.emit("dve", fn_dtb, deps=[t_dta], reads=[b_rate], writes=[b_dt])

                def fn_dec(a):
                    return a.activation(dec[:], dtc[:], AF.Exp, scale=-1.0)
                t_dec = S.emit("act", fn_dec, deps=[t_dtb], reads=[b_dt], writes=[b_dec])

                def fn_ef(v):
                    v.tensor_scalar(ef[:], dec[:], -1.0, 1.0, ALU.mult, ALU.add)
                    return v.tensor_tensor(ef[:], ef[:], P[2][:, :], ALU.mult)
                t_ef = S.emit("dve", fn_ef, deps=[t_dec, t_bt], reads=[b_dec, PB[2]],
                              writes=[b_ef])

                def fn_scan(v, s=s):
                    return v.tensor_tensor_scan(lscr[:], dec[:], ef[:], carry[:, 0:1],
                                                ALU.mult, ALU.add)
                t_sc = S.emit("dve", fn_scan, deps=[t_ef, t_c0], reads=[b_dec, b_ef, b_carry],
                              writes=[b_l1])

                def fn_cup(g, s=s):
                    return g.tensor_tensor(carry[:], lscr[:, M - 1:M], cmask_sb[:, s:s + 1],
                                           ALU.mult)
                t_cup = S.emit("pool", fn_cup, deps=[t_sc], reads=[b_l1, b_consts],
                               writes=[b_carry])

                if s >= 8:
                    def fn_lst(v, s=s):
                        return v.tensor_copy(l_b[:, (s - 8) * M:(s - 7) * M], lscr[:])
                    S.emit("dve", fn_lst, deps=[t_sc], reads=[b_l1], writes=[b_l])

                if debug and s == 15:
                    bt_d = sb1("bt_d", [NST, M], F32)
                    b_btd = B()
                    def fn_btd(v):
                        return v.tensor_copy(bt_d[:], P[2][:, :])
                    t_btd = S.emit("dve", fn_btd, deps=[t_bt], reads=[PB[2]], writes=[b_btd])
                    def fn_d1(e, sem):
                        e.dma_start(out=dbg["xn1"], in_=xnfm[:]).then_inc(sem, 16)
                        e.dma_start(out=dbg["u"], in_=ufm[:]).then_inc(sem, 16)
                        e.dma_start(out=dbg["dec"], in_=dec[:]).then_inc(sem, 16)
                        e.dma_start(out=dbg["ef"], in_=ef[:]).then_inc(sem, 16)
                        e.dma_start(out=dbg["dt"], in_=dtc[:]).then_inc(sem, 16)
                        e.dma_start(out=dbg["bt"], in_=bt_d[:]).then_inc(sem, 16)
                        e.dma_start(out=dbg["rate"], in_=rate_sb[:]).then_inc(sem, 16)
                    S.emit("sync", fn_d1, dma_n=7,
                           reads=[b_xn1, b_u, b_dec, b_ef, b_dt, b_btd, b_rate])

            if debug:
                def fn_dl(e, sem):
                    e.dma_start(out=dbg["l"], in_=l_b[:]).then_inc(sem, 16)
                S.emit("sync", fn_dl, dma_n=1, reads=[b_l])
                def fn_dm(e, sem):
                    e.dma_start(out=dbg["mcols"], in_=mcols[:]).then_inc(sem, 16)
                    e.dma_start(out=dbg["icols"], in_=icols[:]).then_inc(sem, 16)
                S.emit("sync", fn_dm, dma_n=2, reads=[b_mcols, b_icols])

        bar1 = S.barrier_tokens()

        def fn_tab_silu(a):
            return a.add_instruction(mybir.InstLoadActFuncSet(
                name=nc.get_next_instruction_name(), act_func_set_id=_siluset,
                ins=[], outs=[]))
        t_tab2 = S.emit("act", fn_tab_silu, deps=bar1)

        with contextlib.ExitStack() as p2ctx:
            sb2 = lambda name, shape, dt: p2ctx.enter_context(nc.sbuf_tensor(name, shape, dt))
            xtm_t = [sb2(f"xtm{i}", [128, D], F32) for i in range(2)]
            xntm = sb2("xntm", [128, 4, D], F32)
            xnb_t = sb2("xnb_t", [128, D], BF16)
            xq_tm = sb2("xq_tm", [128, 4 * D], BF16)
            xnfm2 = sb2("xnfm2", [128, KD, M], BF16)
            xqfm = sb2("xqfm", [128, KD, M], BF16)
            sgtm = sb2("sgtm", [128, 4, D], BF16)
            stm = sb2("stm", [128, 4, D], F32)
            ztmp = sb2("ztmp", [128, D], F32)
            xnd = sb2("xnd", [128, D], F32)
            zb_t = xnb_t
            zfm = sb2("zfm", [128, KD, 128], BF16)
            h_all = sb2("h_all", [128, 4 * 512], F32)
            sG = sb2("sG", [128, 512], F32)
            hq_tm = sb2("hq_tm", [128, 4 * 512], BF16)
            hq_fm = sb2("hq_fm", [128, KI, M], BF16)
            wqg6 = sb2("wqg6", [128, KD, 512], BF16)
            wqu6 = sb2("wqu6", [128, KD, 512], BF16)
            wqd_sb = [sb2(f"wqd{i}", [128, 512], BF16) for i in range(2)]
            gsc = sG
            xre = sb2("xre", [128, 512], F32)
            outb = sb2("outb", [128, 512], F32)
            st12 = sb2("st12", [128, 4, 12], F32)
            st12c = sb2("st12c", [128, 4, 12], F32)
            mvagg = sb2("mvagg", [128, 4, 2], F32)
            mvc = sb2("mvc", [128, 4, 2], F32)
            vpad = sb2("vpad", [128, 64], F32)
            ipad = sb2("ipad", [128, 64], F32)
            ipc = sb2("ipc", [128, 64], F32)
            b_st12 = B(); b_st12c = B(); b_mv = B(); b_mvc = B()
            b_vpad = B(); b_ipad = B(); b_ipc = B()
            fdbg = xre

            b_xtm = [B(), B()]
            b_xntm = B(); b_xnb = B(); b_xqtm = B(); b_xnfm2 = B(); b_xqfm = B()
            b_z = b_xnb
            b_sg = B(); b_stm = B(); b_zfm = B(); b_hall = B(); b_sG = B()
            b_hqtm = B(); b_hqfm = B(); b_wg6 = B(); b_wu6 = B()
            b_wqdb = [B(), B()]
            b_gsc = b_sG; b_xre = B(); b_outb = B(); b_ztmp = B(); b_xnd = B()
            b_fdbg = b_xre
            b_out = B()

            SIX = [(0, 512), (512, 512), (1024, 512), (1536, 512), (2048, 512), (2560, 256)]

            t_out_last = None
            for S2 in range(st2_limit):
                T0 = S2 * M
                if serialize:
                    bar1 = S.barrier_tokens()
                t_sub = []
                for j in range(4):
                    xb = xtm_t[j % 2]; bxb = b_xtm[j % 2]

                    def fn_lx(e, sem, j=j, xb=xb, T0=T0):
                        e.dma_start(out=xb[:], in_=x_tm[T0 + j * 128:T0 + (j + 1) * 128, :]
                                    ).then_inc(sem, 16)
                    t_lx = S.emit("sync", fn_lx, dma_n=1, deps=bar1, writes=[bxb])

                    def fn_bn(v, j=j, xb=xb):
                        v.bn_stats(st12[:, j, 0:6], xb[:, 0:512])
                        return v.bn_stats(st12[:, j, 6:12], xb[:, 512:1024])
                    t_bn = S.emit("dve", fn_bn, deps=[t_lx] + bar1, reads=[bxb],
                                  writes=[b_st12])

                    def fn_stc(a, j=j):
                        return a.activation(st12c[:, j, :], st12[:, j, :], AF.Copy)
                    t_stc = S.emit("act", fn_stc, deps=[t_bn], reads=[b_st12],
                                   writes=[b_st12c])

                    def fn_ag(v, j=j):
                        return v.bn_aggr(mvagg[:, j, :], st12c[:, j, :])
                    t_ag = S.emit("dve", fn_ag, deps=[t_stc], reads=[b_st12c],
                                  writes=[b_mv])

                    def fn_mvc(a, j=j):
                        return a.activation(mvc[:, j, :], mvagg[:, j, :], AF.Copy)
                    t_mvc = S.emit("act", fn_mvc, deps=[t_ag], reads=[b_mv], writes=[b_mvc])

                    def fn_sub(v, j=j, xb=xb):
                        return v.tensor_scalar(xntm[:, j, :], xb[:], mvc[:, j, 0:1], None,
                                               ALU.subtract)
                    t_sub.append(S.emit("dve", fn_sub, deps=[t_mvc, t_lx],
                                        reads=[bxb, b_mvc], writes=[b_xntm]))

                def fn_gather(v):
                    return v.tensor_scalar(vpad[:, 0:4],
                                           mvc[:].rearrange("p a b -> p (a b)")[:, 1:8:2],
                                           1e-5, None, ALU.add)
                t_ga = S.emit("dve", fn_gather, reads=[b_mvc], writes=[b_vpad])

                def fn_sqrtw(a):
                    a.add_instruction(mybir.InstLoadActFuncSet(
                        name=nc.get_next_instruction_name(), act_func_set_id=_sqrtset,
                        ins=[], outs=[]))
                    a.activation(vpad[:], vpad[:], AF.Sqrt)
                    return a.add_instruction(mybir.InstLoadActFuncSet(
                        name=nc.get_next_instruction_name(), act_func_set_id=_siluset,
                        ins=[], outs=[]))
                t_sq2 = S.emit("act", fn_sqrtw, deps=[t_ga], reads=[b_vpad], writes=[b_vpad])

                def fn_rcp(v):
                    return v.reciprocal(ipad[:], vpad[:])
                t_rcp = S.emit("dve", fn_rcp, deps=[t_sq2], reads=[b_vpad], writes=[b_ipad])

                def fn_ipc(a):
                    return a.activation(ipc[:], ipad[:], AF.Copy)
                t_ipc = S.emit("act", fn_ipc, deps=[t_rcp], reads=[b_ipad], writes=[b_ipc])

                t_xnA = []
                for j in range(4):
                    def fn_xn(v, j=j):
                        return v.tensor_scalar(xntm[:, j, :], xntm[:, j, :], ipc[:, j:j + 1],
                                               None, ALU.mult)
                    t_xn = S.emit("dve", fn_xn, deps=[t_ipc, t_sub[j]] + bar1,
                                  reads=[b_ipc], writes=[b_xntm])
                    t_xnA.append(t_xn)

                    def fn_cast(v, j=j):
                        return v.tensor_copy(xnb_t[:], xntm[:, j, :])
                    t_cb = S.emit("dve", fn_cast, deps=[t_xn], reads=[b_xntm], writes=[b_xnb])

                    def fn_trx(t, j=j):
                        for c in range(KD):
                            pp = P[6] if c < 4 else P[7]
                            last = t.matmul(pp[:, (c % 4) * 128:(c % 4 + 1) * 128],
                                            xnb_t[:, c * 128:(c + 1) * 128], ident_sb[:],
                                            start=True, stop=True)
                        return last
                    t_trx = S.emit("pe", fn_trx, deps=[t_cb], reads=[b_xnb, b_consts],
                                   writes=[PB[6], PB[7]])

                    def fn_evx(a, j=j):
                        for c in range(KD):
                            pp = P[6] if c < 4 else P[7]
                            last = a.activation(xnfm2[:, c, j * 128:(j + 1) * 128],
                                                pp[:, (c % 4) * 128:(c % 4 + 1) * 128], AF.Copy)
                        return last
                    S.emit("act", fn_evx, deps=[t_trx, t_tab2], reads=[PB[6], PB[7]],
                           writes=[b_xnfm2])

                for hh in range(4):
                    tq = _emit_quant(S, xntm[:, hh, :], 1024, 64,
                                     xq_tm[:, hh * D:(hh + 1) * D], q,
                                     deps=[t_xnA[hh]],
                                     reads=[b_xntm], writes=[b_xqtm])
                for j in range(4):
                    def fn_trq(t, j=j):
                        for c in range(KD):
                            pp = P[6] if c < 4 else P[7]
                            last = t.matmul(pp[:, (c % 4) * 128:(c % 4 + 1) * 128],
                                            xq_tm[:, j * D + c * 128:j * D + (c + 1) * 128],
                                            ident_sb[:], start=True, stop=True)
                        return last
                    t_trq = S.emit("pe", fn_trq, deps=[tq], reads=[b_xqtm, b_consts],
                                   writes=[PB[6], PB[7]])

                    def fn_evq(a, j=j):
                        for c in range(KD):
                            pp = P[6] if c < 4 else P[7]
                            last = a.activation(xqfm[:, c, j * 128:(j + 1) * 128],
                                                pp[:, (c % 4) * 128:(c % 4 + 1) * 128], AF.Copy)
                        return last
                    S.emit("act", fn_evq, deps=[t_trq], reads=[PB[6], PB[7]], writes=[b_xqfm])

                for j in range(4):
                    def fn_g(t, j=j):
                        for k in range(KD):
                            t.matmul(P[0][:, :], xnfm2[:, k, j * 128:(j + 1) * 128],
                                     wing_sb[:, k, 0:512], start=(k == 0), stop=(k == KD - 1))
                        for k in range(KD):
                            last = t.matmul(P[1][:, :], xnfm2[:, k, j * 128:(j + 1) * 128],
                                            wing_sb[:, k, 512:1024], start=(k == 0),
                                            stop=(k == KD - 1))
                        return last
                    t_g = S.emit("pe", fn_g, deps=[t_w2], reads=[b_xnfm2, b_wing],
                                 writes=[PB[0], PB[1]])

                    def fn_sg(a, j=j):
                        a.activation(sgtm[:, j, 0:512], P[0][:, :], AF.Silu)
                        return a.activation(sgtm[:, j, 512:1024], P[1][:, :], AF.Silu)
                    t_sg = S.emit("act", fn_sg, deps=[t_g, t_tab2], reads=[PB[0], PB[1]],
                                  writes=[b_sg])

                    def fn_y(t, j=j, T0=T0):
                        t.matmul(P[2][:, :], l_b[:, T0 + j * 128:T0 + (j + 1) * 128],
                                 wct_sb[:, 0:512], start=True, stop=True)
                        return t.matmul(P[3][:, :], l_b[:, T0 + j * 128:T0 + (j + 1) * 128],
                                        wct_sb[:, 512:1024], start=True, stop=True)
                    t_y = S.emit("pe", fn_y, deps=[t_w2] + bar1, reads=[b_l, b_wct],
                                 writes=[PB[2], PB[3]])

                    def fn_xnd(g, j=j):
                        return g.tensor_tensor(xnd[:], xntm[:, j, :], dbc[:], ALU.mult)
                    t_xd = S.emit("pool", fn_xnd, deps=[t_dbc] + bar1, reads=[b_xntm, b_dbc],
                                  writes=[b_xnd])

                    def fn_z1(v, j=j):
                        v.tensor_tensor(ztmp[:, 0:512], sgtm[:, j, 0:512], P[2][:, :], ALU.mult)
                        v.tensor_tensor(ztmp[:, 512:1024], sgtm[:, j, 512:1024], P[3][:, :],
                                        ALU.mult)
                        return v.tensor_tensor(zb_t[:], ztmp[:], xnd[:], ALU.add)
                    t_z = S.emit("dve", fn_z1, deps=[t_sg, t_y, t_xd],
                                 reads=[b_sg, PB[2], PB[3], b_xnd], writes=[b_z, b_ztmp])

                    def fn_trz(t, j=j):
                        for c in range(KD):
                            pp = P[6] if c < 4 else P[7]
                            last = t.matmul(pp[:, (c % 4) * 128:(c % 4 + 1) * 128],
                                            zb_t[:, c * 128:(c + 1) * 128], ident_sb[:],
                                            start=True, stop=True)
                        return last
                    t_trz = S.emit("pe", fn_trz, deps=[t_z], reads=[b_z, b_consts],
                                   writes=[PB[6], PB[7]])

                    def fn_evz(a, j=j):
                        for c in range(KD):
                            pp = P[6] if c < 4 else P[7]
                            last = a.activation(zfm[:, c, :],
                                                pp[:, (c % 4) * 128:(c % 4 + 1) * 128], AF.Copy)
                        return last
                    t_evz = S.emit("act", fn_evz, deps=[t_trz], reads=[PB[6], PB[7]],
                                   writes=[b_zfm])

                    def fn_s(t, j=j):
                        for k in range(KD):
                            t.matmul(P[4][:, :], zfm[:, k, :], wout_sb[:, k, 0:512],
                                     start=(k == 0), stop=(k == KD - 1))
                        for k in range(KD):
                            last = t.matmul(P[5][:, :], zfm[:, k, :], wout_sb[:, k, 512:1024],
                                            start=(k == 0), stop=(k == KD - 1))
                        return last
                    t_s = S.emit("pe", fn_s, deps=[t_evz, t_w2], reads=[b_zfm, b_wout],
                                 writes=[PB[4], PB[5]])

                    def fn_evs(a, j=j):
                        a.activation(stm[:, j, 0:512], P[4][:, :], AF.Copy)
                        return a.activation(stm[:, j, 512:1024], P[5][:, :], AF.Copy)
                    S.emit("act", fn_evs, deps=[t_s], reads=[PB[4], PB[5]], writes=[b_stm])

                if debug:
                    if S2 == 0:
                        def fn_dxn2(e, sem):
                            e.dma_start(out=dbg["xn2"], in_=xntm[:]).then_inc(sem, 16)
                        S.emit("sync", fn_dxn2, dma_n=1, reads=[b_xntm])
                    def fn_ds(e, sem, T0=T0):
                        e.dma_start(out=dbg["s"][T0:T0 + M, :].rearrange("(a p) b -> p a b", p=128),
                                    in_=stm[:]).then_inc(sem, 16)
                    S.emit("sync", fn_ds, dma_n=1, reads=[b_stm])
                    def fn_dxq(e, sem, T0=T0):
                        e.dma_start(out=dbg["xq"][T0:T0 + M, :].rearrange("(a p) b -> p a b", p=128),
                                    in_=xq_tm[:].rearrange("p (a b) -> p a b", a=4)).then_inc(sem, 16)
                    S.emit("sync", fn_dxq, dma_n=1, reads=[b_xqtm])

                for si, (e0, ew) in enumerate(SIX):
                    def fn_lw6(e, sem, e0=e0, ew=ew):
                        e.dma_start(out=wqg6[:, :, 0:ew],
                                    in_=wqgT.rearrange("(k p) e -> p k e", p=128)[:, :, e0:e0 + ew]
                                    ).then_inc(sem, 16)
                        e.dma_start(out=wqu6[:, :, 0:ew],
                                    in_=wquT.rearrange("(k p) e -> p k e", p=128)[:, :, e0:e0 + ew]
                                    ).then_inc(sem, 16)
                    t_lw = S.emit("act", fn_lw6, dma_n=2,
                                  deps=[b_wqg.wtok, b_wqu.wtok] + bar1,
                                  writes=[b_wg6, b_wu6])

                    t_hj = []
                    for j in range(4):
                        def fn_gu(t, j=j, ew=ew):
                            for k in range(KD):
                                t.matmul(P[0][:, 0:ew], xqfm[:, k, j * 128:(j + 1) * 128],
                                         wqg6[:, k, 0:ew], start=(k == 0), stop=(k == KD - 1))
                            for k in range(KD):
                                last = t.matmul(P[1][:, 0:ew], xqfm[:, k, j * 128:(j + 1) * 128],
                                                wqu6[:, k, 0:ew], start=(k == 0),
                                                stop=(k == KD - 1))
                            return last
                        t_gu = S.emit("pe", fn_gu, deps=[t_lw], reads=[b_xqfm, b_wg6, b_wu6],
                                      writes=[PB[0], PB[1]])

                        def fn_sG(a, j=j, ew=ew):
                            return a.activation(sG[:, 0:ew], P[0][:, 0:ew], AF.Silu)
                        t_sG = S.emit("act", fn_sG, deps=[t_gu, t_tab2], reads=[PB[0]],
                                      writes=[b_sG])

                        def fn_h(v, j=j, ew=ew):
                            return v.tensor_tensor(h_all[:, j * ew:(j + 1) * ew], sG[:, 0:ew],
                                                   P[1][:, 0:ew], ALU.mult)
                        t_h = S.emit("dve", fn_h, deps=[t_sG], reads=[b_sG, PB[1]],
                                     writes=[b_hall])
                        t_hj.append(t_h)

                    tqh = None
                    for off in range(0, 4 * ew, 1024):
                        w6 = min(1024, 4 * ew - off)
                        tqh = _emit_quant(S, h_all[:, off:off + w6], w6, w6 // 16,
                                          hq_tm[:, off:off + w6], q, deps=t_hj,
                                          reads=[b_hall], writes=[b_hqtm])

                    nct = ew // 128
                    for j in range(4):
                        def fn_trh(t, j=j, ew=ew, nct=nct):
                            for c in range(nct):
                                last = t.matmul(P[6][:, c * 128:(c + 1) * 128],
                                                hq_tm[:, j * ew + c * 128:j * ew + (c + 1) * 128],
                                                ident_sb[:], start=True, stop=True)
                            return last
                        t_trh = S.emit("pe", fn_trh, deps=[tqh], reads=[b_hqtm, b_consts],
                                       writes=[PB[6]])

                        def fn_evh(a, j=j, e0=e0, nct=nct):
                            for c in range(nct):
                                last = a.activation(hq_fm[:, e0 // 128 + c, j * 128:(j + 1) * 128],
                                                    P[6][:, c * 128:(c + 1) * 128], AF.Copy)
                            return last
                        S.emit("act", fn_evh, deps=[t_trh], reads=[PB[6]], writes=[b_hqfm])

                for dh in range(2):
                    t_fp = [None] * 4
                    for k in range(KI):
                        wb = wqd_sb[k % 2]; bwb = b_wqdb[k % 2]

                        def fn_lwd(e, sem, k=k, dh=dh, wb=wb):
                            e.dma_start(out=wb[:],
                                        in_=wqdT[k * 128:(k + 1) * 128, dh * 512:(dh + 1) * 512]
                                        ).then_inc(sem, 16)
                        t_lwd = S.emit("act", fn_lwd, dma_n=1, deps=[b_wqd.wtok] + bar1,
                                       writes=[bwb])

                        for j in range(4):
                            def fn_f(t, k=k, j=j, wb=wb):
                                return t.matmul(P[j][:, :], hq_fm[:, k, j * 128:(j + 1) * 128],
                                                wb[:], start=(k == 0), stop=(k == KI - 1))
                            t_fp[j] = S.emit("pe", fn_f, deps=[t_lwd], reads=[b_hqfm, bwb],
                                             writes=[PB[j]])

                    for j in range(4):
                        tap = []
                        if debug and S2 == st2_limit - 1 and dh == 1 and j == 3:
                            def fn_fd(v, j=j):
                                return v.tensor_copy(fdbg[:], P[j][:, :])
                            t_fd0 = S.emit("dve", fn_fd, deps=[t_fp[j]], reads=[PB[j]],
                                           writes=[b_fdbg])
                            tap.append(t_fd0)
                            def fn_fdd(e, sem):
                                e.dma_start(out=dbg["fps"], in_=fdbg[:]).then_inc(sem, 16)
                            S.emit("sync", fn_fdd, dma_n=1, deps=[t_fd0], reads=[b_fdbg])
                        def fn_gsc(a, j=j):
                            return a.activation(gsc[:], P[j][:, :], AF.Silu)
                        t_gs = S.emit("act", fn_gsc, deps=[t_fp[j], t_tab2] + tap,
                                      reads=[PB[j]], writes=[b_gsc])

                        def fn_lx2(e, sem, j=j, dh=dh, T0=T0):
                            e.dma_start(out=xre[:],
                                        in_=x_tm[T0 + j * 128:T0 + (j + 1) * 128,
                                                 dh * 512:(dh + 1) * 512]).then_inc(sem, 16)
                        t_x2l = S.emit("sync", fn_lx2, dma_n=1, deps=bar1, writes=[b_xre])

                        def fn_comb(v, j=j, dh=dh):
                            v.tensor_tensor(outb[:], gsc[:], stm[:, j, dh * 512:(dh + 1) * 512],
                                            ALU.mult)
                            return v.tensor_tensor(outb[:], outb[:], xre[:], ALU.add)
                        t_cb2 = S.emit("dve", fn_comb, deps=[t_gs, t_x2l],
                                       reads=[b_gsc, b_stm, b_xre], writes=[b_outb])

                        def fn_so(e, sem, j=j, dh=dh, T0=T0):
                            e.dma_start(out=out[T0 + j * 128:T0 + (j + 1) * 128,
                                                dh * 512:(dh + 1) * 512],
                                        in_=outb[:]).then_inc(sem, 16)
                        t_out_last = S.emit("sync", fn_so, dma_n=1, reads=[b_outb],
                                            writes=[b_out])

            if debug:
                bar_end = S.barrier_tokens()
                f2w = wqd_sb[0]
                t_prev = None
                for k in range(KI):
                    def fn_l2(e, sem, k=k):
                        e.dma_start(out=f2w[:],
                                    in_=wqdT[k * 128:(k + 1) * 128, 512:1024]).then_inc(sem, 16)
                    t_l2 = S.emit("sync", fn_l2, dma_n=1, deps=bar_end, writes=[b_wqdb[0]])
                    def fn_m2(t, k=k):
                        return t.matmul(P[5][:, :], hq_fm[:, k, 3 * 128:4 * 128], f2w[:],
                                        start=(k == 0), stop=(k == KI - 1))
                    t_prev = S.emit("pe", fn_m2, deps=[t_l2] + bar_end,
                                    reads=[b_hqfm, b_wqdb[0]], writes=[PB[5]])
                def fn_e2(v):
                    return v.tensor_copy(fdbg[:], P[5][:, :])
                t_e2 = S.emit("dve", fn_e2, deps=[t_prev], reads=[PB[5]], writes=[b_fdbg])
                def fn_d2(e, sem):
                    e.dma_start(out=dbg["f2"], in_=fdbg[:]).then_inc(sem, 16)
                    e.dma_start(out=dbg["wqd"], in_=wqdT).then_inc(sem, 16)
                S.emit("sync", fn_d2, dma_n=2, deps=[t_e2], reads=[b_fdbg, b_wqd])

                def fn_dfin(e, sem):
                    e.dma_start(out=dbg["hqfm"], in_=hq_fm[:]).then_inc(sem, 16)
                    e.dma_start(out=dbg["xqfm"], in_=xqfm[:]).then_inc(sem, 16)
                    e.dma_start(out=dbg["sgtm"], in_=sgtm[:]).then_inc(sem, 16)
                    e.dma_start(out=dbg["gsc"], in_=gsc[:]).then_inc(sem, 16)
                    e.dma_start(out=dbg["mvc"], in_=mvc[:].rearrange("p a b -> p (a b)")).then_inc(sem, 16)
                    e.dma_start(out=dbg["ipc"], in_=ipc[:]).then_inc(sem, 16)
                    e.dma_start(out=dbg["xnf"], in_=xntm[:]).then_inc(sem, 16)
                    e.dma_start(out=dbg["outb"], in_=outb[:]).then_inc(sem, 16)
                S.emit("sync", fn_dfin, dma_n=8,
                       reads=[b_hqfm, b_xqfm, b_sg, b_gsc, b_outb, b_mvc, b_ipc, b_xntm],
                       deps=S.barrier_tokens())

            def fn_fin(v):
                return v.engine_nop()
            S.emit("dve", fn_fin, deps=[t_out_last] + S.barrier_tokens())

        with nc.Block() as block:
            @block.tensor
            def _(eng):
                for st in S.steps["pe"]:
                    st(eng)

            @block.vector
            def _(eng):
                for st in S.steps["dve"]:
                    st(eng)

            @block.scalar
            def _(eng):
                for st in S.steps["act"]:
                    st(eng)

            @block.gpsimd
            def _(eng):
                for st in S.steps["pool"]:
                    st(eng)

            @block.sync
            def _(eng):
                for st in S.steps["sync"]:
                    st(eng)

    return nc


def _in_maps(inp):
    x = np.asarray(inp["L0_attn_x"], dtype=np.float32)
    W_in = np.asarray(inp["L0_attn_W_in"], dtype=np.float32)
    W_state = np.asarray(inp["L0_attn_W_state"], dtype=np.float32)
    W_dt = np.asarray(inp["L0_attn_W_dt"], dtype=np.float32)
    b_dt = np.asarray(inp["L0_attn_b_dt"], dtype=np.float32)
    A_log = np.asarray(inp["L0_attn_A_log"], dtype=np.float32)
    W_C = np.asarray(inp["L0_attn_W_C"], dtype=np.float32)
    Dv = np.asarray(inp["L0_attn_D"], dtype=np.float32)
    W_out = np.asarray(inp["L0_attn_W_out"], dtype=np.float32)
    w_gate = np.asarray(inp["L0_ffn_w_gate"], dtype=np.float32)
    w_up = np.asarray(inp["L0_ffn_w_up"], dtype=np.float32)
    w_down = np.asarray(inp["L0_ffn_w_down"], dtype=np.float32)

    import ml_dtypes
    WiuT = np.ascontiguousarray(W_in.T[:, :D])
    WigT = np.ascontiguousarray(W_in.T[:, D:])
    WstT = np.ascontiguousarray(W_state.T)
    WdtT = np.ascontiguousarray(W_dt.T)
    WcT = np.ascontiguousarray(W_C.T)
    WoT = np.ascontiguousarray(W_out.T)
    ident = np.eye(128, dtype=ml_dtypes.bfloat16)
    bdtc = np.ascontiguousarray(b_dt.reshape(NST, 1))
    alogc = np.ascontiguousarray(A_log.reshape(NST, 1))
    dvc = np.ascontiguousarray(Dv.reshape(1, D))

    maps = []
    for core in range(8):
        b, half = core // 2, core % 2
        xb = x[b]
        if half == 0:
            xroll = np.concatenate([xb[TOK:], xb[:TOK]], axis=0)
        else:
            xroll = xb
        cm = np.ones((128, ST1), np.float32)
        if half == 0:
            cm[:, 7] = 0.0
        maps.append({
            "xT": np.ascontiguousarray(xroll.T),
            "x_tm": np.ascontiguousarray(xb[half * TOK:(half + 1) * TOK]),
            "WiuT": WiuT, "WigT": WigT, "WstT": WstT, "WdtT": WdtT,
            "WcT": WcT, "WoT": WoT,
            "wg": w_gate, "wu": w_up, "wd": w_down,
            "bdt": bdtc, "alog": alogc, "dvec": dvc,
            "identI": ident, "cmask": cm,
        })
    return maps


def kernel(**inp):
    maps = _in_maps(inp)
    nc = build_nc(debug=False)
    res = run_bass_kernel_spmd(nc, maps, core_ids=list(range(8)))
    out = np.empty((4, TOKF, D), dtype=np.float32)
    for core in range(8):
        b, half = core // 2, core % 2
        out[b, half * TOK:(half + 1) * TOK] = res.results[core]["out"]
    return out


# version 03
# speedup vs baseline: 1.1524x; 1.1524x over previous
import contextlib
import numpy as np

import concourse.bass as bass
import concourse.mybir as mybir
from concourse.bass_utils import run_bass_kernel_spmd

F32 = mybir.dt.float32
BF16 = mybir.dt.bfloat16
AF = mybir.ActivationFunctionType
ALU = mybir.AluOpType
AX = mybir.AxisListType

MAGIC = float(1.5 * 2 ** 23)
T7 = 10.0 / 12.0
D = 1024
E2 = 2048
NST = 128
I3 = 2816
TOK = 4096
TOKF = 8192
ST2 = 8
ST1 = 16
M = 512
KD = D // 128
KI = I3 // 128

ENGS = ("pe", "dve", "act", "pool", "sync")
DMA_ENGS = ("sync", "act", "pool")


class Buf:

    __slots__ = ("wtok", "rtoks")

    def __init__(self):
        self.wtok = None
        self.rtoks = []


class Sched:

    NDMA = 8

    def __init__(self, nc, sems):
        self.nc = nc
        self.sems = sems
        self.steps = {e: [] for e in ENGS}
        self.gcnt = {e: 0 for e in ENGS}
        self.dgidx = {e: 0 for e in DMA_ENGS}
        self.dcnt = {}
        self.hi = {}

    def _wait(self, eng_name, deps):
        best = {}
        for tok in deps:
            if tok is None:
                continue
            semname, val = tok
            if val <= 0:
                continue
            if semname.startswith("g:") and semname[2:] == eng_name:
                continue
            if best.get(semname, 0) < val:
                best[semname] = val
        out = []
        for semname, val in best.items():
            key = (eng_name, semname)
            if self.hi.get(key, 0) >= val:
                continue
            self.hi[key] = val
            out.append((self.sems[semname], val))
        return out

    def _wait_nodedup(self, eng_name, deps, strict=False):
        best = {}
        for tok in deps:
            if tok is None:
                continue
            semname, val = tok
            if val <= 0:
                continue
            if not strict and semname.startswith("g:") and semname[2:] == eng_name:
                continue
            if best.get(semname, 0) < val:
                best[semname] = val
        return [(self.sems[s], v) for s, v in best.items()]

    def emit(self, eng_name, fn, deps=(), reads=(), writes=(), dma_n=0, strict=False):
        deps = list(deps)
        for b in reads:
            deps.append(b.wtok)
        for b in writes:
            deps.append(b.wtok)
            deps.extend(b.rtoks)
        if dma_n:
            rot = self.dgidx[eng_name] % Sched.NDMA
            self.dgidx[eng_name] += 1
            semname = f"d:{eng_name}:{rot}"
            prev = self.dcnt.get((eng_name, rot), 0)
            if prev > 0:
                deps.append((semname, prev))
            new_total = prev + 16 * dma_n
            self.dcnt[(eng_name, rot)] = new_total
            waits = self._wait(eng_name, deps)
            tok = (semname, new_total)
            sem = self.sems[semname]

            def step(eng, waits=waits, fn=fn, sem=sem):
                for s, v in waits:
                    eng.wait_ge(s, v)
                fn(eng, sem)

        else:
            waits = self._wait(eng_name, deps)
            semname = "g:" + eng_name
            self.gcnt[eng_name] += 1
            tok = (semname, self.gcnt[eng_name])
            sem = self.sems[semname]

            def step(eng, waits=waits, fn=fn, sem=sem):
                for s, v in waits:
                    eng.wait_ge(s, v)
                last = fn(eng)
                last.then_inc(sem, 1)

        self.steps[eng_name].append(step)
        for b in reads:
            b.rtoks.append(tok)
        for b in writes:
            b.wtok = tok
            b.rtoks = []
        return tok

    def barrier_tokens(self):
        toks = [("g:" + e, self.gcnt[e]) for e in ENGS]
        toks += [(f"d:{q}:{r}", v) for (q, r), v in self.dcnt.items()]
        return toks


def _emit_quant(S, xn_ap, nelem, nb, out_ap, q, dve="dve", pool="pool",
                 deps=(), reads=(), writes=()):
    x3 = xn_ap.rearrange("p (b w) -> p b w", w=16)
    bmax, rinv, sc12 = q["bmax"], q["rinv"], q["sc12"]
    tF, b1, b2, b3, b4 = q["tF"], q["b1"], q["b2"], q["b3"], q["b4"]

    if nb >= 64:
        def red(v):
            v.reduce_max(bmax[:, :nb], x3, axis=AX.X, apply_absolute_value=True)
            v.tensor_scalar(rinv[:, :nb], bmax[:, :nb], 6e-12, None, ALU.max)
            v.reciprocal(bmax[:, :nb], rinv[:, :nb])
            return v.tensor_scalar(sc12[:, :nb], rinv[:, :nb], 1.0 / 12.0, None, ALU.mult)

        t_red = S.emit(dve, red, deps=deps, reads=list(reads),
                       writes=[q["_bmax"], q["_rinv"], q["_sc12"]])
    else:
        def red_a(v):
            return v.reduce_max(bmax[:, :nb], x3, axis=AX.X, apply_absolute_value=True)
        t_ra = S.emit(dve, red_a, deps=deps, reads=list(reads), writes=[q["_bmax"]])

        def red_b(g):
            return g.tensor_scalar(rinv[:, :nb], bmax[:, :nb], 6e-12, None, ALU.max)
        t_rb = S.emit(pool, red_b, deps=[t_ra], reads=[q["_bmax"]], writes=[q["_rinv"]])

        def red_c(v):
            return v.reciprocal(bmax[:, :nb], rinv[:, :nb])
        t_rc = S.emit(dve, red_c, deps=[t_rb], reads=[q["_rinv"]], writes=[q["_bmax"]])

        def red_d(g):
            return g.tensor_scalar(sc12[:, :nb], rinv[:, :nb], 1.0 / 12.0, None, ALU.mult)
        t_red = S.emit(pool, red_d, deps=[t_rc], reads=[q["_rinv"]], writes=[q["_sc12"]])

    def pmul(g):
        rb = bmax[:, :nb].unsqueeze(2).broadcast_to([128, nb, 16])
        return g.tensor_tensor(q["xnq"][:, :nelem].rearrange("p (b w) -> p b w", w=16),
                               x3, rb, ALU.mult)

    t_xn = S.emit(pool, pmul, deps=[t_red], reads=list(reads) + [q["_bmax"]],
                  writes=[q["_xnq"]])
    xq = q["xnq"][:, :nelem]

    def ladder(v):
        v.tensor_scalar(tF[:, :nelem], xq, 12.0, MAGIC, ALU.mult, ALU.add)
        v.tensor_scalar(b1[:, :nelem], tF[:, :nelem], MAGIC, 4.0, ALU.subtract, ALU.min)
        v.tensor_scalar(tF[:, :nelem], xq, 6.0, MAGIC, ALU.mult, ALU.add)
        v.tensor_scalar(b2[:, :nelem], tF[:, :nelem], MAGIC, 4.0, ALU.subtract, ALU.min)
        v.tensor_scalar(b3[:, :nelem], b1[:, :nelem], -4.0, None, ALU.max)
        v.tensor_scalar(b1[:, :nelem], b2[:, :nelem], 2.0, None, ALU.max)
        v.tensor_scalar(b4[:, :nelem], b2[:, :nelem], -2.0, -4.0, ALU.min, ALU.max)
        v.tensor_tensor(b2[:, :nelem], b1[:, :nelem], b4[:, :nelem], ALU.add)
        v.scalar_tensor_tensor(b1[:, :nelem], b2[:, :nelem], 2.0, b3[:, :nelem],
                               ALU.mult, ALU.add)
        v.tensor_scalar(b3[:, :nelem], xq, T7, 4.0, ALU.is_gt, ALU.mult)
        v.tensor_scalar(b4[:, :nelem], xq, -T7, 4.0, ALU.is_lt, ALU.mult)
        v.tensor_tensor(b2[:, :nelem], b3[:, :nelem], b4[:, :nelem], ALU.subtract)
        return v.tensor_tensor(b3[:, :nelem], b1[:, :nelem], b2[:, :nelem], ALU.add)

    t_lad = S.emit(dve, ladder, deps=[t_xn], reads=[q["_xnq"]],
                   writes=[q["_tF"], q["_b1"], q["_b2"], q["_b3"], q["_b4"]])

    def fmul(g):
        scb = sc12[:, :nb].unsqueeze(2).broadcast_to([128, nb, 16])
        return g.tensor_tensor(out_ap.rearrange("p (b w) -> p b w", w=16),
                               q["b3"][:, :nelem].rearrange("p (b w) -> p b w", w=16),
                               scb, ALU.mult)

    return S.emit(pool, fmul, deps=[t_lad], reads=[q["_b3"], q["_sc12"]],
                  writes=list(writes))


def build_nc(debug=False, st2_limit=ST2, serialize=False):
    nc = bass.Bass("TRN2", target_bir_lowering=False)

    xT = nc.dram_tensor("xT", [D, TOKF], F32, kind="ExternalInput").ap()
    x_tm = nc.dram_tensor("x_tm", [TOK, D], F32, kind="ExternalInput").ap()
    WiuT = nc.dram_tensor("WiuT", [D, D], F32, kind="ExternalInput").ap()
    WigT = nc.dram_tensor("WigT", [D, D], F32, kind="ExternalInput").ap()
    WstT = nc.dram_tensor("WstT", [D, NST], F32, kind="ExternalInput").ap()
    WdtT = nc.dram_tensor("WdtT", [D, NST], F32, kind="ExternalInput").ap()
    WcT = nc.dram_tensor("WcT", [NST, D], F32, kind="ExternalInput").ap()
    WoT = nc.dram_tensor("WoT", [D, D], F32, kind="ExternalInput").ap()
    wg = nc.dram_tensor("wg", [I3, D], F32, kind="ExternalInput").ap()
    wu = nc.dram_tensor("wu", [I3, D], F32, kind="ExternalInput").ap()
    wd = nc.dram_tensor("wd", [D, I3], F32, kind="ExternalInput").ap()
    bdt = nc.dram_tensor("bdt", [NST, 1], F32, kind="ExternalInput").ap()
    alog = nc.dram_tensor("alog", [NST, 1], F32, kind="ExternalInput").ap()
    dvec = nc.dram_tensor("dvec", [1, D], F32, kind="ExternalInput").ap()
    identI = nc.dram_tensor("identI", [128, 128], BF16, kind="ExternalInput").ap()
    cmask = nc.dram_tensor("cmask", [128, ST1], F32, kind="ExternalInput").ap()
    out = nc.dram_tensor("out", [TOK, D], F32, kind="ExternalOutput").ap()
    wqgT = nc.dram_tensor("wqgT", [D, I3], BF16).ap()
    wquT = nc.dram_tensor("wquT", [D, I3], BF16).ap()
    wqdT = nc.dram_tensor("wqdT", [I3, D], BF16).ap()

    dbg = {}
    if debug:
        dbg["l"] = nc.dram_tensor("dbg_l", [NST, TOK], BF16, kind="ExternalOutput").ap()
        dbg["mcols"] = nc.dram_tensor("dbg_mcols", [128, 32], F32, kind="ExternalOutput").ap()
        dbg["icols"] = nc.dram_tensor("dbg_icols", [128, 32], F32, kind="ExternalOutput").ap()
        dbg["wqg"] = nc.dram_tensor("dbg_wqg", [D, I3], BF16, kind="ExternalOutput").ap()
        dbg["s"] = nc.dram_tensor("dbg_s", [TOK, D], F32, kind="ExternalOutput").ap()
        dbg["xn1"] = nc.dram_tensor("dbg_xn1", [128, KD, M], BF16, kind="ExternalOutput").ap()
        dbg["u"] = nc.dram_tensor("dbg_u", [128, KD, M], BF16, kind="ExternalOutput").ap()
        dbg["dec"] = nc.dram_tensor("dbg_dec", [NST, M], F32, kind="ExternalOutput").ap()
        dbg["ef"] = nc.dram_tensor("dbg_ef", [NST, M], F32, kind="ExternalOutput").ap()
        dbg["xn2"] = nc.dram_tensor("dbg_xn2", [128, 4, D], F32, kind="ExternalOutput").ap()
        dbg["dt"] = nc.dram_tensor("dbg_dt", [NST, M], F32, kind="ExternalOutput").ap()
        dbg["bt"] = nc.dram_tensor("dbg_bt", [NST, M], F32, kind="ExternalOutput").ap()
        dbg["rate"] = nc.dram_tensor("dbg_rate", [NST, 1], F32, kind="ExternalOutput").ap()
        dbg["hqfm"] = nc.dram_tensor("dbg_hqfm", [128, KI, M], BF16, kind="ExternalOutput").ap()
        dbg["xqfm"] = nc.dram_tensor("dbg_xqfm", [128, KD, M], BF16, kind="ExternalOutput").ap()
        dbg["sgtm"] = nc.dram_tensor("dbg_sgtm", [128, 4, D], BF16, kind="ExternalOutput").ap()
        dbg["fps"] = nc.dram_tensor("dbg_fps", [128, 512], F32, kind="ExternalOutput").ap()
        dbg["f2"] = nc.dram_tensor("dbg_f2", [128, 512], F32, kind="ExternalOutput").ap()
        dbg["wqd"] = nc.dram_tensor("dbg_wqd", [I3, D], BF16, kind="ExternalOutput").ap()
        dbg["gsc"] = nc.dram_tensor("dbg_gsc", [128, 512], F32, kind="ExternalOutput").ap()
        dbg["outb"] = nc.dram_tensor("dbg_outb", [128, 512], F32, kind="ExternalOutput").ap()
        dbg["mvc"] = nc.dram_tensor("dbg_mvc", [128, 8], F32, kind="ExternalOutput").ap()
        dbg["ipc"] = nc.dram_tensor("dbg_ipc", [128, 64], F32, kind="ExternalOutput").ap()
        dbg["xnf"] = nc.dram_tensor("dbg_xnf", [128, 4, D], F32, kind="ExternalOutput").ap()
        dbg["xq"] = nc.dram_tensor("dbg_xq", [TOK, D], BF16, kind="ExternalOutput").ap()

    semnames = ["g:" + e for e in ENGS] + [
        f"d:{q}:{r}" for q in DMA_ENGS for r in range(Sched.NDMA)]

    with contextlib.ExitStack() as ctx:
        sems = {n: ctx.enter_context(nc.semaphore(n.replace(":", "_"))) for n in semnames}
        S = Sched(nc, sems)
        B = lambda: Buf()

        sb = lambda name, shape, dt: ctx.enter_context(nc.sbuf_tensor(name, shape, dt))
        ps = lambda name, shape: ctx.enter_context(nc.psum_tensor(name, shape, F32))

        l_b = sb("l_b", [NST, TOK], BF16)
        ident_sb = sb("ident_sb", [128, 128], BF16)
        ident1 = sb("ident1", [1, 1], F32)
        ones_col = sb("ones_col", [128, 1], BF16)
        ones_row = sb("ones_row", [1, 128], F32)
        bdt_sb = sb("bdt_sb", [NST, 1], F32)
        rate_sb = sb("rate_sb", [NST, 1], F32)
        dbc = sb("dbc", [128, D], F32)
        cmask_sb = sb("cmask_sb", [128, ST1], F32)
        mcols = sb("mcols", [128, 32], F32)
        icols = sb("icols", [128, 32], F32)
        wing_sb = sb("wing_sb", [128, KD, D], BF16)
        wct_sb = sb("wct_sb", [NST, D], BF16)
        wout_sb = sb("wout_sb", [128, KD, D], BF16)
        QF = 1024
        q = {"bmax": sb("q_bmax", [128, QF // 16], F32),
             "rinv": sb("q_rinv", [128, QF // 16], F32),
             "sc12": sb("q_sc12", [128, QF // 16], F32),
             "xnq": sb("q_xnq", [128, QF], F32),
             "tF": sb("q_tF", [128, QF], F32),
             "b1": sb("q_b1", [128, QF], BF16),
             "b2": sb("q_b2", [128, QF], BF16),
             "b3": sb("q_b3", [128, QF], BF16),
             "b4": sb("q_b4", [128, QF], BF16)}
        for k in ("bmax", "rinv", "sc12", "xnq", "tF", "b1", "b2", "b3", "b4"):
            q["_" + k] = B()

        P = [ps(f"ps{i}", [128, 512]) for i in range(8)]
        PB = [B() for _ in range(8)]

        b_l = B(); b_mcols = B(); b_icols = B(); b_consts = B()
        b_wing = B(); b_wct = B(); b_wout = B()
        b_wqg = B(); b_wqu = B(); b_wqd = B()

        def fn_consts(e, sem):
            e.dma_start(out=ident_sb[:], in_=identI).then_inc(sem, 16)
            e.dma_start(out=bdt_sb[:], in_=bdt).then_inc(sem, 16)
            e.dma_start(out=rate_sb[:], in_=alog).then_inc(sem, 16)
            e.dma_start(out=cmask_sb[:], in_=cmask).then_inc(sem, 16)
            e.dma_start(out=dbc[0:1, :], in_=dvec).then_inc(sem, 16)
        t_c0 = S.emit("sync", fn_consts, dma_n=5, writes=[b_consts])

        def fn_memset(v):
            v.memset(ones_col[:], 1.0)
            v.memset(ones_row[:], 1.0)
            return v.memset(ident1[:], 1.0)
        t_ms = S.emit("dve", fn_memset, writes=[b_consts])

        def fn_table1(a):
            return a.preload_act_table(AF.Ln)
        from concourse.hw_specs import get_activation_tables
        _tabs = get_activation_tables(nc.m.arch)
        _explnset = None
        for _i, (_n, _fs) in enumerate(_tabs.items()):
            if AF.Exp in _fs and AF.Ln in _fs:
                _explnset = _i
                break
        _siluset = None
        for _i, (_n, _fs) in enumerate(_tabs.items()):
            if AF.Silu in _fs:
                _siluset = _i
                break
        _sqrtset = None
        for _i, (_n, _fs) in enumerate(_tabs.items()):
            if AF.Sqrt in _fs:
                _sqrtset = _i
                break

        def fn_tab_expln(a):
            return a.add_instruction(mybir.InstLoadActFuncSet(
                name=nc.get_next_instruction_name(), act_func_set_id=_explnset,
                ins=[], outs=[]))
        t_tab1 = S.emit("act", fn_tab_expln)

        rt1 = sb("rt1", [NST, 1], F32); rt2 = sb("rt2", [NST, 1], F32)
        b_rate = B()

        def fn_rate(a):
            a.activation(rt1[:], rate_sb[:], AF.Abs)
            a.activation(rt1[:], rt1[:], AF.Exp, scale=-1.0)
            a.activation(rt1[:], rt1[:], AF.Ln, bias=1.0)
            return a.activation(rt2[:], rate_sb[:], AF.Relu)
        t_r1 = S.emit("act", fn_rate, deps=[t_c0, t_tab1], reads=[b_consts], writes=[b_rate])

        def fn_rate2(v):
            return v.scalar_tensor_tensor(rate_sb[:], rt1[:], 1e-4, rt2[:],
                                          ALU.add, ALU.add)
        t_rate = S.emit("dve", fn_rate2, deps=[t_r1], writes=[b_rate])

        b_dbc = B()

        def fn_dbc_mm(t):
            t.matmul(P[6][:, 0:512], ones_row[:], dbc[0:1, 0:512], start=True, stop=True)
            return t.matmul(P[7][:, 0:512], ones_row[:], dbc[0:1, 512:1024], start=True, stop=True)
        t_dbm = S.emit("pe", fn_dbc_mm, deps=[t_c0, t_ms], reads=[b_consts],
                       writes=[PB[6], PB[7]])

        def fn_dbc_ev(a):
            a.activation(dbc[:, 0:512], P[6][:, 0:512], AF.Copy)
            return a.activation(dbc[:, 512:1024], P[7][:, 0:512], AF.Copy)
        t_dbc = S.emit("act", fn_dbc_ev, deps=[t_dbm, t_tab1], reads=[PB[6], PB[7]], writes=[b_dbc])

        def fn_w2(e, sem):
            e.dma_start(out=wing_sb[:], in_=WigT.rearrange("(k p) e -> p k e", p=128)).then_inc(sem, 16)
            e.dma_start(out=wct_sb[:], in_=WcT).then_inc(sem, 16)
            e.dma_start(out=wout_sb[:], in_=WoT.rearrange("(k p) e -> p k e", p=128)).then_inc(sem, 16)
        t_w2 = S.emit("pool", fn_w2, dma_n=3, writes=[b_wing, b_wct, b_wout])

        with contextlib.ExitStack() as prep_ctx:
            stg = prep_ctx.enter_context(nc.sbuf_tensor("stg", [128, KI * D], BF16))
            wrow = prep_ctx.enter_context(nc.sbuf_tensor("wrow", [128, I3], F32))
            wqrow = prep_ctx.enter_context(nc.sbuf_tensor("wqrow", [128, I3], BF16))
            b_stg = [B() for _ in range(KI)]
            b_wrow = B(); b_wqrow = B()

            def prep_weight(src, dst, rows, cols, b_dst):
                nrt = rows // 128
                nct = cols // 128
                for rt in range(nrt):
                    def fn_ld(e, sem, rt=rt):
                        e.dma_start(out=wrow[:, 0:cols],
                                    in_=src[rt * 128:(rt + 1) * 128, :]).then_inc(sem, 16)
                    t_ld = S.emit("sync", fn_ld, dma_n=1, writes=[b_wrow])
                    toks = []
                    for off in range(0, cols, QF):
                        w = min(QF, cols - off)
                        tq = _emit_quant(S, wrow[:, off:off + w], w, w // 16,
                                         wqrow[:, off:off + w], q,
                                         deps=[t_ld], reads=[b_wrow], writes=[b_wqrow])
                        toks.append(tq)
                    for cc in range(nct):
                        pslot = 6 + (cc % 2)
                        def fn_tr(t, cc=cc, pslot=pslot):
                            return t.matmul(P[pslot][:, 0:128],
                                            wqrow[:, cc * 128:(cc + 1) * 128],
                                            ident_sb[:], start=True, stop=True)
                        t_tr = S.emit("pe", fn_tr, deps=toks + [t_c0],
                                      reads=[b_wqrow, b_consts], writes=[PB[pslot]])
                        def fn_ev(a, cc=cc, rt=rt, pslot=pslot, rows=rows):
                            return a.activation(stg[:, cc * rows + rt * 128:cc * rows + (rt + 1) * 128],
                                                P[pslot][:, 0:128], AF.Copy)
                        S.emit("act", fn_ev, deps=[t_tr], reads=[PB[pslot]], writes=[b_stg[cc]])
                for cc in range(nct):
                    def fn_st(e, sem, cc=cc, rows=rows):
                        e.dma_start(out=dst[cc * 128:(cc + 1) * 128, :],
                                    in_=stg[:, cc * rows:(cc + 1) * rows]).then_inc(sem, 16)
                    S.emit("sync", fn_st, dma_n=1, reads=[b_stg[cc]], writes=[b_dst])

            prep_weight(wg, wqgT, I3, D, b_wqg)
            prep_weight(wu, wquT, I3, D, b_wqu)
            prep_weight(wd, wqdT, D, I3, b_wqd)

            if debug:
                def fn_dwq(e, sem):
                    e.dma_start(out=dbg["wqg"], in_=wqgT).then_inc(sem, 16)
                S.emit("sync", fn_dwq, dma_n=1, reads=[b_wqg])

        bar0 = S.barrier_tokens()
        p1 = {}
        with contextlib.ExitStack() as p1ctx:
            sb1 = lambda name, shape, dt: p1ctx.enter_context(nc.sbuf_tensor(name, shape, dt))
            wiu_sb = sb1("wiu_sb", [128, KD, D], BF16)
            wst_sb = sb1("wst_sb", [128, KD, NST], BF16)
            wdt_sb = sb1("wdt_sb", [128, KD, NST], BF16)
            b_wiu = B(); b_wst = B(); b_wdt = B()

            def fn_w1(e, sem):
                e.dma_start(out=wiu_sb[:], in_=WiuT.rearrange("(k p) e -> p k e", p=128)).then_inc(sem, 16)
                e.dma_start(out=wst_sb[:], in_=WstT.rearrange("(k p) n -> p k n", p=128)).then_inc(sem, 16)
                e.dma_start(out=wdt_sb[:], in_=WdtT.rearrange("(k p) n -> p k n", p=128)).then_inc(sem, 16)
            t_w1 = S.emit("pool", fn_w1, dma_n=3, deps=bar0,
                          writes=[b_wiu, b_wst, b_wdt])
            xfm = [sb1(f"xfm{i}", [128, KD, M], BF16) for i in range(2)]
            x2fm = sb1("x2fm", [128, KD, M], BF16)
            xnfm = sb1("xnfm", [128, KD, M], BF16)
            ufm = sb1("ufm", [128, KD, M], BF16)
            tscr = sb1("tscr", [128, M], F32)
            mrow = sb1("mrow", [1, M], F32)
            irow = sb1("irow", [1, M], F32)
            mbs = sb1("mbs", [128, M], F32)
            ibs = sb1("ibs", [128, M], F32)
            dtc = sb1("dtc", [128, M], F32)
            dtc2 = sb1("dtc2", [128, M], F32)
            dec = sb1("dec", [128, M], F32)
            ef = sb1("ef", [128, M], F32)
            lscr = sb1("lscr", [128, M], F32)
            carry = sb1("carry", [NST, 1], F32)
            b_xfm = [B(), B()]
            b_x2 = B(); b_xn1 = B(); b_u = B(); b_stat = B(); b_bc = B()
            b_dt = B(); b_dec = B(); b_ef = B(); b_l1 = B(); b_carry = B()
            b_tscr = B()

            def fn_carry0(v):
                return v.memset(carry[:], 0.0)
            t_cy = S.emit("dve", fn_carry0, deps=bar0, writes=[b_carry])

            for s in range(ST1):
                xb = xfm[s % 2]
                bxb = b_xfm[s % 2]

                def fn_lx(e, sem, s=s, xb=xb):
                    e.dma_start(out=xb[:],
                                in_=xT.rearrange("(c p) t -> p c t", p=128)[:, :, s * M:(s + 1) * M]
                                ).then_inc(sem, 16)
                t_lx = S.emit("pool", fn_lx, dma_n=1, deps=bar0, writes=[bxb])

                def fn_sq(a, xb=xb):
                    return a.activation(x2fm[:], xb[:], AF.Square)
                t_sq = S.emit("act", fn_sq, deps=[t_lx, t_tab1], reads=[bxb], writes=[b_x2])

                def fn_cs(t, xb=xb):
                    for c in range(KD):
                        t.matmul(P[4][0:1, :], ones_col[:], xb[:, c, :],
                                 start=(c == 0), stop=(c == KD - 1))
                    for c in range(KD):
                        last = t.matmul(P[5][0:1, :], ones_col[:], x2fm[:, c, :],
                                        start=(c == 0), stop=(c == KD - 1))
                    return last
                t_cs = S.emit("pe", fn_cs, deps=[t_lx, t_sq, t_ms], reads=[bxb, b_x2],
                              writes=[PB[4], PB[5]])

                def fn_stat(v):
                    v.tensor_scalar(mrow[:], P[4][0:1, :], 1.0 / D, None, ALU.mult)
                    v.tensor_scalar(irow[:], P[5][0:1, :], 1.0 / D, None, ALU.mult)
                    v.tensor_tensor(tscr[0:1, :], mrow[:], mrow[:], ALU.mult)
                    v.tensor_tensor(irow[:], irow[:], tscr[0:1, :], ALU.subtract)
                    return v.tensor_scalar(irow[:], irow[:], 1e-5, None, ALU.add)
                t_st = S.emit("dve", fn_stat, deps=[t_cs], reads=[PB[4], PB[5]],
                              writes=[b_stat, b_tscr])

                def fn_inv(a):
                    a.activation(irow[:], irow[:], AF.Ln)
                    return a.activation(irow[:], irow[:], AF.Exp, scale=-0.5)
                t_inv = S.emit("act", fn_inv, deps=[t_st], writes=[b_stat])

                def fn_bc(t):
                    t.matmul(P[4][:, :], ones_row[:], mrow[:], start=True, stop=True)
                    return t.matmul(P[5][:, :], ones_row[:], irow[:], start=True, stop=True)
                t_bc = S.emit("pe", fn_bc, deps=[t_inv, t_ms], reads=[b_stat],
                              writes=[PB[4], PB[5]])

                def fn_bc_ev(a):
                    a.activation(mbs[:], P[4][:, :], AF.Copy)
                    return a.activation(ibs[:], P[5][:, :], AF.Copy)
                t_bce = S.emit("act", fn_bc_ev, deps=[t_bc], reads=[PB[4], PB[5]], writes=[b_bc])

                if s >= 8:
                    S2i = s - 8

                    def fn_strm(t, S2i=S2i):
                        for cc in range(4):
                            t.matmul(P[6][:, cc:cc + 1], mrow[0:1, cc * 128:(cc + 1) * 128],
                                     ident1[:], start=True, stop=True)
                        for cc in range(4):
                            last = t.matmul(P[6][:, 4 + cc:5 + cc],
                                            irow[0:1, cc * 128:(cc + 1) * 128],
                                            ident1[:], start=True, stop=True)
                        return last
                    t_str = S.emit("pe", fn_strm, deps=[t_inv, t_ms], reads=[b_stat],
                                   writes=[PB[6]])

                    def fn_stre(a, S2i=S2i):
                        a.activation(mcols[:, S2i * 4:(S2i + 1) * 4], P[6][:, 0:4], AF.Copy)
                        return a.activation(icols[:, S2i * 4:(S2i + 1) * 4], P[6][:, 4:8], AF.Copy)
                    S.emit("act", fn_stre, deps=[t_str], reads=[PB[6]],
                           writes=[b_mcols, b_icols])

                def fn_xn1(v, xb=xb):
                    for c in range(KD):
                        last = v.tensor_tensor(xnfm[:, c, :], xb[:, c, :], mbs[:], ALU.subtract)
                    return last
                t_x1 = S.emit("dve", fn_xn1, deps=[t_bce], reads=[bxb, b_bc], writes=[b_xn1])

                def fn_xn2(g):
                    for c in range(KD):
                        last = g.tensor_tensor(xnfm[:, c, :], xnfm[:, c, :], ibs[:], ALU.mult)
                    return last
                t_x2 = S.emit("pool", fn_xn2, deps=[t_x1], reads=[b_bc], writes=[b_xn1])

                t_uev = []
                for e8 in range(KD):
                    pslot = e8 % 2
                    def fn_mm(t, e8=e8, pslot=pslot):
                        for k in range(KD):
                            last = t.matmul(P[pslot][:, :],
                                            wiu_sb[:, k, e8 * 128:(e8 + 1) * 128],
                                            xnfm[:, k, :], start=(k == 0), stop=(k == KD - 1))
                        return last
                    t_mm = S.emit("pe", fn_mm, deps=[t_x2, t_w1], reads=[b_xn1, b_wiu],
                                  writes=[PB[pslot]])

                    def fn_ev(a, e8=e8, pslot=pslot):
                        return a.activation(ufm[:, e8, :], P[pslot][:, :], AF.Copy)
                    t_uev.append(S.emit("act", fn_ev, deps=[t_mm], reads=[PB[pslot]],
                                        writes=[b_u]))

                def fn_bt2(t):
                    for k in range(KD):
                        t.matmul(P[2][:, :], wst_sb[:, k, :], ufm[:, k, :],
                                 start=(k == 0), stop=(k == KD - 1))
                    for k in range(KD):
                        lb = t.matmul(P[3][:, :], wdt_sb[:, k, :], ufm[:, k, :],
                                      start=(k == 0), stop=(k == KD - 1))
                    return lb
                t_bt = S.emit("pe", fn_bt2, deps=t_uev + [t_w1], reads=[b_u, b_wst, b_wdt],
                              writes=[PB[2], PB[3]])

                def fn_dta(a):
                    a.activation(dtc[:], P[3][:, :], AF.Abs, bias=bdt_sb[:])
                    a.activation(dtc2[:], P[3][:, :], AF.Relu, bias=bdt_sb[:])
                    a.activation(dtc[:], dtc[:], AF.Exp, scale=-1.0)
                    return a.activation(dtc[:], dtc[:], AF.Ln, bias=1.0)
                t_dta = S.emit("act", fn_dta, deps=[t_bt, t_rate, t_c0], reads=[PB[3], b_consts],
                               writes=[b_dt])

                def fn_dtb(v):
                    v.tensor_tensor(dtc[:], dtc[:], dtc2[:], ALU.add)
                    v.tensor_scalar(dtc[:], dtc[:], 10.0, None, ALU.min)
                    return v.tensor_scalar(dtc[:], dtc[:], rate_sb[:], None, ALU.mult)
                t_dtb = S.emit("dve", fn_dtb, deps=[t_dta], reads=[b_rate], writes=[b_dt])

                def fn_dec(a):
                    return a.activation(dec[:], dtc[:], AF.Exp, scale=-1.0)
                t_dec = S.emit("act", fn_dec, deps=[t_dtb], reads=[b_dt], writes=[b_dec])

                def fn_ef(v):
                    v.tensor_scalar(ef[:], dec[:], -1.0, 1.0, ALU.mult, ALU.add)
                    return v.tensor_tensor(ef[:], ef[:], P[2][:, :], ALU.mult)
                t_ef = S.emit("dve", fn_ef, deps=[t_dec, t_bt], reads=[b_dec, PB[2]],
                              writes=[b_ef])

                def fn_scan(v, s=s):
                    return v.tensor_tensor_scan(lscr[:], dec[:], ef[:], carry[:, 0:1],
                                                ALU.mult, ALU.add)
                t_sc = S.emit("dve", fn_scan, deps=[t_ef, t_c0], reads=[b_dec, b_ef, b_carry],
                              writes=[b_l1])

                def fn_cup(g, s=s):
                    return g.tensor_tensor(carry[:], lscr[:, M - 1:M], cmask_sb[:, s:s + 1],
                                           ALU.mult)
                t_cup = S.emit("pool", fn_cup, deps=[t_sc], reads=[b_l1, b_consts],
                               writes=[b_carry])

                if s >= 8:
                    def fn_lst(v, s=s):
                        return v.tensor_copy(l_b[:, (s - 8) * M:(s - 7) * M], lscr[:])
                    S.emit("dve", fn_lst, deps=[t_sc], reads=[b_l1], writes=[b_l])

                if debug and s == 15:
                    bt_d = sb1("bt_d", [NST, M], F32)
                    b_btd = B()
                    def fn_btd(v):
                        return v.tensor_copy(bt_d[:], P[2][:, :])
                    t_btd = S.emit("dve", fn_btd, deps=[t_bt], reads=[PB[2]], writes=[b_btd])
                    def fn_d1(e, sem):
                        e.dma_start(out=dbg["xn1"], in_=xnfm[:]).then_inc(sem, 16)
                        e.dma_start(out=dbg["u"], in_=ufm[:]).then_inc(sem, 16)
                        e.dma_start(out=dbg["dec"], in_=dec[:]).then_inc(sem, 16)
                        e.dma_start(out=dbg["ef"], in_=ef[:]).then_inc(sem, 16)
                        e.dma_start(out=dbg["dt"], in_=dtc[:]).then_inc(sem, 16)
                        e.dma_start(out=dbg["bt"], in_=bt_d[:]).then_inc(sem, 16)
                        e.dma_start(out=dbg["rate"], in_=rate_sb[:]).then_inc(sem, 16)
                    S.emit("sync", fn_d1, dma_n=7,
                           reads=[b_xn1, b_u, b_dec, b_ef, b_dt, b_btd, b_rate])

            if debug:
                def fn_dl(e, sem):
                    e.dma_start(out=dbg["l"], in_=l_b[:]).then_inc(sem, 16)
                S.emit("sync", fn_dl, dma_n=1, reads=[b_l])
                def fn_dm(e, sem):
                    e.dma_start(out=dbg["mcols"], in_=mcols[:]).then_inc(sem, 16)
                    e.dma_start(out=dbg["icols"], in_=icols[:]).then_inc(sem, 16)
                S.emit("sync", fn_dm, dma_n=2, reads=[b_mcols, b_icols])

        bar1 = S.barrier_tokens()

        def fn_tab_silu(a):
            return a.add_instruction(mybir.InstLoadActFuncSet(
                name=nc.get_next_instruction_name(), act_func_set_id=_siluset,
                ins=[], outs=[]))
        t_tab2 = S.emit("act", fn_tab_silu, deps=bar1)

        with contextlib.ExitStack() as p2ctx:
            sb2 = lambda name, shape, dt: p2ctx.enter_context(nc.sbuf_tensor(name, shape, dt))
            xtm_t = [sb2(f"xtm{i}", [128, D], F32) for i in range(2)]
            xntm = sb2("xntm", [128, 4, D], F32)
            xnb_t = sb2("xnb_t", [128, D], BF16)
            xq_tm = sb2("xq_tm", [128, 4 * D], BF16)
            xnfm2 = sb2("xnfm2", [128, KD, M], BF16)
            xqfm = sb2("xqfm", [128, KD, M], BF16)
            sgtm = sb2("sgtm", [128, 4, D], BF16)
            stm = sb2("stm", [128, 4, D], F32)
            ztmp = sb2("ztmp", [128, D], F32)
            xnd = sb2("xnd", [128, D], F32)
            zb_t = xnb_t
            zfm = sb2("zfm", [128, KD, 128], BF16)
            h_all = sb2("h_all", [128, 4 * 512], F32)
            sG = sb2("sG", [128, 512], F32)
            hq_tm = sb2("hq_tm", [128, 4 * 512], BF16)
            hq_fm = sb2("hq_fm", [128, KI, M], BF16)
            wqg6 = sb2("wqg6", [128, KD, 512], BF16)
            wqu6 = sb2("wqu6", [128, KD, 512], BF16)
            wqd_sb = [sb2(f"wqd{i}", [128, 512], BF16) for i in range(2)]
            gsc = sG
            xre = sb2("xre", [128, 512], F32)
            outb = sb2("outb", [128, 512], F32)
            st12 = sb2("st12", [128, 4, 12], F32)
            st12c = sb2("st12c", [128, 4, 12], F32)
            mvagg = sb2("mvagg", [128, 4, 2], F32)
            mvc = sb2("mvc", [128, 4, 2], F32)
            vpad = sb2("vpad", [128, 64], F32)
            ipad = sb2("ipad", [128, 64], F32)
            ipc = sb2("ipc", [128, 64], F32)
            b_st12 = B(); b_st12c = B(); b_mv = B(); b_mvc = B()
            b_vpad = B(); b_ipad = B(); b_ipc = B()
            fdbg = xre

            b_xtm = [B(), B()]
            b_xntm = B(); b_xnb = B(); b_xqtm = B(); b_xnfm2 = B(); b_xqfm = B()
            b_z = b_xnb
            b_sg = B(); b_stm = B(); b_zfm = B(); b_hall = B(); b_sG = B()
            b_hqtm = B(); b_hqfm = B(); b_wg6 = B(); b_wu6 = B()
            b_wqdb = [B(), B()]
            b_gsc = b_sG; b_xre = B(); b_outb = B(); b_ztmp = B(); b_xnd = B()
            b_fdbg = b_xre
            b_out = B()

            SIX = [(0, 512), (512, 512), (1024, 512), (1536, 512), (2048, 512), (2560, 256)]

            t_out_last = None
            for S2 in range(st2_limit):
                T0 = S2 * M
                if serialize:
                    bar1 = S.barrier_tokens()
                t_sub = []
                for j in range(4):
                    xb = xtm_t[j % 2]; bxb = b_xtm[j % 2]

                    def fn_lx(e, sem, j=j, xb=xb, T0=T0):
                        e.dma_start(out=xb[:], in_=x_tm[T0 + j * 128:T0 + (j + 1) * 128, :]
                                    ).then_inc(sem, 16)
                    t_lx = S.emit("sync", fn_lx, dma_n=1, deps=bar1, writes=[bxb])

                    def fn_bn(v, j=j, xb=xb):
                        v.bn_stats(st12[:, j, 0:6], xb[:, 0:512])
                        return v.bn_stats(st12[:, j, 6:12], xb[:, 512:1024])
                    t_bn = S.emit("dve", fn_bn, deps=[t_lx] + bar1, reads=[bxb],
                                  writes=[b_st12])

                    def fn_stc(a, j=j):
                        return a.activation(st12c[:, j, :], st12[:, j, :], AF.Copy)
                    t_stc = S.emit("act", fn_stc, deps=[t_bn], reads=[b_st12],
                                   writes=[b_st12c])

                    def fn_ag(v, j=j):
                        return v.bn_aggr(mvagg[:, j, :], st12c[:, j, :])
                    t_ag = S.emit("dve", fn_ag, deps=[t_stc], reads=[b_st12c],
                                  writes=[b_mv])

                    def fn_mvc(a, j=j):
                        return a.activation(mvc[:, j, :], mvagg[:, j, :], AF.Copy)
                    t_mvc = S.emit("act", fn_mvc, deps=[t_ag], reads=[b_mv], writes=[b_mvc])

                    def fn_sub(v, j=j, xb=xb):
                        return v.tensor_scalar(xntm[:, j, :], xb[:], mvc[:, j, 0:1], None,
                                               ALU.subtract)
                    t_sub.append(S.emit("dve", fn_sub, deps=[t_mvc, t_lx],
                                        reads=[bxb, b_mvc], writes=[b_xntm]))

                def fn_gather(v):
                    return v.tensor_scalar(vpad[:, 0:4],
                                           mvc[:].rearrange("p a b -> p (a b)")[:, 1:8:2],
                                           1e-5, None, ALU.add)
                t_ga = S.emit("dve", fn_gather, reads=[b_mvc], writes=[b_vpad])

                def fn_sqrtw(a):
                    a.add_instruction(mybir.InstLoadActFuncSet(
                        name=nc.get_next_instruction_name(), act_func_set_id=_sqrtset,
                        ins=[], outs=[]))
                    a.activation(vpad[:], vpad[:], AF.Sqrt)
                    return a.add_instruction(mybir.InstLoadActFuncSet(
                        name=nc.get_next_instruction_name(), act_func_set_id=_siluset,
                        ins=[], outs=[]))
                t_sq2 = S.emit("act", fn_sqrtw, deps=[t_ga], reads=[b_vpad], writes=[b_vpad])

                def fn_rcp(v):
                    return v.reciprocal(ipad[:], vpad[:])
                t_rcp = S.emit("dve", fn_rcp, deps=[t_sq2], reads=[b_vpad], writes=[b_ipad])

                def fn_ipc(a):
                    return a.activation(ipc[:], ipad[:], AF.Copy)
                t_ipc = S.emit("act", fn_ipc, deps=[t_rcp], reads=[b_ipad], writes=[b_ipc])

                t_xnA = []
                for j in range(4):
                    def fn_xn(v, j=j):
                        return v.tensor_scalar(xntm[:, j, :], xntm[:, j, :], ipc[:, j:j + 1],
                                               None, ALU.mult)
                    t_xn = S.emit("dve", fn_xn, deps=[t_ipc, t_sub[j]] + bar1,
                                  reads=[b_ipc], writes=[b_xntm])
                    t_xnA.append(t_xn)

                    def fn_cast(v, j=j):
                        return v.tensor_copy(xnb_t[:], xntm[:, j, :])
                    t_cb = S.emit("dve", fn_cast, deps=[t_xn], reads=[b_xntm], writes=[b_xnb])

                    def fn_trx(t, j=j):
                        for c in range(KD):
                            pp = P[6] if c < 4 else P[7]
                            last = t.matmul(pp[:, (c % 4) * 128:(c % 4 + 1) * 128],
                                            xnb_t[:, c * 128:(c + 1) * 128], ident_sb[:],
                                            start=True, stop=True)
                        return last
                    t_trx = S.emit("pe", fn_trx, deps=[t_cb], reads=[b_xnb, b_consts],
                                   writes=[PB[6], PB[7]])

                    def fn_evx(a, j=j):
                        a.activation(xnfm2[:, 0:4, j * 128:(j + 1) * 128],
                                     P[6][:, :].rearrange("p (c t) -> p c t", t=128), AF.Copy)
                        return a.activation(xnfm2[:, 4:8, j * 128:(j + 1) * 128],
                                            P[7][:, :].rearrange("p (c t) -> p c t", t=128),
                                            AF.Copy)
                    S.emit("act", fn_evx, deps=[t_trx, t_tab2], reads=[PB[6], PB[7]],
                           writes=[b_xnfm2])

                for hh in range(4):
                    tq = _emit_quant(S, xntm[:, hh, :], 1024, 64,
                                     xq_tm[:, hh * D:(hh + 1) * D], q,
                                     deps=[t_xnA[hh]],
                                     reads=[b_xntm], writes=[b_xqtm])
                for j in range(4):
                    def fn_trq(t, j=j):
                        for c in range(KD):
                            pp = P[6] if c < 4 else P[7]
                            last = t.matmul(pp[:, (c % 4) * 128:(c % 4 + 1) * 128],
                                            xq_tm[:, j * D + c * 128:j * D + (c + 1) * 128],
                                            ident_sb[:], start=True, stop=True)
                        return last
                    t_trq = S.emit("pe", fn_trq, deps=[tq], reads=[b_xqtm, b_consts],
                                   writes=[PB[6], PB[7]])

                    def fn_evq(a, j=j):
                        a.activation(xqfm[:, 0:4, j * 128:(j + 1) * 128],
                                     P[6][:, :].rearrange("p (c t) -> p c t", t=128), AF.Copy)
                        return a.activation(xqfm[:, 4:8, j * 128:(j + 1) * 128],
                                            P[7][:, :].rearrange("p (c t) -> p c t", t=128),
                                            AF.Copy)
                    S.emit("act", fn_evq, deps=[t_trq], reads=[PB[6], PB[7]], writes=[b_xqfm])

                for j in range(4):
                    def fn_g(t, j=j):
                        for k in range(KD):
                            t.matmul(P[0][:, :], xnfm2[:, k, j * 128:(j + 1) * 128],
                                     wing_sb[:, k, 0:512], start=(k == 0), stop=(k == KD - 1))
                        for k in range(KD):
                            last = t.matmul(P[1][:, :], xnfm2[:, k, j * 128:(j + 1) * 128],
                                            wing_sb[:, k, 512:1024], start=(k == 0),
                                            stop=(k == KD - 1))
                        return last
                    t_g = S.emit("pe", fn_g, deps=[t_w2], reads=[b_xnfm2, b_wing],
                                 writes=[PB[0], PB[1]])

                    def fn_sg(a, j=j):
                        a.activation(sgtm[:, j, 0:512], P[0][:, :], AF.Silu)
                        return a.activation(sgtm[:, j, 512:1024], P[1][:, :], AF.Silu)
                    t_sg = S.emit("act", fn_sg, deps=[t_g, t_tab2], reads=[PB[0], PB[1]],
                                  writes=[b_sg])

                    def fn_y(t, j=j, T0=T0):
                        t.matmul(P[2][:, :], l_b[:, T0 + j * 128:T0 + (j + 1) * 128],
                                 wct_sb[:, 0:512], start=True, stop=True)
                        return t.matmul(P[3][:, :], l_b[:, T0 + j * 128:T0 + (j + 1) * 128],
                                        wct_sb[:, 512:1024], start=True, stop=True)
                    t_y = S.emit("pe", fn_y, deps=[t_w2] + bar1, reads=[b_l, b_wct],
                                 writes=[PB[2], PB[3]])

                    def fn_xnd(g, j=j):
                        return g.tensor_tensor(xnd[:], xntm[:, j, :], dbc[:], ALU.mult)
                    t_xd = S.emit("pool", fn_xnd, deps=[t_dbc] + bar1, reads=[b_xntm, b_dbc],
                                  writes=[b_xnd])

                    def fn_z1(v, j=j):
                        v.tensor_tensor(ztmp[:, 0:512], sgtm[:, j, 0:512], P[2][:, :], ALU.mult)
                        v.tensor_tensor(ztmp[:, 512:1024], sgtm[:, j, 512:1024], P[3][:, :],
                                        ALU.mult)
                        return v.tensor_tensor(zb_t[:], ztmp[:], xnd[:], ALU.add)
                    t_z = S.emit("dve", fn_z1, deps=[t_sg, t_y, t_xd],
                                 reads=[b_sg, PB[2], PB[3], b_xnd], writes=[b_z, b_ztmp])

                    def fn_trz(t, j=j):
                        for c in range(KD):
                            pp = P[6] if c < 4 else P[7]
                            last = t.matmul(pp[:, (c % 4) * 128:(c % 4 + 1) * 128],
                                            zb_t[:, c * 128:(c + 1) * 128], ident_sb[:],
                                            start=True, stop=True)
                        return last
                    t_trz = S.emit("pe", fn_trz, deps=[t_z], reads=[b_z, b_consts],
                                   writes=[PB[6], PB[7]])

                    def fn_evz(a, j=j):
                        a.activation(zfm[:, 0:4, :],
                                     P[6][:, :].rearrange("p (c t) -> p c t", t=128), AF.Copy)
                        return a.activation(zfm[:, 4:8, :],
                                            P[7][:, :].rearrange("p (c t) -> p c t", t=128),
                                            AF.Copy)
                    t_evz = S.emit("act", fn_evz, deps=[t_trz], reads=[PB[6], PB[7]],
                                   writes=[b_zfm])

                    def fn_s(t, j=j):
                        for k in range(KD):
                            t.matmul(P[4][:, :], zfm[:, k, :], wout_sb[:, k, 0:512],
                                     start=(k == 0), stop=(k == KD - 1))
                        for k in range(KD):
                            last = t.matmul(P[5][:, :], zfm[:, k, :], wout_sb[:, k, 512:1024],
                                            start=(k == 0), stop=(k == KD - 1))
                        return last
                    t_s = S.emit("pe", fn_s, deps=[t_evz, t_w2], reads=[b_zfm, b_wout],
                                 writes=[PB[4], PB[5]])

                    def fn_evs(a, j=j):
                        a.activation(stm[:, j, 0:512], P[4][:, :], AF.Copy)
                        return a.activation(stm[:, j, 512:1024], P[5][:, :], AF.Copy)
                    S.emit("act", fn_evs, deps=[t_s], reads=[PB[4], PB[5]], writes=[b_stm])

                if debug:
                    if S2 == 0:
                        def fn_dxn2(e, sem):
                            e.dma_start(out=dbg["xn2"], in_=xntm[:]).then_inc(sem, 16)
                        S.emit("sync", fn_dxn2, dma_n=1, reads=[b_xntm])
                    def fn_ds(e, sem, T0=T0):
                        e.dma_start(out=dbg["s"][T0:T0 + M, :].rearrange("(a p) b -> p a b", p=128),
                                    in_=stm[:]).then_inc(sem, 16)
                    S.emit("sync", fn_ds, dma_n=1, reads=[b_stm])
                    def fn_dxq(e, sem, T0=T0):
                        e.dma_start(out=dbg["xq"][T0:T0 + M, :].rearrange("(a p) b -> p a b", p=128),
                                    in_=xq_tm[:].rearrange("p (a b) -> p a b", a=4)).then_inc(sem, 16)
                    S.emit("sync", fn_dxq, dma_n=1, reads=[b_xqtm])

                for si, (e0, ew) in enumerate(SIX):
                    def fn_lw6(e, sem, e0=e0, ew=ew):
                        e.dma_start(out=wqg6[:, :, 0:ew],
                                    in_=wqgT.rearrange("(k p) e -> p k e", p=128)[:, :, e0:e0 + ew]
                                    ).then_inc(sem, 16)
                        e.dma_start(out=wqu6[:, :, 0:ew],
                                    in_=wquT.rearrange("(k p) e -> p k e", p=128)[:, :, e0:e0 + ew]
                                    ).then_inc(sem, 16)
                    t_lw = S.emit("act", fn_lw6, dma_n=2,
                                  deps=[b_wqg.wtok, b_wqu.wtok] + bar1,
                                  writes=[b_wg6, b_wu6])

                    t_hj = []
                    for j in range(4):
                        pg, pu = (P[0], P[1]) if j % 2 == 0 else (P[2], P[3])
                        bg, bu = (PB[0], PB[1]) if j % 2 == 0 else (PB[2], PB[3])

                        def fn_gu(t, j=j, ew=ew, pg=pg, pu=pu):
                            for k in range(KD):
                                t.matmul(pg[:, 0:ew], xqfm[:, k, j * 128:(j + 1) * 128],
                                         wqg6[:, k, 0:ew], start=(k == 0), stop=(k == KD - 1))
                            for k in range(KD):
                                last = t.matmul(pu[:, 0:ew], xqfm[:, k, j * 128:(j + 1) * 128],
                                                wqu6[:, k, 0:ew], start=(k == 0),
                                                stop=(k == KD - 1))
                            return last
                        t_gu = S.emit("pe", fn_gu, deps=[t_lw], reads=[b_xqfm, b_wg6, b_wu6],
                                      writes=[bg, bu])

                        def fn_sG(a, j=j, ew=ew, pg=pg):
                            return a.activation(sG[:, 0:ew], pg[:, 0:ew], AF.Silu)
                        t_sG = S.emit("act", fn_sG, deps=[t_gu, t_tab2], reads=[bg],
                                      writes=[b_sG])

                        def fn_h(v, j=j, ew=ew, pu=pu):
                            return v.tensor_tensor(h_all[:, j * ew:(j + 1) * ew], sG[:, 0:ew],
                                                   pu[:, 0:ew], ALU.mult)
                        t_h = S.emit("dve", fn_h, deps=[t_sG], reads=[b_sG, bu],
                                     writes=[b_hall])
                        t_hj.append(t_h)

                    tqh = None
                    for off in range(0, 4 * ew, 1024):
                        w6 = min(1024, 4 * ew - off)
                        tqh = _emit_quant(S, h_all[:, off:off + w6], w6, w6 // 16,
                                          hq_tm[:, off:off + w6], q, deps=t_hj,
                                          reads=[b_hall], writes=[b_hqtm])

                    nct = ew // 128
                    for j in range(4):
                        def fn_trh(t, j=j, ew=ew, nct=nct):
                            for c in range(nct):
                                last = t.matmul(P[6][:, c * 128:(c + 1) * 128],
                                                hq_tm[:, j * ew + c * 128:j * ew + (c + 1) * 128],
                                                ident_sb[:], start=True, stop=True)
                            return last
                        t_trh = S.emit("pe", fn_trh, deps=[tqh], reads=[b_hqtm, b_consts],
                                       writes=[PB[6]])

                        def fn_evh(a, j=j, e0=e0, nct=nct):
                            return a.activation(
                                hq_fm[:, e0 // 128:e0 // 128 + nct, j * 128:(j + 1) * 128],
                                P[6][:, 0:nct * 128].rearrange("p (c t) -> p c t", t=128),
                                AF.Copy)
                        S.emit("act", fn_evh, deps=[t_trh], reads=[PB[6]], writes=[b_hqfm])

                for dh in range(2):
                    PF = [P[4 * dh + 0], P[4 * dh + 1], P[4 * dh + 2], P[4 * dh + 3]]
                    PBF = [PB[4 * dh + 0], PB[4 * dh + 1], PB[4 * dh + 2], PB[4 * dh + 3]]
                    t_fp = [None] * 4
                    for k in range(KI):
                        wb = wqd_sb[k % 2]; bwb = b_wqdb[k % 2]

                        def fn_lwd(e, sem, k=k, dh=dh, wb=wb):
                            e.dma_start(out=wb[:],
                                        in_=wqdT[k * 128:(k + 1) * 128, dh * 512:(dh + 1) * 512]
                                        ).then_inc(sem, 16)
                        t_lwd = S.emit("act", fn_lwd, dma_n=1, deps=[b_wqd.wtok] + bar1,
                                       writes=[bwb])

                        for j in range(4):
                            def fn_f(t, k=k, j=j, wb=wb, PF=PF):
                                return t.matmul(PF[j][:, :], hq_fm[:, k, j * 128:(j + 1) * 128],
                                                wb[:], start=(k == 0), stop=(k == KI - 1))
                            t_fp[j] = S.emit("pe", fn_f, deps=[t_lwd], reads=[b_hqfm, bwb],
                                             writes=[PBF[j]])

                    for j in range(4):
                        tap = []
                        if debug and S2 == st2_limit - 1 and dh == 1 and j == 3:
                            def fn_fd(v, j=j, PF=PF):
                                return v.tensor_copy(fdbg[:], PF[j][:, :])
                            t_fd0 = S.emit("dve", fn_fd, deps=[t_fp[j]], reads=[PBF[j]],
                                           writes=[b_fdbg])
                            tap.append(t_fd0)
                            def fn_fdd(e, sem):
                                e.dma_start(out=dbg["fps"], in_=fdbg[:]).then_inc(sem, 16)
                            S.emit("sync", fn_fdd, dma_n=1, deps=[t_fd0], reads=[b_fdbg])
                        def fn_gsc(a, j=j, PF=PF):
                            return a.activation(gsc[:], PF[j][:, :], AF.Silu)
                        t_gs = S.emit("act", fn_gsc, deps=[t_fp[j], t_tab2] + tap,
                                      reads=[PBF[j]], writes=[b_gsc])

                        def fn_lx2(e, sem, j=j, dh=dh, T0=T0):
                            e.dma_start(out=xre[:],
                                        in_=x_tm[T0 + j * 128:T0 + (j + 1) * 128,
                                                 dh * 512:(dh + 1) * 512]).then_inc(sem, 16)
                        t_x2l = S.emit("sync", fn_lx2, dma_n=1, deps=bar1, writes=[b_xre])

                        def fn_comb(v, j=j, dh=dh):
                            v.tensor_tensor(outb[:], gsc[:], stm[:, j, dh * 512:(dh + 1) * 512],
                                            ALU.mult)
                            return v.tensor_tensor(outb[:], outb[:], xre[:], ALU.add)
                        t_cb2 = S.emit("dve", fn_comb, deps=[t_gs, t_x2l],
                                       reads=[b_gsc, b_stm, b_xre], writes=[b_outb])

                        def fn_so(e, sem, j=j, dh=dh, T0=T0):
                            e.dma_start(out=out[T0 + j * 128:T0 + (j + 1) * 128,
                                                dh * 512:(dh + 1) * 512],
                                        in_=outb[:]).then_inc(sem, 16)
                        t_out_last = S.emit("sync", fn_so, dma_n=1, reads=[b_outb],
                                            writes=[b_out])

            if debug:
                bar_end = S.barrier_tokens()
                f2w = wqd_sb[0]
                t_prev = None
                for k in range(KI):
                    def fn_l2(e, sem, k=k):
                        e.dma_start(out=f2w[:],
                                    in_=wqdT[k * 128:(k + 1) * 128, 512:1024]).then_inc(sem, 16)
                    t_l2 = S.emit("sync", fn_l2, dma_n=1, deps=bar_end, writes=[b_wqdb[0]])
                    def fn_m2(t, k=k):
                        return t.matmul(P[5][:, :], hq_fm[:, k, 3 * 128:4 * 128], f2w[:],
                                        start=(k == 0), stop=(k == KI - 1))
                    t_prev = S.emit("pe", fn_m2, deps=[t_l2] + bar_end,
                                    reads=[b_hqfm, b_wqdb[0]], writes=[PB[5]])
                def fn_e2(v):
                    return v.tensor_copy(fdbg[:], P[5][:, :])
                t_e2 = S.emit("dve", fn_e2, deps=[t_prev], reads=[PB[5]], writes=[b_fdbg])
                def fn_d2(e, sem):
                    e.dma_start(out=dbg["f2"], in_=fdbg[:]).then_inc(sem, 16)
                    e.dma_start(out=dbg["wqd"], in_=wqdT).then_inc(sem, 16)
                S.emit("sync", fn_d2, dma_n=2, deps=[t_e2], reads=[b_fdbg, b_wqd])

                def fn_dfin(e, sem):
                    e.dma_start(out=dbg["hqfm"], in_=hq_fm[:]).then_inc(sem, 16)
                    e.dma_start(out=dbg["xqfm"], in_=xqfm[:]).then_inc(sem, 16)
                    e.dma_start(out=dbg["sgtm"], in_=sgtm[:]).then_inc(sem, 16)
                    e.dma_start(out=dbg["gsc"], in_=gsc[:]).then_inc(sem, 16)
                    e.dma_start(out=dbg["mvc"], in_=mvc[:].rearrange("p a b -> p (a b)")).then_inc(sem, 16)
                    e.dma_start(out=dbg["ipc"], in_=ipc[:]).then_inc(sem, 16)
                    e.dma_start(out=dbg["xnf"], in_=xntm[:]).then_inc(sem, 16)
                    e.dma_start(out=dbg["outb"], in_=outb[:]).then_inc(sem, 16)
                S.emit("sync", fn_dfin, dma_n=8,
                       reads=[b_hqfm, b_xqfm, b_sg, b_gsc, b_outb, b_mvc, b_ipc, b_xntm],
                       deps=S.barrier_tokens())

            def fn_fin(v):
                return v.engine_nop()
            S.emit("dve", fn_fin, deps=[t_out_last] + S.barrier_tokens())

        with nc.Block() as block:
            @block.tensor
            def _(eng):
                for st in S.steps["pe"]:
                    st(eng)

            @block.vector
            def _(eng):
                for st in S.steps["dve"]:
                    st(eng)

            @block.scalar
            def _(eng):
                for st in S.steps["act"]:
                    st(eng)

            @block.gpsimd
            def _(eng):
                for st in S.steps["pool"]:
                    st(eng)

            @block.sync
            def _(eng):
                for st in S.steps["sync"]:
                    st(eng)

    return nc


def _in_maps(inp):
    x = np.asarray(inp["L0_attn_x"], dtype=np.float32)
    W_in = np.asarray(inp["L0_attn_W_in"], dtype=np.float32)
    W_state = np.asarray(inp["L0_attn_W_state"], dtype=np.float32)
    W_dt = np.asarray(inp["L0_attn_W_dt"], dtype=np.float32)
    b_dt = np.asarray(inp["L0_attn_b_dt"], dtype=np.float32)
    A_log = np.asarray(inp["L0_attn_A_log"], dtype=np.float32)
    W_C = np.asarray(inp["L0_attn_W_C"], dtype=np.float32)
    Dv = np.asarray(inp["L0_attn_D"], dtype=np.float32)
    W_out = np.asarray(inp["L0_attn_W_out"], dtype=np.float32)
    w_gate = np.asarray(inp["L0_ffn_w_gate"], dtype=np.float32)
    w_up = np.asarray(inp["L0_ffn_w_up"], dtype=np.float32)
    w_down = np.asarray(inp["L0_ffn_w_down"], dtype=np.float32)

    import ml_dtypes
    WiuT = np.ascontiguousarray(W_in.T[:, :D])
    WigT = np.ascontiguousarray(W_in.T[:, D:])
    WstT = np.ascontiguousarray(W_state.T)
    WdtT = np.ascontiguousarray(W_dt.T)
    WcT = np.ascontiguousarray(W_C.T)
    WoT = np.ascontiguousarray(W_out.T)
    ident = np.eye(128, dtype=ml_dtypes.bfloat16)
    bdtc = np.ascontiguousarray(b_dt.reshape(NST, 1))
    alogc = np.ascontiguousarray(A_log.reshape(NST, 1))
    dvc = np.ascontiguousarray(Dv.reshape(1, D))

    maps = []
    for core in range(8):
        b, half = core // 2, core % 2
        xb = x[b]
        if half == 0:
            xroll = np.concatenate([xb[TOK:], xb[:TOK]], axis=0)
        else:
            xroll = xb
        cm = np.ones((128, ST1), np.float32)
        if half == 0:
            cm[:, 7] = 0.0
        maps.append({
            "xT": np.ascontiguousarray(xroll.T),
            "x_tm": np.ascontiguousarray(xb[half * TOK:(half + 1) * TOK]),
            "WiuT": WiuT, "WigT": WigT, "WstT": WstT, "WdtT": WdtT,
            "WcT": WcT, "WoT": WoT,
            "wg": w_gate, "wu": w_up, "wd": w_down,
            "bdt": bdtc, "alog": alogc, "dvec": dvc,
            "identI": ident, "cmask": cm,
        })
    return maps


def kernel(**inp):
    maps = _in_maps(inp)
    nc = build_nc(debug=False)
    res = run_bass_kernel_spmd(nc, maps, core_ids=list(range(8)))
    out = np.empty((4, TOKF, D), dtype=np.float32)
    for core in range(8):
        b, half = core // 2, core % 2
        out[b, half * TOK:(half + 1) * TOK] = res.results[core]["out"]
    return out


# version 04
# speedup vs baseline: 1.2431x; 1.0787x over previous
import contextlib
import numpy as np

import concourse.bass as bass
import concourse.mybir as mybir
from concourse.bass_utils import run_bass_kernel_spmd

F32 = mybir.dt.float32
BF16 = mybir.dt.bfloat16
AF = mybir.ActivationFunctionType
ALU = mybir.AluOpType
AX = mybir.AxisListType

MAGIC = float(1.5 * 2 ** 23)
T7 = 10.0 / 12.0
D = 1024
E2 = 2048
NST = 128
I3 = 2816
TOK = 4096
TOKF = 8192
ST2 = 8
ST1 = 16
M = 512
KD = D // 128
KI = I3 // 128

ENGS = ("pe", "dve", "act", "pool", "sync")
DMA_ENGS = ("sync", "act", "pool")


class Buf:

    __slots__ = ("wtok", "rtoks")

    def __init__(self):
        self.wtok = None
        self.rtoks = []


class Sched:

    NDMA = 8

    def __init__(self, nc, sems):
        self.nc = nc
        self.sems = sems
        self.steps = {e: [] for e in ENGS}
        self.gcnt = {e: 0 for e in ENGS}
        self.dgidx = {e: 0 for e in DMA_ENGS}
        self.dcnt = {}
        self.hi = {}

    def _wait(self, eng_name, deps):
        best = {}
        for tok in deps:
            if tok is None:
                continue
            semname, val = tok
            if val <= 0:
                continue
            if semname.startswith("g:") and semname[2:] == eng_name:
                continue
            if best.get(semname, 0) < val:
                best[semname] = val
        out = []
        for semname, val in best.items():
            key = (eng_name, semname)
            if self.hi.get(key, 0) >= val:
                continue
            self.hi[key] = val
            out.append((self.sems[semname], val))
        return out

    def _wait_nodedup(self, eng_name, deps, strict=False):
        best = {}
        for tok in deps:
            if tok is None:
                continue
            semname, val = tok
            if val <= 0:
                continue
            if not strict and semname.startswith("g:") and semname[2:] == eng_name:
                continue
            if best.get(semname, 0) < val:
                best[semname] = val
        return [(self.sems[s], v) for s, v in best.items()]

    def emit(self, eng_name, fn, deps=(), reads=(), writes=(), dma_n=0, strict=False):
        deps = list(deps)
        for b in reads:
            deps.append(b.wtok)
        for b in writes:
            deps.append(b.wtok)
            deps.extend(b.rtoks)
        if dma_n:
            rot = self.dgidx[eng_name] % Sched.NDMA
            self.dgidx[eng_name] += 1
            semname = f"d:{eng_name}:{rot}"
            prev = self.dcnt.get((eng_name, rot), 0)
            if prev > 0:
                deps.append((semname, prev))
            new_total = prev + 16 * dma_n
            self.dcnt[(eng_name, rot)] = new_total
            waits = self._wait(eng_name, deps)
            tok = (semname, new_total)
            sem = self.sems[semname]

            def step(eng, waits=waits, fn=fn, sem=sem):
                for s, v in waits:
                    eng.wait_ge(s, v)
                fn(eng, sem)

        else:
            waits = self._wait(eng_name, deps)
            semname = "g:" + eng_name
            self.gcnt[eng_name] += 1
            tok = (semname, self.gcnt[eng_name])
            sem = self.sems[semname]

            def step(eng, waits=waits, fn=fn, sem=sem):
                for s, v in waits:
                    eng.wait_ge(s, v)
                last = fn(eng)
                last.then_inc(sem, 1)

        self.steps[eng_name].append(step)
        for b in reads:
            b.rtoks.append(tok)
        for b in writes:
            b.wtok = tok
            b.rtoks = []
        return tok

    def barrier_tokens(self):
        toks = [("g:" + e, self.gcnt[e]) for e in ENGS]
        toks += [(f"d:{q}:{r}", v) for (q, r), v in self.dcnt.items()]
        return toks


def _emit_quant(S, xn_ap, nelem, nb, out_ap, q, dve="dve", pool="pool",
                 deps=(), reads=(), writes=()):
    x3 = xn_ap.rearrange("p (b w) -> p b w", w=16)
    bmax, rinv, sc12 = q["bmax"], q["rinv"], q["sc12"]
    tF, b1, b2, b3, b4 = q["tF"], q["b1"], q["b2"], q["b3"], q["b4"]

    if nb >= 64:
        def red(v):
            v.reduce_max(bmax[:, :nb], x3, axis=AX.X, apply_absolute_value=True)
            v.tensor_scalar(rinv[:, :nb], bmax[:, :nb], 6e-12, None, ALU.max)
            v.reciprocal(bmax[:, :nb], rinv[:, :nb])
            return v.tensor_scalar(sc12[:, :nb], rinv[:, :nb], 1.0 / 12.0, None, ALU.mult)

        t_red = S.emit(dve, red, deps=deps, reads=list(reads),
                       writes=[q["_bmax"], q["_rinv"], q["_sc12"]])
    else:
        def red_a(v):
            return v.reduce_max(bmax[:, :nb], x3, axis=AX.X, apply_absolute_value=True)
        t_ra = S.emit(dve, red_a, deps=deps, reads=list(reads), writes=[q["_bmax"]])

        def red_b(g):
            return g.tensor_scalar(rinv[:, :nb], bmax[:, :nb], 6e-12, None, ALU.max)
        t_rb = S.emit(pool, red_b, deps=[t_ra], reads=[q["_bmax"]], writes=[q["_rinv"]])

        def red_c(v):
            return v.reciprocal(bmax[:, :nb], rinv[:, :nb])
        t_rc = S.emit(dve, red_c, deps=[t_rb], reads=[q["_rinv"]], writes=[q["_bmax"]])

        def red_d(g):
            return g.tensor_scalar(sc12[:, :nb], rinv[:, :nb], 1.0 / 12.0, None, ALU.mult)
        t_red = S.emit(pool, red_d, deps=[t_rc], reads=[q["_rinv"]], writes=[q["_sc12"]])

    def pmul(g):
        rb = bmax[:, :nb].unsqueeze(2).broadcast_to([128, nb, 16])
        return g.tensor_tensor(q["xnq"][:, :nelem].rearrange("p (b w) -> p b w", w=16),
                               x3, rb, ALU.mult)

    t_xn = S.emit(pool, pmul, deps=[t_red], reads=list(reads) + [q["_bmax"]],
                  writes=[q["_xnq"]])
    xq = q["xnq"][:, :nelem]

    def ladder(v):
        v.tensor_scalar(tF[:, :nelem], xq, 12.0, MAGIC, ALU.mult, ALU.add)
        v.tensor_scalar(b1[:, :nelem], tF[:, :nelem], MAGIC, 4.0, ALU.subtract, ALU.min)
        v.tensor_scalar(tF[:, :nelem], xq, 6.0, MAGIC, ALU.mult, ALU.add)
        v.tensor_scalar(b2[:, :nelem], tF[:, :nelem], MAGIC, 4.0, ALU.subtract, ALU.min)
        v.tensor_scalar(b3[:, :nelem], b1[:, :nelem], -4.0, None, ALU.max)
        v.tensor_scalar(b1[:, :nelem], b2[:, :nelem], 2.0, None, ALU.max)
        v.tensor_scalar(b4[:, :nelem], b2[:, :nelem], -2.0, -4.0, ALU.min, ALU.max)
        v.tensor_tensor(b2[:, :nelem], b1[:, :nelem], b4[:, :nelem], ALU.add)
        v.scalar_tensor_tensor(b1[:, :nelem], b2[:, :nelem], 2.0, b3[:, :nelem],
                               ALU.mult, ALU.add)
        v.tensor_scalar(b3[:, :nelem], xq, T7, 4.0, ALU.is_gt, ALU.mult)
        v.tensor_scalar(b4[:, :nelem], xq, -T7, 4.0, ALU.is_lt, ALU.mult)
        v.tensor_tensor(b2[:, :nelem], b3[:, :nelem], b4[:, :nelem], ALU.subtract)
        return v.tensor_tensor(b3[:, :nelem], b1[:, :nelem], b2[:, :nelem], ALU.add)

    t_lad = S.emit(dve, ladder, deps=[t_xn], reads=[q["_xnq"]],
                   writes=[q["_tF"], q["_b1"], q["_b2"], q["_b3"], q["_b4"]])

    def fmul(g):
        scb = sc12[:, :nb].unsqueeze(2).broadcast_to([128, nb, 16])
        return g.tensor_tensor(out_ap.rearrange("p (b w) -> p b w", w=16),
                               q["b3"][:, :nelem].rearrange("p (b w) -> p b w", w=16),
                               scb, ALU.mult)

    return S.emit(pool, fmul, deps=[t_lad], reads=[q["_b3"], q["_sc12"]],
                  writes=list(writes))


def build_nc(debug=False, st2_limit=ST2, serialize=False):
    nc = bass.Bass("TRN2", target_bir_lowering=False)

    xT = nc.dram_tensor("xT", [D, TOKF], F32, kind="ExternalInput").ap()
    x_tm = nc.dram_tensor("x_tm", [TOK, D], F32, kind="ExternalInput").ap()
    WiuT = nc.dram_tensor("WiuT", [D, D], F32, kind="ExternalInput").ap()
    WigT = nc.dram_tensor("WigT", [D, D], F32, kind="ExternalInput").ap()
    WstT = nc.dram_tensor("WstT", [D, NST], F32, kind="ExternalInput").ap()
    WdtT = nc.dram_tensor("WdtT", [D, NST], F32, kind="ExternalInput").ap()
    WcT = nc.dram_tensor("WcT", [NST, D], F32, kind="ExternalInput").ap()
    WoT = nc.dram_tensor("WoT", [D, D], F32, kind="ExternalInput").ap()
    wg = nc.dram_tensor("wg", [I3, D], F32, kind="ExternalInput").ap()
    wu = nc.dram_tensor("wu", [I3, D], F32, kind="ExternalInput").ap()
    wd = nc.dram_tensor("wd", [D, I3], F32, kind="ExternalInput").ap()
    bdt = nc.dram_tensor("bdt", [NST, 1], F32, kind="ExternalInput").ap()
    alog = nc.dram_tensor("alog", [NST, 1], F32, kind="ExternalInput").ap()
    dvec = nc.dram_tensor("dvec", [1, D], F32, kind="ExternalInput").ap()
    identI = nc.dram_tensor("identI", [128, 128], BF16, kind="ExternalInput").ap()
    cmask = nc.dram_tensor("cmask", [128, ST1], F32, kind="ExternalInput").ap()
    out = nc.dram_tensor("out", [TOK, D], F32, kind="ExternalOutput").ap()
    wqgT = nc.dram_tensor("wqgT", [D, I3], BF16).ap()
    wquT = nc.dram_tensor("wquT", [D, I3], BF16).ap()
    wqdT = nc.dram_tensor("wqdT", [I3, D], BF16).ap()

    dbg = {}
    if debug:
        dbg["l"] = nc.dram_tensor("dbg_l", [NST, TOK], BF16, kind="ExternalOutput").ap()
        dbg["mcols"] = nc.dram_tensor("dbg_mcols", [128, 32], F32, kind="ExternalOutput").ap()
        dbg["icols"] = nc.dram_tensor("dbg_icols", [128, 32], F32, kind="ExternalOutput").ap()
        dbg["wqg"] = nc.dram_tensor("dbg_wqg", [D, I3], BF16, kind="ExternalOutput").ap()
        dbg["s"] = nc.dram_tensor("dbg_s", [TOK, D], BF16, kind="ExternalOutput").ap()
        dbg["xn1"] = nc.dram_tensor("dbg_xn1", [128, KD, M], BF16, kind="ExternalOutput").ap()
        dbg["u"] = nc.dram_tensor("dbg_u", [128, KD, M], BF16, kind="ExternalOutput").ap()
        dbg["dec"] = nc.dram_tensor("dbg_dec", [NST, M], F32, kind="ExternalOutput").ap()
        dbg["ef"] = nc.dram_tensor("dbg_ef", [NST, M], F32, kind="ExternalOutput").ap()
        dbg["xn2"] = nc.dram_tensor("dbg_xn2", [128, 4, D], F32, kind="ExternalOutput").ap()
        dbg["dt"] = nc.dram_tensor("dbg_dt", [NST, M], F32, kind="ExternalOutput").ap()
        dbg["bt"] = nc.dram_tensor("dbg_bt", [NST, M], F32, kind="ExternalOutput").ap()
        dbg["rate"] = nc.dram_tensor("dbg_rate", [NST, 1], F32, kind="ExternalOutput").ap()
        dbg["hqfm"] = nc.dram_tensor("dbg_hqfm", [128, KI, M], BF16, kind="ExternalOutput").ap()
        dbg["xqfm"] = nc.dram_tensor("dbg_xqfm", [128, KD, M], BF16, kind="ExternalOutput").ap()
        dbg["sgtm"] = nc.dram_tensor("dbg_sgtm", [128, D], BF16, kind="ExternalOutput").ap()
        dbg["fps"] = nc.dram_tensor("dbg_fps", [128, 512], F32, kind="ExternalOutput").ap()
        dbg["f2"] = nc.dram_tensor("dbg_f2", [128, 512], F32, kind="ExternalOutput").ap()
        dbg["wqd"] = nc.dram_tensor("dbg_wqd", [I3, D], BF16, kind="ExternalOutput").ap()
        dbg["gsc"] = nc.dram_tensor("dbg_gsc", [128, 512], F32, kind="ExternalOutput").ap()
        dbg["outb"] = nc.dram_tensor("dbg_outb", [128, 512], F32, kind="ExternalOutput").ap()
        dbg["mvc"] = nc.dram_tensor("dbg_mvc", [128, 8], F32, kind="ExternalOutput").ap()
        dbg["ipc"] = nc.dram_tensor("dbg_ipc", [128, 64], F32, kind="ExternalOutput").ap()
        dbg["xnf"] = nc.dram_tensor("dbg_xnf", [128, 4, D], F32, kind="ExternalOutput").ap()
        dbg["xq"] = nc.dram_tensor("dbg_xq", [TOK, D], BF16, kind="ExternalOutput").ap()

    semnames = ["g:" + e for e in ENGS] + [
        f"d:{q}:{r}" for q in DMA_ENGS for r in range(Sched.NDMA)]

    with contextlib.ExitStack() as ctx:
        sems = {n: ctx.enter_context(nc.semaphore(n.replace(":", "_"))) for n in semnames}
        S = Sched(nc, sems)
        B = lambda: Buf()

        sb = lambda name, shape, dt: ctx.enter_context(nc.sbuf_tensor(name, shape, dt))
        ps = lambda name, shape: ctx.enter_context(nc.psum_tensor(name, shape, F32))

        l_b = sb("l_b", [NST, TOK], BF16)
        ident_sb = sb("ident_sb", [128, 128], BF16)
        ident1 = sb("ident1", [1, 1], F32)
        ones_col = sb("ones_col", [128, 1], BF16)
        ones_row = sb("ones_row", [1, 128], F32)
        bdt_sb = sb("bdt_sb", [NST, 1], F32)
        rate_sb = sb("rate_sb", [NST, 1], F32)
        dbc = sb("dbc", [128, D], F32)
        cmask_sb = sb("cmask_sb", [128, ST1], F32)
        mcols = sb("mcols", [128, 32], F32)
        icols = sb("icols", [128, 32], F32)
        wing_sb = sb("wing_sb", [128, KD, D], BF16)
        wct_sb = sb("wct_sb", [NST, D], BF16)
        wout_sb = sb("wout_sb", [128, KD, D], BF16)
        QF = 1024
        q = {"bmax": sb("q_bmax", [128, QF // 16], F32),
             "rinv": sb("q_rinv", [128, QF // 16], F32),
             "sc12": sb("q_sc12", [128, QF // 16], F32),
             "xnq": sb("q_xnq", [128, QF], F32),
             "tF": sb("q_tF", [128, QF], F32),
             "b1": sb("q_b1", [128, QF], BF16),
             "b2": sb("q_b2", [128, QF], BF16),
             "b3": sb("q_b3", [128, QF], BF16),
             "b4": sb("q_b4", [128, QF], BF16)}
        for k in ("bmax", "rinv", "sc12", "xnq", "tF", "b1", "b2", "b3", "b4"):
            q["_" + k] = B()

        P = [ps(f"ps{i}", [128, 512]) for i in range(8)]
        PB = [B() for _ in range(8)]

        b_l = B(); b_mcols = B(); b_icols = B(); b_consts = B()
        b_wing = B(); b_wct = B(); b_wout = B()
        b_wqg = B(); b_wqu = B(); b_wqd = B()

        def fn_consts(e, sem):
            e.dma_start(out=ident_sb[:], in_=identI).then_inc(sem, 16)
            e.dma_start(out=bdt_sb[:], in_=bdt).then_inc(sem, 16)
            e.dma_start(out=rate_sb[:], in_=alog).then_inc(sem, 16)
            e.dma_start(out=cmask_sb[:], in_=cmask).then_inc(sem, 16)
            e.dma_start(out=dbc[0:1, :], in_=dvec).then_inc(sem, 16)
        t_c0 = S.emit("sync", fn_consts, dma_n=5, writes=[b_consts])

        def fn_memset(v):
            v.memset(ones_col[:], 1.0)
            v.memset(ones_row[:], 1.0)
            return v.memset(ident1[:], 1.0)
        t_ms = S.emit("dve", fn_memset, writes=[b_consts])

        def fn_table1(a):
            return a.preload_act_table(AF.Ln)
        from concourse.hw_specs import get_activation_tables
        _tabs = get_activation_tables(nc.m.arch)
        _explnset = None
        for _i, (_n, _fs) in enumerate(_tabs.items()):
            if AF.Exp in _fs and AF.Ln in _fs:
                _explnset = _i
                break
        _siluset = None
        for _i, (_n, _fs) in enumerate(_tabs.items()):
            if AF.Silu in _fs:
                _siluset = _i
                break
        _sqrtset = None
        for _i, (_n, _fs) in enumerate(_tabs.items()):
            if AF.Sqrt in _fs:
                _sqrtset = _i
                break

        def fn_tab_expln(a):
            return a.add_instruction(mybir.InstLoadActFuncSet(
                name=nc.get_next_instruction_name(), act_func_set_id=_explnset,
                ins=[], outs=[]))
        t_tab1 = S.emit("act", fn_tab_expln)

        rt1 = sb("rt1", [NST, 1], F32); rt2 = sb("rt2", [NST, 1], F32)
        b_rate = B()

        def fn_rate(a):
            a.activation(rt1[:], rate_sb[:], AF.Abs)
            a.activation(rt1[:], rt1[:], AF.Exp, scale=-1.0)
            a.activation(rt1[:], rt1[:], AF.Ln, bias=1.0)
            return a.activation(rt2[:], rate_sb[:], AF.Relu)
        t_r1 = S.emit("act", fn_rate, deps=[t_c0, t_tab1], reads=[b_consts], writes=[b_rate])

        def fn_rate2(v):
            return v.scalar_tensor_tensor(rate_sb[:], rt1[:], 1e-4, rt2[:],
                                          ALU.add, ALU.add)
        t_rate = S.emit("dve", fn_rate2, deps=[t_r1], writes=[b_rate])

        b_dbc = B()

        def fn_dbc_mm(t):
            t.matmul(P[6][:, 0:512], ones_row[:], dbc[0:1, 0:512], start=True, stop=True)
            return t.matmul(P[7][:, 0:512], ones_row[:], dbc[0:1, 512:1024], start=True, stop=True)
        t_dbm = S.emit("pe", fn_dbc_mm, deps=[t_c0, t_ms], reads=[b_consts],
                       writes=[PB[6], PB[7]])

        def fn_dbc_ev(a):
            a.activation(dbc[:, 0:512], P[6][:, 0:512], AF.Copy)
            return a.activation(dbc[:, 512:1024], P[7][:, 0:512], AF.Copy)
        t_dbc = S.emit("act", fn_dbc_ev, deps=[t_dbm, t_tab1], reads=[PB[6], PB[7]], writes=[b_dbc])

        def fn_w2(e, sem):
            e.dma_start(out=wing_sb[:], in_=WigT.rearrange("(k p) e -> p k e", p=128)).then_inc(sem, 16)
            e.dma_start(out=wct_sb[:], in_=WcT).then_inc(sem, 16)
            e.dma_start(out=wout_sb[:], in_=WoT.rearrange("(k p) e -> p k e", p=128)).then_inc(sem, 16)
        t_w2 = S.emit("pool", fn_w2, dma_n=3, writes=[b_wing, b_wct, b_wout])

        with contextlib.ExitStack() as prep_ctx:
            stg = prep_ctx.enter_context(nc.sbuf_tensor("stg", [128, KI * D], BF16))
            wrow = prep_ctx.enter_context(nc.sbuf_tensor("wrow", [128, I3], F32))
            wqrow = prep_ctx.enter_context(nc.sbuf_tensor("wqrow", [128, I3], BF16))
            b_stg = [B() for _ in range(KI)]
            b_wrow = B(); b_wqrow = B()

            def prep_weight(src, dst, rows, cols, b_dst):
                nrt = rows // 128
                nct = cols // 128
                for rt in range(nrt):
                    def fn_ld(e, sem, rt=rt):
                        e.dma_start(out=wrow[:, 0:cols],
                                    in_=src[rt * 128:(rt + 1) * 128, :]).then_inc(sem, 16)
                    t_ld = S.emit("sync", fn_ld, dma_n=1, writes=[b_wrow])
                    toks = []
                    for off in range(0, cols, QF):
                        w = min(QF, cols - off)
                        tq = _emit_quant(S, wrow[:, off:off + w], w, w // 16,
                                         wqrow[:, off:off + w], q,
                                         deps=[t_ld], reads=[b_wrow], writes=[b_wqrow])
                        toks.append(tq)
                    for cc in range(nct):
                        pslot = 6 + (cc % 2)
                        def fn_tr(t, cc=cc, pslot=pslot):
                            return t.matmul(P[pslot][:, 0:128],
                                            wqrow[:, cc * 128:(cc + 1) * 128],
                                            ident_sb[:], start=True, stop=True)
                        t_tr = S.emit("pe", fn_tr, deps=toks + [t_c0],
                                      reads=[b_wqrow, b_consts], writes=[PB[pslot]])
                        def fn_ev(a, cc=cc, rt=rt, pslot=pslot, rows=rows):
                            return a.activation(stg[:, cc * rows + rt * 128:cc * rows + (rt + 1) * 128],
                                                P[pslot][:, 0:128], AF.Copy)
                        S.emit("act", fn_ev, deps=[t_tr], reads=[PB[pslot]], writes=[b_stg[cc]])
                for cc in range(nct):
                    def fn_st(e, sem, cc=cc, rows=rows):
                        e.dma_start(out=dst[cc * 128:(cc + 1) * 128, :],
                                    in_=stg[:, cc * rows:(cc + 1) * rows]).then_inc(sem, 16)
                    S.emit("sync", fn_st, dma_n=1, reads=[b_stg[cc]], writes=[b_dst])

            prep_weight(wg, wqgT, I3, D, b_wqg)
            prep_weight(wu, wquT, I3, D, b_wqu)
            prep_weight(wd, wqdT, D, I3, b_wqd)

            if debug:
                def fn_dwq(e, sem):
                    e.dma_start(out=dbg["wqg"], in_=wqgT).then_inc(sem, 16)
                S.emit("sync", fn_dwq, dma_n=1, reads=[b_wqg])

        bar0 = S.barrier_tokens()
        p1 = {}
        with contextlib.ExitStack() as p1ctx:
            sb1 = lambda name, shape, dt: p1ctx.enter_context(nc.sbuf_tensor(name, shape, dt))
            wiu_sb = sb1("wiu_sb", [128, KD, D], BF16)
            wst_sb = sb1("wst_sb", [128, KD, NST], BF16)
            wdt_sb = sb1("wdt_sb", [128, KD, NST], BF16)
            b_wiu = B(); b_wst = B(); b_wdt = B()

            def fn_w1(e, sem):
                e.dma_start(out=wiu_sb[:], in_=WiuT.rearrange("(k p) e -> p k e", p=128)).then_inc(sem, 16)
                e.dma_start(out=wst_sb[:], in_=WstT.rearrange("(k p) n -> p k n", p=128)).then_inc(sem, 16)
                e.dma_start(out=wdt_sb[:], in_=WdtT.rearrange("(k p) n -> p k n", p=128)).then_inc(sem, 16)
            t_w1 = S.emit("pool", fn_w1, dma_n=3, deps=bar0,
                          writes=[b_wiu, b_wst, b_wdt])
            xfm = [sb1(f"xfm{i}", [128, KD, M], BF16) for i in range(2)]
            x2fm = sb1("x2fm", [128, KD, M], BF16)
            xnfm = sb1("xnfm", [128, KD, M], BF16)
            ufm = sb1("ufm", [128, KD, M], BF16)
            tscr = sb1("tscr", [128, M], F32)
            mrow = sb1("mrow", [1, M], F32)
            irow = sb1("irow", [1, M], F32)
            mbs = sb1("mbs", [128, M], F32)
            ibs = sb1("ibs", [128, M], F32)
            dtc = sb1("dtc", [128, M], F32)
            dtc2 = sb1("dtc2", [128, M], F32)
            dec = sb1("dec", [128, M], F32)
            ef = sb1("ef", [128, M], F32)
            lscr = sb1("lscr", [128, M], F32)
            carry = sb1("carry", [NST, 1], F32)
            b_xfm = [B(), B()]
            b_x2 = B(); b_xn1 = B(); b_u = B(); b_stat = B(); b_bc = B()
            b_dt = B(); b_dec = B(); b_ef = B(); b_l1 = B(); b_carry = B()
            b_tscr = B()

            def fn_carry0(v):
                return v.memset(carry[:], 0.0)
            t_cy = S.emit("dve", fn_carry0, deps=bar0, writes=[b_carry])

            for s in range(ST1):
                xb = xfm[s % 2]
                bxb = b_xfm[s % 2]

                def fn_lx(e, sem, s=s, xb=xb):
                    e.dma_start(out=xb[:],
                                in_=xT.rearrange("(c p) t -> p c t", p=128)[:, :, s * M:(s + 1) * M]
                                ).then_inc(sem, 16)
                t_lx = S.emit("pool", fn_lx, dma_n=1, deps=bar0, writes=[bxb])

                def fn_sq(a, xb=xb):
                    return a.activation(x2fm[:], xb[:], AF.Square)
                t_sq = S.emit("act", fn_sq, deps=[t_lx, t_tab1], reads=[bxb], writes=[b_x2])

                def fn_cs(t, xb=xb):
                    for c in range(KD):
                        t.matmul(P[4][0:1, :], ones_col[:], xb[:, c, :],
                                 start=(c == 0), stop=(c == KD - 1))
                    for c in range(KD):
                        last = t.matmul(P[5][0:1, :], ones_col[:], x2fm[:, c, :],
                                        start=(c == 0), stop=(c == KD - 1))
                    return last
                t_cs = S.emit("pe", fn_cs, deps=[t_lx, t_sq, t_ms], reads=[bxb, b_x2],
                              writes=[PB[4], PB[5]])

                def fn_stat(v):
                    v.tensor_scalar(mrow[:], P[4][0:1, :], 1.0 / D, None, ALU.mult)
                    v.tensor_scalar(irow[:], P[5][0:1, :], 1.0 / D, None, ALU.mult)
                    v.tensor_tensor(tscr[0:1, :], mrow[:], mrow[:], ALU.mult)
                    v.tensor_tensor(irow[:], irow[:], tscr[0:1, :], ALU.subtract)
                    return v.tensor_scalar(irow[:], irow[:], 1e-5, None, ALU.add)
                t_st = S.emit("dve", fn_stat, deps=[t_cs], reads=[PB[4], PB[5]],
                              writes=[b_stat, b_tscr])

                def fn_inv(a):
                    a.activation(irow[:], irow[:], AF.Ln)
                    return a.activation(irow[:], irow[:], AF.Exp, scale=-0.5)
                t_inv = S.emit("act", fn_inv, deps=[t_st], writes=[b_stat])

                def fn_bc(t):
                    t.matmul(P[4][:, :], ones_row[:], mrow[:], start=True, stop=True)
                    return t.matmul(P[5][:, :], ones_row[:], irow[:], start=True, stop=True)
                t_bc = S.emit("pe", fn_bc, deps=[t_inv, t_ms], reads=[b_stat],
                              writes=[PB[4], PB[5]])

                def fn_bc_ev(a):
                    a.activation(mbs[:], P[4][:, :], AF.Copy)
                    return a.activation(ibs[:], P[5][:, :], AF.Copy)
                t_bce = S.emit("act", fn_bc_ev, deps=[t_bc], reads=[PB[4], PB[5]], writes=[b_bc])

                if s >= 8:
                    S2i = s - 8

                    def fn_strm(t, S2i=S2i):
                        for cc in range(4):
                            t.matmul(P[6][:, cc:cc + 1], mrow[0:1, cc * 128:(cc + 1) * 128],
                                     ident1[:], start=True, stop=True)
                        for cc in range(4):
                            last = t.matmul(P[6][:, 4 + cc:5 + cc],
                                            irow[0:1, cc * 128:(cc + 1) * 128],
                                            ident1[:], start=True, stop=True)
                        return last
                    t_str = S.emit("pe", fn_strm, deps=[t_inv, t_ms], reads=[b_stat],
                                   writes=[PB[6]])

                    def fn_stre(a, S2i=S2i):
                        a.activation(mcols[:, S2i * 4:(S2i + 1) * 4], P[6][:, 0:4], AF.Copy)
                        return a.activation(icols[:, S2i * 4:(S2i + 1) * 4], P[6][:, 4:8], AF.Copy)
                    S.emit("act", fn_stre, deps=[t_str], reads=[PB[6]],
                           writes=[b_mcols, b_icols])

                def fn_xn1(v, xb=xb):
                    for c in range(KD):
                        last = v.tensor_tensor(xnfm[:, c, :], xb[:, c, :], mbs[:], ALU.subtract)
                    return last
                t_x1 = S.emit("dve", fn_xn1, deps=[t_bce], reads=[bxb, b_bc], writes=[b_xn1])

                def fn_xn2(g):
                    for c in range(KD):
                        last = g.tensor_tensor(xnfm[:, c, :], xnfm[:, c, :], ibs[:], ALU.mult)
                    return last
                t_x2 = S.emit("pool", fn_xn2, deps=[t_x1], reads=[b_bc], writes=[b_xn1])

                t_uev = []
                for e8 in range(KD):
                    pslot = e8 % 2
                    def fn_mm(t, e8=e8, pslot=pslot):
                        for k in range(KD):
                            last = t.matmul(P[pslot][:, :],
                                            wiu_sb[:, k, e8 * 128:(e8 + 1) * 128],
                                            xnfm[:, k, :], start=(k == 0), stop=(k == KD - 1))
                        return last
                    t_mm = S.emit("pe", fn_mm, deps=[t_x2, t_w1], reads=[b_xn1, b_wiu],
                                  writes=[PB[pslot]])

                    def fn_ev(a, e8=e8, pslot=pslot):
                        return a.activation(ufm[:, e8, :], P[pslot][:, :], AF.Copy)
                    t_uev.append(S.emit("act", fn_ev, deps=[t_mm], reads=[PB[pslot]],
                                        writes=[b_u]))

                def fn_bt2(t):
                    for k in range(KD):
                        t.matmul(P[2][:, :], wst_sb[:, k, :], ufm[:, k, :],
                                 start=(k == 0), stop=(k == KD - 1))
                    for k in range(KD):
                        lb = t.matmul(P[3][:, :], wdt_sb[:, k, :], ufm[:, k, :],
                                      start=(k == 0), stop=(k == KD - 1))
                    return lb
                t_bt = S.emit("pe", fn_bt2, deps=t_uev + [t_w1], reads=[b_u, b_wst, b_wdt],
                              writes=[PB[2], PB[3]])

                def fn_dta(a):
                    a.activation(dtc[:], P[3][:, :], AF.Abs, bias=bdt_sb[:])
                    a.activation(dtc2[:], P[3][:, :], AF.Relu, bias=bdt_sb[:])
                    a.activation(dtc[:], dtc[:], AF.Exp, scale=-1.0)
                    return a.activation(dtc[:], dtc[:], AF.Ln, bias=1.0)
                t_dta = S.emit("act", fn_dta, deps=[t_bt, t_rate, t_c0], reads=[PB[3], b_consts],
                               writes=[b_dt])

                def fn_dtb(v):
                    v.tensor_tensor(dtc[:], dtc[:], dtc2[:], ALU.add)
                    v.tensor_scalar(dtc[:], dtc[:], 10.0, None, ALU.min)
                    return v.tensor_scalar(dtc[:], dtc[:], rate_sb[:], None, ALU.mult)
                t_dtb = S.emit("dve", fn_dtb, deps=[t_dta], reads=[b_rate], writes=[b_dt])

                def fn_dec(a):
                    return a.activation(dec[:], dtc[:], AF.Exp, scale=-1.0)
                t_dec = S.emit("act", fn_dec, deps=[t_dtb], reads=[b_dt], writes=[b_dec])

                def fn_ef(v):
                    v.tensor_scalar(ef[:], dec[:], -1.0, 1.0, ALU.mult, ALU.add)
                    return v.tensor_tensor(ef[:], ef[:], P[2][:, :], ALU.mult)
                t_ef = S.emit("dve", fn_ef, deps=[t_dec, t_bt], reads=[b_dec, PB[2]],
                              writes=[b_ef])

                def fn_scan(v, s=s):
                    return v.tensor_tensor_scan(lscr[:], dec[:], ef[:], carry[:, 0:1],
                                                ALU.mult, ALU.add)
                t_sc = S.emit("dve", fn_scan, deps=[t_ef, t_c0], reads=[b_dec, b_ef, b_carry],
                              writes=[b_l1])

                def fn_cup(g, s=s):
                    return g.tensor_tensor(carry[:], lscr[:, M - 1:M], cmask_sb[:, s:s + 1],
                                           ALU.mult)
                t_cup = S.emit("pool", fn_cup, deps=[t_sc], reads=[b_l1, b_consts],
                               writes=[b_carry])

                if s >= 8:
                    def fn_lst(v, s=s):
                        return v.tensor_copy(l_b[:, (s - 8) * M:(s - 7) * M], lscr[:])
                    S.emit("dve", fn_lst, deps=[t_sc], reads=[b_l1], writes=[b_l])

                if debug and s == 15:
                    bt_d = sb1("bt_d", [NST, M], F32)
                    b_btd = B()
                    def fn_btd(v):
                        return v.tensor_copy(bt_d[:], P[2][:, :])
                    t_btd = S.emit("dve", fn_btd, deps=[t_bt], reads=[PB[2]], writes=[b_btd])
                    def fn_d1(e, sem):
                        e.dma_start(out=dbg["xn1"], in_=xnfm[:]).then_inc(sem, 16)
                        e.dma_start(out=dbg["u"], in_=ufm[:]).then_inc(sem, 16)
                        e.dma_start(out=dbg["dec"], in_=dec[:]).then_inc(sem, 16)
                        e.dma_start(out=dbg["ef"], in_=ef[:]).then_inc(sem, 16)
                        e.dma_start(out=dbg["dt"], in_=dtc[:]).then_inc(sem, 16)
                        e.dma_start(out=dbg["bt"], in_=bt_d[:]).then_inc(sem, 16)
                        e.dma_start(out=dbg["rate"], in_=rate_sb[:]).then_inc(sem, 16)
                    S.emit("sync", fn_d1, dma_n=7,
                           reads=[b_xn1, b_u, b_dec, b_ef, b_dt, b_btd, b_rate])

            if debug:
                def fn_dl(e, sem):
                    e.dma_start(out=dbg["l"], in_=l_b[:]).then_inc(sem, 16)
                S.emit("sync", fn_dl, dma_n=1, reads=[b_l])
                def fn_dm(e, sem):
                    e.dma_start(out=dbg["mcols"], in_=mcols[:]).then_inc(sem, 16)
                    e.dma_start(out=dbg["icols"], in_=icols[:]).then_inc(sem, 16)
                S.emit("sync", fn_dm, dma_n=2, reads=[b_mcols, b_icols])

        bar1 = S.barrier_tokens()

        def fn_tab_silu(a):
            return a.add_instruction(mybir.InstLoadActFuncSet(
                name=nc.get_next_instruction_name(), act_func_set_id=_siluset,
                ins=[], outs=[]))
        t_tab2 = S.emit("act", fn_tab_silu, deps=bar1)

        with contextlib.ExitStack() as p2ctx:
            sb2 = lambda name, shape, dt: p2ctx.enter_context(nc.sbuf_tensor(name, shape, dt))
            xtm_t = [sb2(f"xtm{i}", [128, D], F32) for i in range(2)]
            xntm = sb2("xntm", [128, 4, D], F32)
            xnb_t = sb2("xnb_t", [128, D], BF16)
            xq_tm = sb2("xq_tm", [128, D], BF16)
            xnfm2 = sb2("xnfm2", [128, KD, M], BF16)
            xqfm = sb2("xqfm", [128, KD, M], BF16)
            sgtm = sb2("sgtm", [128, D], BF16)
            stm = sb2("stm", [128, 4, D], BF16)
            ztmp = sb2("ztmp", [128, D], F32)
            xnd = sb2("xnd", [128, D], F32)
            zb_t = xnb_t
            zfm = sb2("zfm", [128, KD, 128], BF16)
            h_all = [sb2(f"h_all{i}", [128, 4 * 512], F32) for i in range(2)]
            sG = [sb2(f"sG{i}", [128, 512], F32) for i in range(2)]
            hq_tm = sb2("hq_tm", [128, 4 * 512], BF16)
            hq_fm = sb2("hq_fm", [128, KI, M], BF16)
            wqg6 = sb2("wqg6", [128, KD, 512], BF16)
            wqu6 = sb2("wqu6", [128, KD, 512], BF16)
            wqd_sb = [sb2(f"wqd{i}", [128, 512], BF16) for i in range(2)]
            gsc = sG[0]
            xre = sb2("xre", [128, 512], F32)
            outb = sb2("outb", [128, 512], F32)
            st12 = sb2("st12", [128, 4, 12], F32)
            st12c = sb2("st12c", [128, 4, 12], F32)
            mvagg = sb2("mvagg", [128, 4, 2], F32)
            mvc = sb2("mvc", [128, 4, 2], F32)
            vpad = sb2("vpad", [128, 64], F32)
            ipad = sb2("ipad", [128, 64], F32)
            ipc = sb2("ipc", [128, 64], F32)
            b_st12 = B(); b_st12c = B(); b_mv = B(); b_mvc = B()
            b_vpad = B(); b_ipad = B(); b_ipc = B()
            fdbg = xre

            b_xtm = [B(), B()]
            b_xntm = B(); b_xnb = B(); b_xqtm = B(); b_xnfm2 = B(); b_xqfm = B()
            b_z = b_xnb
            b_sg = B(); b_stm = B(); b_zfm = B(); b_hall = [B(), B()]; b_sG = [B(), B()]
            b_hqtm = B(); b_hqfm = B(); b_wg6 = B(); b_wu6 = B()
            b_wqdb = [B(), B()]
            b_gsc = b_sG[0]; b_xre = B(); b_outb = B(); b_ztmp = B(); b_xnd = B()
            b_fdbg = b_xre
            b_out = B()

            SIX = [(0, 512), (512, 512), (1024, 512), (1536, 512), (2048, 512), (2560, 256)]

            t_out_last = None
            for S2 in range(st2_limit):
                T0 = S2 * M
                if serialize:
                    bar1 = S.barrier_tokens()
                t_sub = []
                for j in range(4):
                    xb = xtm_t[j % 2]; bxb = b_xtm[j % 2]

                    def fn_lx(e, sem, j=j, xb=xb, T0=T0):
                        e.dma_start(out=xb[:], in_=x_tm[T0 + j * 128:T0 + (j + 1) * 128, :]
                                    ).then_inc(sem, 16)
                    t_lx = S.emit("sync", fn_lx, dma_n=1, deps=bar1, writes=[bxb])

                    def fn_bn(v, j=j, xb=xb):
                        v.bn_stats(st12[:, j, 0:6], xb[:, 0:512])
                        return v.bn_stats(st12[:, j, 6:12], xb[:, 512:1024])
                    t_bn = S.emit("dve", fn_bn, deps=[t_lx] + bar1, reads=[bxb],
                                  writes=[b_st12])

                    def fn_stc(a, j=j):
                        return a.activation(st12c[:, j, :], st12[:, j, :], AF.Copy)
                    t_stc = S.emit("act", fn_stc, deps=[t_bn], reads=[b_st12],
                                   writes=[b_st12c])

                    def fn_ag(v, j=j):
                        return v.bn_aggr(mvagg[:, j, :], st12c[:, j, :])
                    t_ag = S.emit("dve", fn_ag, deps=[t_stc], reads=[b_st12c],
                                  writes=[b_mv])

                    def fn_mvc(a, j=j):
                        return a.activation(mvc[:, j, :], mvagg[:, j, :], AF.Copy)
                    t_mvc = S.emit("act", fn_mvc, deps=[t_ag], reads=[b_mv], writes=[b_mvc])

                    def fn_sub(v, j=j, xb=xb):
                        return v.tensor_scalar(xntm[:, j, :], xb[:], mvc[:, j, 0:1], None,
                                               ALU.subtract)
                    t_sub.append(S.emit("dve", fn_sub, deps=[t_mvc, t_lx],
                                        reads=[bxb, b_mvc], writes=[b_xntm]))

                def fn_gather(v):
                    return v.tensor_scalar(vpad[:, 0:4],
                                           mvc[:].rearrange("p a b -> p (a b)")[:, 1:8:2],
                                           1e-5, None, ALU.add)
                t_ga = S.emit("dve", fn_gather, reads=[b_mvc], writes=[b_vpad])

                def fn_sqrtw(a):
                    a.add_instruction(mybir.InstLoadActFuncSet(
                        name=nc.get_next_instruction_name(), act_func_set_id=_sqrtset,
                        ins=[], outs=[]))
                    a.activation(vpad[:], vpad[:], AF.Sqrt)
                    return a.add_instruction(mybir.InstLoadActFuncSet(
                        name=nc.get_next_instruction_name(), act_func_set_id=_siluset,
                        ins=[], outs=[]))
                t_sq2 = S.emit("act", fn_sqrtw, deps=[t_ga], reads=[b_vpad], writes=[b_vpad])

                def fn_rcp(v):
                    return v.reciprocal(ipad[:], vpad[:])
                t_rcp = S.emit("dve", fn_rcp, deps=[t_sq2], reads=[b_vpad], writes=[b_ipad])

                def fn_ipc(a):
                    return a.activation(ipc[:], ipad[:], AF.Copy)
                t_ipc = S.emit("act", fn_ipc, deps=[t_rcp], reads=[b_ipad], writes=[b_ipc])

                t_xnA = []
                for j in range(4):
                    def fn_xn(v, j=j):
                        return v.tensor_scalar(xntm[:, j, :], xntm[:, j, :], ipc[:, j:j + 1],
                                               None, ALU.mult)
                    t_xn = S.emit("dve", fn_xn, deps=[t_ipc, t_sub[j]] + bar1,
                                  reads=[b_ipc], writes=[b_xntm])
                    t_xnA.append(t_xn)

                    def fn_cast(v, j=j):
                        return v.tensor_copy(xnb_t[:], xntm[:, j, :])
                    t_cb = S.emit("dve", fn_cast, deps=[t_xn], reads=[b_xntm], writes=[b_xnb])

                    def fn_trx(t, j=j):
                        for c in range(KD):
                            pp = P[6] if c < 4 else P[7]
                            last = t.matmul(pp[:, (c % 4) * 128:(c % 4 + 1) * 128],
                                            xnb_t[:, c * 128:(c + 1) * 128], ident_sb[:],
                                            start=True, stop=True)
                        return last
                    t_trx = S.emit("pe", fn_trx, deps=[t_cb], reads=[b_xnb, b_consts],
                                   writes=[PB[6], PB[7]])

                    def fn_evx(a, j=j):
                        a.activation(xnfm2[:, 0:4, j * 128:(j + 1) * 128],
                                     P[6][:, :].rearrange("p (c t) -> p c t", t=128), AF.Copy)
                        return a.activation(xnfm2[:, 4:8, j * 128:(j + 1) * 128],
                                            P[7][:, :].rearrange("p (c t) -> p c t", t=128),
                                            AF.Copy)
                    S.emit("act", fn_evx, deps=[t_trx, t_tab2], reads=[PB[6], PB[7]],
                           writes=[b_xnfm2])

                for j in range(4):
                    tq = _emit_quant(S, xntm[:, j, :], 1024, 64,
                                     xq_tm[:], q,
                                     deps=[t_xnA[j]],
                                     reads=[b_xntm], writes=[b_xqtm])

                    def fn_trq(t, j=j):
                        for c in range(KD):
                            pp = P[6] if c < 4 else P[7]
                            last = t.matmul(pp[:, (c % 4) * 128:(c % 4 + 1) * 128],
                                            xq_tm[:, c * 128:(c + 1) * 128],
                                            ident_sb[:], start=True, stop=True)
                        return last
                    t_trq = S.emit("pe", fn_trq, deps=[tq], reads=[b_xqtm, b_consts],
                                   writes=[PB[6], PB[7]])

                    def fn_evq(a, j=j):
                        a.activation(xqfm[:, 0:4, j * 128:(j + 1) * 128],
                                     P[6][:, :].rearrange("p (c t) -> p c t", t=128), AF.Copy)
                        return a.activation(xqfm[:, 4:8, j * 128:(j + 1) * 128],
                                            P[7][:, :].rearrange("p (c t) -> p c t", t=128),
                                            AF.Copy)
                    S.emit("act", fn_evq, deps=[t_trq], reads=[PB[6], PB[7]], writes=[b_xqfm])

                for j in range(4):
                    def fn_g(t, j=j):
                        for k in range(KD):
                            t.matmul(P[0][:, :], xnfm2[:, k, j * 128:(j + 1) * 128],
                                     wing_sb[:, k, 0:512], start=(k == 0), stop=(k == KD - 1))
                        for k in range(KD):
                            last = t.matmul(P[1][:, :], xnfm2[:, k, j * 128:(j + 1) * 128],
                                            wing_sb[:, k, 512:1024], start=(k == 0),
                                            stop=(k == KD - 1))
                        return last
                    t_g = S.emit("pe", fn_g, deps=[t_w2], reads=[b_xnfm2, b_wing],
                                 writes=[PB[0], PB[1]])

                    def fn_sg(a, j=j):
                        a.activation(sgtm[:, 0:512], P[0][:, :], AF.Silu)
                        return a.activation(sgtm[:, 512:1024], P[1][:, :], AF.Silu)
                    t_sg = S.emit("act", fn_sg, deps=[t_g, t_tab2], reads=[PB[0], PB[1]],
                                  writes=[b_sg])

                    def fn_y(t, j=j, T0=T0):
                        t.matmul(P[2][:, :], l_b[:, T0 + j * 128:T0 + (j + 1) * 128],
                                 wct_sb[:, 0:512], start=True, stop=True)
                        return t.matmul(P[3][:, :], l_b[:, T0 + j * 128:T0 + (j + 1) * 128],
                                        wct_sb[:, 512:1024], start=True, stop=True)
                    t_y = S.emit("pe", fn_y, deps=[t_w2] + bar1, reads=[b_l, b_wct],
                                 writes=[PB[2], PB[3]])

                    def fn_xnd(g, j=j):
                        return g.tensor_tensor(xnd[:], xntm[:, j, :], dbc[:], ALU.mult)
                    t_xd = S.emit("pool", fn_xnd, deps=[t_dbc] + bar1, reads=[b_xntm, b_dbc],
                                  writes=[b_xnd])

                    def fn_z1(v, j=j):
                        v.tensor_tensor(ztmp[:, 0:512], sgtm[:, 0:512], P[2][:, :], ALU.mult)
                        v.tensor_tensor(ztmp[:, 512:1024], sgtm[:, 512:1024], P[3][:, :],
                                        ALU.mult)
                        return v.tensor_tensor(zb_t[:], ztmp[:], xnd[:], ALU.add)
                    t_z = S.emit("dve", fn_z1, deps=[t_sg, t_y, t_xd],
                                 reads=[b_sg, PB[2], PB[3], b_xnd], writes=[b_z, b_ztmp])

                    def fn_trz(t, j=j):
                        for c in range(KD):
                            pp = P[6] if c < 4 else P[7]
                            last = t.matmul(pp[:, (c % 4) * 128:(c % 4 + 1) * 128],
                                            zb_t[:, c * 128:(c + 1) * 128], ident_sb[:],
                                            start=True, stop=True)
                        return last
                    t_trz = S.emit("pe", fn_trz, deps=[t_z], reads=[b_z, b_consts],
                                   writes=[PB[6], PB[7]])

                    def fn_evz(a, j=j):
                        a.activation(zfm[:, 0:4, :],
                                     P[6][:, :].rearrange("p (c t) -> p c t", t=128), AF.Copy)
                        return a.activation(zfm[:, 4:8, :],
                                            P[7][:, :].rearrange("p (c t) -> p c t", t=128),
                                            AF.Copy)
                    t_evz = S.emit("act", fn_evz, deps=[t_trz], reads=[PB[6], PB[7]],
                                   writes=[b_zfm])

                    def fn_s(t, j=j):
                        for k in range(KD):
                            t.matmul(P[4][:, :], zfm[:, k, :], wout_sb[:, k, 0:512],
                                     start=(k == 0), stop=(k == KD - 1))
                        for k in range(KD):
                            last = t.matmul(P[5][:, :], zfm[:, k, :], wout_sb[:, k, 512:1024],
                                            start=(k == 0), stop=(k == KD - 1))
                        return last
                    t_s = S.emit("pe", fn_s, deps=[t_evz, t_w2], reads=[b_zfm, b_wout],
                                 writes=[PB[4], PB[5]])

                    def fn_evs(a, j=j):
                        a.activation(stm[:, j, 0:512], P[4][:, :], AF.Copy)
                        return a.activation(stm[:, j, 512:1024], P[5][:, :], AF.Copy)
                    S.emit("act", fn_evs, deps=[t_s], reads=[PB[4], PB[5]], writes=[b_stm])

                if debug:
                    if S2 == 0:
                        def fn_dxn2(e, sem):
                            e.dma_start(out=dbg["xn2"], in_=xntm[:]).then_inc(sem, 16)
                        S.emit("sync", fn_dxn2, dma_n=1, reads=[b_xntm])
                    def fn_ds(e, sem, T0=T0):
                        e.dma_start(out=dbg["s"][T0:T0 + M, :].rearrange("(a p) b -> p a b", p=128),
                                    in_=stm[:]).then_inc(sem, 16)
                    S.emit("sync", fn_ds, dma_n=1, reads=[b_stm])


                for si, (e0, ew) in enumerate(SIX):
                    hb = h_all[si % 2]; bhb = b_hall[si % 2]
                    sGb = sG[si % 2]; bsGb = b_sG[si % 2]
                    def fn_lw6(e, sem, e0=e0, ew=ew):
                        e.dma_start(out=wqg6[:, :, 0:ew],
                                    in_=wqgT.rearrange("(k p) e -> p k e", p=128)[:, :, e0:e0 + ew]
                                    ).then_inc(sem, 16)
                        e.dma_start(out=wqu6[:, :, 0:ew],
                                    in_=wquT.rearrange("(k p) e -> p k e", p=128)[:, :, e0:e0 + ew]
                                    ).then_inc(sem, 16)
                    t_lw = S.emit("sync", fn_lw6, dma_n=2,
                                  deps=[b_wqg.wtok, b_wqu.wtok] + bar1,
                                  writes=[b_wg6, b_wu6])

                    t_hj = []
                    for j in range(4):
                        pg, pu = (P[0], P[1]) if j % 2 == 0 else (P[2], P[3])
                        bg, bu = (PB[0], PB[1]) if j % 2 == 0 else (PB[2], PB[3])

                        def fn_gu(t, j=j, ew=ew, pg=pg, pu=pu):
                            for k in range(KD):
                                t.matmul(pg[:, 0:ew], xqfm[:, k, j * 128:(j + 1) * 128],
                                         wqg6[:, k, 0:ew], start=(k == 0), stop=(k == KD - 1))
                            for k in range(KD):
                                last = t.matmul(pu[:, 0:ew], xqfm[:, k, j * 128:(j + 1) * 128],
                                                wqu6[:, k, 0:ew], start=(k == 0),
                                                stop=(k == KD - 1))
                            return last
                        t_gu = S.emit("pe", fn_gu, deps=[t_lw], reads=[b_xqfm, b_wg6, b_wu6],
                                      writes=[bg, bu])

                        def fn_sG(a, j=j, ew=ew, pg=pg, sGb=sGb):
                            return a.activation(sGb[:, 0:ew], pg[:, 0:ew], AF.Silu)
                        t_sG = S.emit("act", fn_sG, deps=[t_gu, t_tab2], reads=[bg],
                                      writes=[bsGb])

                        def fn_h(v, j=j, ew=ew, pu=pu, hb=hb, sGb=sGb):
                            return v.tensor_tensor(hb[:, j * ew:(j + 1) * ew], sGb[:, 0:ew],
                                                   pu[:, 0:ew], ALU.mult)
                        t_h = S.emit("dve", fn_h, deps=[t_sG], reads=[bsGb, bu],
                                     writes=[bhb])
                        t_hj.append(t_h)

                    tqh = None
                    for off in range(0, 4 * ew, 1024):
                        w6 = min(1024, 4 * ew - off)
                        tqh = _emit_quant(S, hb[:, off:off + w6], w6, w6 // 16,
                                          hq_tm[:, off:off + w6], q, deps=t_hj,
                                          reads=[bhb], writes=[b_hqtm])

                    nct = ew // 128
                    for j in range(4):
                        def fn_trh(t, j=j, ew=ew, nct=nct):
                            for c in range(nct):
                                last = t.matmul(P[6][:, c * 128:(c + 1) * 128],
                                                hq_tm[:, j * ew + c * 128:j * ew + (c + 1) * 128],
                                                ident_sb[:], start=True, stop=True)
                            return last
                        t_trh = S.emit("pe", fn_trh, deps=[tqh], reads=[b_hqtm, b_consts],
                                       writes=[PB[6]])

                        def fn_evh(a, j=j, e0=e0, nct=nct):
                            return a.activation(
                                hq_fm[:, e0 // 128:e0 // 128 + nct, j * 128:(j + 1) * 128],
                                P[6][:, 0:nct * 128].rearrange("p (c t) -> p c t", t=128),
                                AF.Copy)
                        S.emit("act", fn_evh, deps=[t_trh], reads=[PB[6]], writes=[b_hqfm])

                for dh in range(2):
                    PF = [P[4 * dh + 0], P[4 * dh + 1], P[4 * dh + 2], P[4 * dh + 3]]
                    PBF = [PB[4 * dh + 0], PB[4 * dh + 1], PB[4 * dh + 2], PB[4 * dh + 3]]
                    t_fp = [None] * 4
                    for k in range(KI):
                        wb = wqd_sb[k % 2]; bwb = b_wqdb[k % 2]

                        def fn_lwd(e, sem, k=k, dh=dh, wb=wb):
                            e.dma_start(out=wb[:],
                                        in_=wqdT[k * 128:(k + 1) * 128, dh * 512:(dh + 1) * 512]
                                        ).then_inc(sem, 16)
                        t_lwd = S.emit("sync", fn_lwd, dma_n=1, deps=[b_wqd.wtok] + bar1,
                                       writes=[bwb])

                        for j in range(4):
                            def fn_f(t, k=k, j=j, wb=wb, PF=PF):
                                return t.matmul(PF[j][:, :], hq_fm[:, k, j * 128:(j + 1) * 128],
                                                wb[:], start=(k == 0), stop=(k == KI - 1))
                            t_fp[j] = S.emit("pe", fn_f, deps=[t_lwd], reads=[b_hqfm, bwb],
                                             writes=[PBF[j]])

                    for j in range(4):
                        tap = []
                        if debug and S2 == st2_limit - 1 and dh == 1 and j == 3:
                            def fn_fd(v, j=j, PF=PF):
                                return v.tensor_copy(fdbg[:], PF[j][:, :])
                            t_fd0 = S.emit("dve", fn_fd, deps=[t_fp[j]], reads=[PBF[j]],
                                           writes=[b_fdbg])
                            tap.append(t_fd0)
                            def fn_fdd(e, sem):
                                e.dma_start(out=dbg["fps"], in_=fdbg[:]).then_inc(sem, 16)
                            S.emit("sync", fn_fdd, dma_n=1, deps=[t_fd0], reads=[b_fdbg])
                        def fn_gsc(a, j=j, PF=PF):
                            return a.activation(gsc[:], PF[j][:, :], AF.Silu)
                        t_gs = S.emit("act", fn_gsc, deps=[t_fp[j], t_tab2] + tap,
                                      reads=[PBF[j]], writes=[b_gsc])

                        def fn_lx2(e, sem, j=j, dh=dh, T0=T0):
                            e.dma_start(out=xre[:],
                                        in_=x_tm[T0 + j * 128:T0 + (j + 1) * 128,
                                                 dh * 512:(dh + 1) * 512]).then_inc(sem, 16)
                        t_x2l = S.emit("sync", fn_lx2, dma_n=1, deps=bar1, writes=[b_xre])

                        def fn_comb(v, j=j, dh=dh):
                            v.tensor_tensor(outb[:], gsc[:], stm[:, j, dh * 512:(dh + 1) * 512],
                                            ALU.mult)
                            return v.tensor_tensor(outb[:], outb[:], xre[:], ALU.add)
                        t_cb2 = S.emit("dve", fn_comb, deps=[t_gs, t_x2l],
                                       reads=[b_gsc, b_stm, b_xre], writes=[b_outb])

                        def fn_so(e, sem, j=j, dh=dh, T0=T0):
                            e.dma_start(out=out[T0 + j * 128:T0 + (j + 1) * 128,
                                                dh * 512:(dh + 1) * 512],
                                        in_=outb[:]).then_inc(sem, 16)
                        t_out_last = S.emit("sync", fn_so, dma_n=1, reads=[b_outb],
                                            writes=[b_out])

            if debug:
                bar_end = S.barrier_tokens()
                f2w = wqd_sb[0]
                t_prev = None
                for k in range(KI):
                    def fn_l2(e, sem, k=k):
                        e.dma_start(out=f2w[:],
                                    in_=wqdT[k * 128:(k + 1) * 128, 512:1024]).then_inc(sem, 16)
                    t_l2 = S.emit("sync", fn_l2, dma_n=1, deps=bar_end, writes=[b_wqdb[0]])
                    def fn_m2(t, k=k):
                        return t.matmul(P[5][:, :], hq_fm[:, k, 3 * 128:4 * 128], f2w[:],
                                        start=(k == 0), stop=(k == KI - 1))
                    t_prev = S.emit("pe", fn_m2, deps=[t_l2] + bar_end,
                                    reads=[b_hqfm, b_wqdb[0]], writes=[PB[5]])
                def fn_e2(v):
                    return v.tensor_copy(fdbg[:], P[5][:, :])
                t_e2 = S.emit("dve", fn_e2, deps=[t_prev], reads=[PB[5]], writes=[b_fdbg])
                def fn_d2(e, sem):
                    e.dma_start(out=dbg["f2"], in_=fdbg[:]).then_inc(sem, 16)
                    e.dma_start(out=dbg["wqd"], in_=wqdT).then_inc(sem, 16)
                S.emit("sync", fn_d2, dma_n=2, deps=[t_e2], reads=[b_fdbg, b_wqd])

                def fn_dfin(e, sem):
                    e.dma_start(out=dbg["hqfm"], in_=hq_fm[:]).then_inc(sem, 16)
                    e.dma_start(out=dbg["xqfm"], in_=xqfm[:]).then_inc(sem, 16)
                    e.dma_start(out=dbg["sgtm"], in_=sgtm[:]).then_inc(sem, 16)
                    e.dma_start(out=dbg["gsc"], in_=gsc[:]).then_inc(sem, 16)
                    e.dma_start(out=dbg["mvc"], in_=mvc[:].rearrange("p a b -> p (a b)")).then_inc(sem, 16)
                    e.dma_start(out=dbg["ipc"], in_=ipc[:]).then_inc(sem, 16)
                    e.dma_start(out=dbg["xnf"], in_=xntm[:]).then_inc(sem, 16)
                    e.dma_start(out=dbg["outb"], in_=outb[:]).then_inc(sem, 16)
                S.emit("sync", fn_dfin, dma_n=8,
                       reads=[b_hqfm, b_xqfm, b_sg, b_gsc, b_outb, b_mvc, b_ipc, b_xntm],
                       deps=S.barrier_tokens())

            def fn_fin(v):
                return v.engine_nop()
            S.emit("dve", fn_fin, deps=[t_out_last] + S.barrier_tokens())

        with nc.Block() as block:
            @block.tensor
            def _(eng):
                for st in S.steps["pe"]:
                    st(eng)

            @block.vector
            def _(eng):
                for st in S.steps["dve"]:
                    st(eng)

            @block.scalar
            def _(eng):
                for st in S.steps["act"]:
                    st(eng)

            @block.gpsimd
            def _(eng):
                for st in S.steps["pool"]:
                    st(eng)

            @block.sync
            def _(eng):
                for st in S.steps["sync"]:
                    st(eng)

    return nc


def _in_maps(inp):
    x = np.asarray(inp["L0_attn_x"], dtype=np.float32)
    W_in = np.asarray(inp["L0_attn_W_in"], dtype=np.float32)
    W_state = np.asarray(inp["L0_attn_W_state"], dtype=np.float32)
    W_dt = np.asarray(inp["L0_attn_W_dt"], dtype=np.float32)
    b_dt = np.asarray(inp["L0_attn_b_dt"], dtype=np.float32)
    A_log = np.asarray(inp["L0_attn_A_log"], dtype=np.float32)
    W_C = np.asarray(inp["L0_attn_W_C"], dtype=np.float32)
    Dv = np.asarray(inp["L0_attn_D"], dtype=np.float32)
    W_out = np.asarray(inp["L0_attn_W_out"], dtype=np.float32)
    w_gate = np.asarray(inp["L0_ffn_w_gate"], dtype=np.float32)
    w_up = np.asarray(inp["L0_ffn_w_up"], dtype=np.float32)
    w_down = np.asarray(inp["L0_ffn_w_down"], dtype=np.float32)

    import ml_dtypes
    WiuT = np.ascontiguousarray(W_in.T[:, :D])
    WigT = np.ascontiguousarray(W_in.T[:, D:])
    WstT = np.ascontiguousarray(W_state.T)
    WdtT = np.ascontiguousarray(W_dt.T)
    WcT = np.ascontiguousarray(W_C.T)
    WoT = np.ascontiguousarray(W_out.T)
    ident = np.eye(128, dtype=ml_dtypes.bfloat16)
    bdtc = np.ascontiguousarray(b_dt.reshape(NST, 1))
    alogc = np.ascontiguousarray(A_log.reshape(NST, 1))
    dvc = np.ascontiguousarray(Dv.reshape(1, D))

    maps = []
    for core in range(8):
        b, half = core // 2, core % 2
        xb = x[b]
        if half == 0:
            xroll = np.concatenate([xb[TOK:], xb[:TOK]], axis=0)
        else:
            xroll = xb
        cm = np.ones((128, ST1), np.float32)
        if half == 0:
            cm[:, 7] = 0.0
        maps.append({
            "xT": np.ascontiguousarray(xroll.T),
            "x_tm": np.ascontiguousarray(xb[half * TOK:(half + 1) * TOK]),
            "WiuT": WiuT, "WigT": WigT, "WstT": WstT, "WdtT": WdtT,
            "WcT": WcT, "WoT": WoT,
            "wg": w_gate, "wu": w_up, "wd": w_down,
            "bdt": bdtc, "alog": alogc, "dvec": dvc,
            "identI": ident, "cmask": cm,
        })
    return maps


def kernel(**inp):
    maps = _in_maps(inp)
    nc = build_nc(debug=False)
    res = run_bass_kernel_spmd(nc, maps, core_ids=list(range(8)))
    out = np.empty((4, TOKF, D), dtype=np.float32)
    for core in range(8):
        b, half = core // 2, core % 2
        out[b, half * TOK:(half + 1) * TOK] = res.results[core]["out"]
    return out


# version 05 (final)
# speedup vs baseline: 1.2764x; 1.0268x over previous
import contextlib
import numpy as np

import concourse.bass as bass
import concourse.mybir as mybir
from concourse.bass_utils import run_bass_kernel_spmd

F32 = mybir.dt.float32
BF16 = mybir.dt.bfloat16
AF = mybir.ActivationFunctionType
ALU = mybir.AluOpType
AX = mybir.AxisListType

MAGIC = float(1.5 * 2 ** 23)
T7 = 10.0 / 12.0
D = 1024
E2 = 2048
NST = 128
I3 = 2816
TOK = 4096
TOKF = 8192
ST2 = 8
ST1 = 16
M = 512
KD = D // 128
KI = I3 // 128

ENGS = ("pe", "dve", "act", "pool", "sync")
DMA_ENGS = ("sync", "act", "pool")


class Buf:

    __slots__ = ("wtok", "rtoks")

    def __init__(self):
        self.wtok = None
        self.rtoks = []


class Sched:

    NDMA = 8

    def __init__(self, nc, sems):
        self.nc = nc
        self.sems = sems
        self.steps = {e: [] for e in ENGS}
        self.gcnt = {e: 0 for e in ENGS}
        self.dgidx = {e: 0 for e in DMA_ENGS}
        self.dcnt = {}
        self.hi = {}

    def _wait(self, eng_name, deps):
        best = {}
        for tok in deps:
            if tok is None:
                continue
            semname, val = tok
            if val <= 0:
                continue
            if semname.startswith("g:") and semname[2:] == eng_name:
                continue
            if best.get(semname, 0) < val:
                best[semname] = val
        out = []
        for semname, val in best.items():
            key = (eng_name, semname)
            if self.hi.get(key, 0) >= val:
                continue
            self.hi[key] = val
            out.append((self.sems[semname], val))
        return out

    def _wait_nodedup(self, eng_name, deps, strict=False):
        best = {}
        for tok in deps:
            if tok is None:
                continue
            semname, val = tok
            if val <= 0:
                continue
            if not strict and semname.startswith("g:") and semname[2:] == eng_name:
                continue
            if best.get(semname, 0) < val:
                best[semname] = val
        return [(self.sems[s], v) for s, v in best.items()]

    def emit(self, eng_name, fn, deps=(), reads=(), writes=(), dma_n=0, strict=False):
        deps = list(deps)
        for b in reads:
            deps.append(b.wtok)
        for b in writes:
            deps.append(b.wtok)
            deps.extend(b.rtoks)
        if dma_n:
            rot = self.dgidx[eng_name] % Sched.NDMA
            self.dgidx[eng_name] += 1
            semname = f"d:{eng_name}:{rot}"
            prev = self.dcnt.get((eng_name, rot), 0)
            if prev > 0:
                deps.append((semname, prev))
            new_total = prev + 16 * dma_n
            self.dcnt[(eng_name, rot)] = new_total
            waits = self._wait(eng_name, deps)
            tok = (semname, new_total)
            sem = self.sems[semname]

            def step(eng, waits=waits, fn=fn, sem=sem):
                for s, v in waits:
                    eng.wait_ge(s, v)
                fn(eng, sem)

        else:
            waits = self._wait(eng_name, deps)
            semname = "g:" + eng_name
            self.gcnt[eng_name] += 1
            tok = (semname, self.gcnt[eng_name])
            sem = self.sems[semname]

            def step(eng, waits=waits, fn=fn, sem=sem):
                for s, v in waits:
                    eng.wait_ge(s, v)
                last = fn(eng)
                last.then_inc(sem, 1)

        self.steps[eng_name].append(step)
        for b in reads:
            b.rtoks.append(tok)
        for b in writes:
            b.wtok = tok
            b.rtoks = []
        return tok

    def barrier_tokens(self):
        toks = [("g:" + e, self.gcnt[e]) for e in ENGS]
        toks += [(f"d:{q}:{r}", v) for (q, r), v in self.dcnt.items()]
        return toks


def _emit_quant(S, xn_ap, nelem, nb, out_ap, q, dve="dve", pool="pool",
                 deps=(), reads=(), writes=()):
    x3 = xn_ap.rearrange("p (b w) -> p b w", w=16)
    par = q["_inv"][0] % 2
    q["_inv"][0] += 1
    bmax, rinv, sc12 = q["bmax"], q["rinv"], q["sc12"][par]
    tF, b1, b2, b3, b4 = q["tF"], q["b1"], q["b2"], q["b3"][par], q["b4"]
    b_sc12 = q["_sc12"][par]; b_xnq = q["_xnq"][par]; b_b3 = q["_b3"][par]
    xnq = q["xnq"][par]

    if nb >= 64:
        def red(v):
            v.reduce_max(bmax[:, :nb], x3, axis=AX.X, apply_absolute_value=True)
            v.tensor_scalar(rinv[:, :nb], bmax[:, :nb], 6e-12, None, ALU.max)
            v.reciprocal(bmax[:, :nb], rinv[:, :nb])
            return v.tensor_scalar(sc12[:, :nb], rinv[:, :nb], 1.0 / 12.0, None, ALU.mult)

        t_red = S.emit(dve, red, deps=deps, reads=list(reads),
                       writes=[q["_bmax"], q["_rinv"], b_sc12])
    else:
        def red_a(v):
            return v.reduce_max(bmax[:, :nb], x3, axis=AX.X, apply_absolute_value=True)
        t_ra = S.emit(dve, red_a, deps=deps, reads=list(reads), writes=[q["_bmax"]])

        def red_b(g):
            return g.tensor_scalar(rinv[:, :nb], bmax[:, :nb], 6e-12, None, ALU.max)
        t_rb = S.emit(pool, red_b, deps=[t_ra], reads=[q["_bmax"]], writes=[q["_rinv"]])

        def red_c(v):
            return v.reciprocal(bmax[:, :nb], rinv[:, :nb])
        t_rc = S.emit(dve, red_c, deps=[t_rb], reads=[q["_rinv"]], writes=[q["_bmax"]])

        def red_d(g):
            return g.tensor_scalar(sc12[:, :nb], rinv[:, :nb], 1.0 / 12.0, None, ALU.mult)
        t_red = S.emit(pool, red_d, deps=[t_rc], reads=[q["_rinv"]], writes=[b_sc12])

    def pmul(g):
        rb = bmax[:, :nb].unsqueeze(2).broadcast_to([128, nb, 16])
        return g.tensor_tensor(xnq[:, :nelem].rearrange("p (b w) -> p b w", w=16),
                               x3, rb, ALU.mult)

    t_xn = S.emit(pool, pmul, deps=[t_red], reads=list(reads) + [q["_bmax"]],
                  writes=[b_xnq])
    xq = xnq[:, :nelem]

    def ladder(v):
        v.tensor_scalar(tF[:, :nelem], xq, 12.0, MAGIC, ALU.mult, ALU.add)
        v.tensor_scalar(b1[:, :nelem], tF[:, :nelem], MAGIC, 4.0, ALU.subtract, ALU.min)
        v.tensor_scalar(tF[:, :nelem], xq, 6.0, MAGIC, ALU.mult, ALU.add)
        v.tensor_scalar(b2[:, :nelem], tF[:, :nelem], MAGIC, 4.0, ALU.subtract, ALU.min)
        v.tensor_scalar(b3[:, :nelem], b1[:, :nelem], -4.0, None, ALU.max)
        v.tensor_scalar(b1[:, :nelem], b2[:, :nelem], 2.0, None, ALU.max)
        v.tensor_scalar(b4[:, :nelem], b2[:, :nelem], -2.0, -4.0, ALU.min, ALU.max)
        v.tensor_tensor(b2[:, :nelem], b1[:, :nelem], b4[:, :nelem], ALU.add)
        v.scalar_tensor_tensor(b1[:, :nelem], b2[:, :nelem], 2.0, b3[:, :nelem],
                               ALU.mult, ALU.add)
        v.tensor_scalar(b3[:, :nelem], xq, T7, 4.0, ALU.is_gt, ALU.mult)
        v.tensor_scalar(b4[:, :nelem], xq, -T7, 4.0, ALU.is_lt, ALU.mult)
        v.tensor_tensor(b2[:, :nelem], b3[:, :nelem], b4[:, :nelem], ALU.subtract)
        return v.tensor_tensor(b3[:, :nelem], b1[:, :nelem], b2[:, :nelem], ALU.add)

    t_lad = S.emit(dve, ladder, deps=[t_xn], reads=[b_xnq],
                   writes=[q["_tF"], q["_b1"], q["_b2"], b_b3, q["_b4"]])

    def fmul(g):
        scb = sc12[:, :nb].unsqueeze(2).broadcast_to([128, nb, 16])
        return g.tensor_tensor(out_ap.rearrange("p (b w) -> p b w", w=16),
                               b3[:, :nelem].rearrange("p (b w) -> p b w", w=16),
                               scb, ALU.mult)

    return S.emit(pool, fmul, deps=[t_lad], reads=[b_b3, b_sc12],
                  writes=list(writes))


def build_nc(debug=False, st2_limit=ST2, serialize=False):
    nc = bass.Bass("TRN2", target_bir_lowering=False)

    xT = nc.dram_tensor("xT", [D, TOKF], F32, kind="ExternalInput").ap()
    x_tm = nc.dram_tensor("x_tm", [TOK, D], F32, kind="ExternalInput").ap()
    WiuT = nc.dram_tensor("WiuT", [D, D], F32, kind="ExternalInput").ap()
    WigT = nc.dram_tensor("WigT", [D, D], F32, kind="ExternalInput").ap()
    WstT = nc.dram_tensor("WstT", [D, NST], F32, kind="ExternalInput").ap()
    WdtT = nc.dram_tensor("WdtT", [D, NST], F32, kind="ExternalInput").ap()
    WcT = nc.dram_tensor("WcT", [NST, D], F32, kind="ExternalInput").ap()
    WoT = nc.dram_tensor("WoT", [D, D], F32, kind="ExternalInput").ap()
    wg = nc.dram_tensor("wg", [I3, D], F32, kind="ExternalInput").ap()
    wu = nc.dram_tensor("wu", [I3, D], F32, kind="ExternalInput").ap()
    wd = nc.dram_tensor("wd", [D, I3], F32, kind="ExternalInput").ap()
    bdt = nc.dram_tensor("bdt", [NST, 1], F32, kind="ExternalInput").ap()
    alog = nc.dram_tensor("alog", [NST, 1], F32, kind="ExternalInput").ap()
    dvec = nc.dram_tensor("dvec", [1, D], F32, kind="ExternalInput").ap()
    identI = nc.dram_tensor("identI", [128, 128], BF16, kind="ExternalInput").ap()
    cmask = nc.dram_tensor("cmask", [128, ST1], F32, kind="ExternalInput").ap()
    out = nc.dram_tensor("out", [TOK, D], F32, kind="ExternalOutput").ap()
    wqgT = nc.dram_tensor("wqgT", [D, I3], BF16).ap()
    wquT = nc.dram_tensor("wquT", [D, I3], BF16).ap()
    wqdT = nc.dram_tensor("wqdT", [I3, D], BF16).ap()

    dbg = {}
    if debug:
        dbg["l"] = nc.dram_tensor("dbg_l", [NST, TOK], BF16, kind="ExternalOutput").ap()
        dbg["mcols"] = nc.dram_tensor("dbg_mcols", [128, 32], F32, kind="ExternalOutput").ap()
        dbg["icols"] = nc.dram_tensor("dbg_icols", [128, 32], F32, kind="ExternalOutput").ap()
        dbg["wqg"] = nc.dram_tensor("dbg_wqg", [D, I3], BF16, kind="ExternalOutput").ap()
        dbg["s"] = nc.dram_tensor("dbg_s", [TOK, D], BF16, kind="ExternalOutput").ap()
        dbg["xn1"] = nc.dram_tensor("dbg_xn1", [128, KD, M], BF16, kind="ExternalOutput").ap()
        dbg["u"] = nc.dram_tensor("dbg_u", [128, KD, M], BF16, kind="ExternalOutput").ap()
        dbg["dec"] = nc.dram_tensor("dbg_dec", [NST, M], F32, kind="ExternalOutput").ap()
        dbg["ef"] = nc.dram_tensor("dbg_ef", [NST, M], F32, kind="ExternalOutput").ap()
        dbg["xn2"] = nc.dram_tensor("dbg_xn2", [128, 4, D], F32, kind="ExternalOutput").ap()
        dbg["dt"] = nc.dram_tensor("dbg_dt", [NST, M], F32, kind="ExternalOutput").ap()
        dbg["bt"] = nc.dram_tensor("dbg_bt", [NST, M], F32, kind="ExternalOutput").ap()
        dbg["rate"] = nc.dram_tensor("dbg_rate", [NST, 1], F32, kind="ExternalOutput").ap()
        dbg["hqfm"] = nc.dram_tensor("dbg_hqfm", [128, KI, M], BF16, kind="ExternalOutput").ap()
        dbg["xqfm"] = nc.dram_tensor("dbg_xqfm", [128, KD, M], BF16, kind="ExternalOutput").ap()
        dbg["sgtm"] = nc.dram_tensor("dbg_sgtm", [128, D], BF16, kind="ExternalOutput").ap()
        dbg["fps"] = nc.dram_tensor("dbg_fps", [128, 512], F32, kind="ExternalOutput").ap()
        dbg["f2"] = nc.dram_tensor("dbg_f2", [128, 512], F32, kind="ExternalOutput").ap()
        dbg["wqd"] = nc.dram_tensor("dbg_wqd", [I3, D], BF16, kind="ExternalOutput").ap()
        dbg["gsc"] = nc.dram_tensor("dbg_gsc", [128, 512], F32, kind="ExternalOutput").ap()
        dbg["outb"] = nc.dram_tensor("dbg_outb", [128, 512], F32, kind="ExternalOutput").ap()
        dbg["mvc"] = nc.dram_tensor("dbg_mvc", [128, 8], F32, kind="ExternalOutput").ap()
        dbg["ipc"] = nc.dram_tensor("dbg_ipc", [128, 64], F32, kind="ExternalOutput").ap()
        dbg["xnf"] = nc.dram_tensor("dbg_xnf", [128, 4, D], F32, kind="ExternalOutput").ap()
        dbg["xq"] = nc.dram_tensor("dbg_xq", [TOK, D], BF16, kind="ExternalOutput").ap()

    semnames = ["g:" + e for e in ENGS] + [
        f"d:{q}:{r}" for q in DMA_ENGS for r in range(Sched.NDMA)]

    with contextlib.ExitStack() as ctx:
        sems = {n: ctx.enter_context(nc.semaphore(n.replace(":", "_"))) for n in semnames}
        S = Sched(nc, sems)
        B = lambda: Buf()

        sb = lambda name, shape, dt: ctx.enter_context(nc.sbuf_tensor(name, shape, dt))
        ps = lambda name, shape: ctx.enter_context(nc.psum_tensor(name, shape, F32))

        l_b = sb("l_b", [NST, TOK], BF16)
        ident_sb = sb("ident_sb", [128, 128], BF16)
        ident1 = sb("ident1", [1, 1], F32)
        ones_col = sb("ones_col", [128, 1], BF16)
        ones_row = sb("ones_row", [1, 128], F32)
        bdt_sb = sb("bdt_sb", [NST, 1], F32)
        rate_sb = sb("rate_sb", [NST, 1], F32)
        dbc = sb("dbc", [128, D], F32)
        cmask_sb = sb("cmask_sb", [128, ST1], F32)
        mcols = sb("mcols", [128, 32], F32)
        icols = sb("icols", [128, 32], F32)
        wing_sb = sb("wing_sb", [128, KD, D], BF16)
        wct_sb = sb("wct_sb", [NST, D], BF16)
        wout_sb = sb("wout_sb", [128, KD, D], BF16)
        QF = 1024
        q = {"bmax": sb("q_bmax", [128, QF // 16], F32),
             "rinv": sb("q_rinv", [128, QF // 16], F32),
             "sc12": [sb(f"q_sc12{i}", [128, QF // 16], F32) for i in range(2)],
             "xnq": [sb(f"q_xnq{i}", [128, QF], F32) for i in range(2)],
             "tF": sb("q_tF", [128, QF], F32),
             "b1": sb("q_b1", [128, QF], BF16),
             "b2": sb("q_b2", [128, QF], BF16),
             "b3": [sb(f"q_b3{i}", [128, QF], BF16) for i in range(2)],
             "b4": sb("q_b4", [128, QF], BF16)}
        for k in ("bmax", "rinv", "tF", "b1", "b2", "b4"):
            q["_" + k] = B()
        for k in ("sc12", "xnq", "b3"):
            q["_" + k] = [B(), B()]
        q["_inv"] = [0]

        P = [ps(f"ps{i}", [128, 512]) for i in range(8)]
        PB = [B() for _ in range(8)]

        b_l = B(); b_mcols = B(); b_icols = B(); b_consts = B()
        b_wing = B(); b_wct = B(); b_wout = B()
        b_wqg = B(); b_wqu = B(); b_wqd = B()

        def fn_consts(e, sem):
            e.dma_start(out=ident_sb[:], in_=identI).then_inc(sem, 16)
            e.dma_start(out=bdt_sb[:], in_=bdt).then_inc(sem, 16)
            e.dma_start(out=rate_sb[:], in_=alog).then_inc(sem, 16)
            e.dma_start(out=cmask_sb[:], in_=cmask).then_inc(sem, 16)
            e.dma_start(out=dbc[0:1, :], in_=dvec).then_inc(sem, 16)
        t_c0 = S.emit("sync", fn_consts, dma_n=5, writes=[b_consts])

        def fn_memset(v):
            v.memset(ones_col[:], 1.0)
            v.memset(ones_row[:], 1.0)
            return v.memset(ident1[:], 1.0)
        t_ms = S.emit("dve", fn_memset, writes=[b_consts])

        def fn_table1(a):
            return a.preload_act_table(AF.Ln)
        from concourse.hw_specs import get_activation_tables
        _tabs = get_activation_tables(nc.m.arch)
        _explnset = None
        for _i, (_n, _fs) in enumerate(_tabs.items()):
            if AF.Exp in _fs and AF.Ln in _fs:
                _explnset = _i
                break
        _siluset = None
        for _i, (_n, _fs) in enumerate(_tabs.items()):
            if AF.Silu in _fs:
                _siluset = _i
                break
        _sqrtset = None
        for _i, (_n, _fs) in enumerate(_tabs.items()):
            if AF.Sqrt in _fs:
                _sqrtset = _i
                break

        def fn_tab_expln(a):
            return a.add_instruction(mybir.InstLoadActFuncSet(
                name=nc.get_next_instruction_name(), act_func_set_id=_explnset,
                ins=[], outs=[]))
        t_tab1 = S.emit("act", fn_tab_expln)

        rt1 = sb("rt1", [NST, 1], F32); rt2 = sb("rt2", [NST, 1], F32)
        b_rate = B()

        def fn_rate(a):
            a.activation(rt1[:], rate_sb[:], AF.Abs)
            a.activation(rt1[:], rt1[:], AF.Exp, scale=-1.0)
            a.activation(rt1[:], rt1[:], AF.Ln, bias=1.0)
            return a.activation(rt2[:], rate_sb[:], AF.Relu)
        t_r1 = S.emit("act", fn_rate, deps=[t_c0, t_tab1], reads=[b_consts], writes=[b_rate])

        def fn_rate2(v):
            return v.scalar_tensor_tensor(rate_sb[:], rt1[:], 1e-4, rt2[:],
                                          ALU.add, ALU.add)
        t_rate = S.emit("dve", fn_rate2, deps=[t_r1], writes=[b_rate])

        b_dbc = B()

        def fn_dbc_mm(t):
            t.matmul(P[6][:, 0:512], ones_row[:], dbc[0:1, 0:512], start=True, stop=True)
            return t.matmul(P[7][:, 0:512], ones_row[:], dbc[0:1, 512:1024], start=True, stop=True)
        t_dbm = S.emit("pe", fn_dbc_mm, deps=[t_c0, t_ms], reads=[b_consts],
                       writes=[PB[6], PB[7]])

        def fn_dbc_ev(a):
            a.activation(dbc[:, 0:512], P[6][:, 0:512], AF.Copy)
            return a.activation(dbc[:, 512:1024], P[7][:, 0:512], AF.Copy)
        t_dbc = S.emit("act", fn_dbc_ev, deps=[t_dbm, t_tab1], reads=[PB[6], PB[7]], writes=[b_dbc])

        def fn_w2(e, sem):
            e.dma_start(out=wing_sb[:], in_=WigT.rearrange("(k p) e -> p k e", p=128)).then_inc(sem, 16)
            e.dma_start(out=wct_sb[:], in_=WcT).then_inc(sem, 16)
            e.dma_start(out=wout_sb[:], in_=WoT.rearrange("(k p) e -> p k e", p=128)).then_inc(sem, 16)
        t_w2 = S.emit("pool", fn_w2, dma_n=3, writes=[b_wing, b_wct, b_wout])

        with contextlib.ExitStack() as prep_ctx:
            stg = prep_ctx.enter_context(nc.sbuf_tensor("stg", [128, KI * D], BF16))
            wrow = prep_ctx.enter_context(nc.sbuf_tensor("wrow", [128, I3], F32))
            wqrow = prep_ctx.enter_context(nc.sbuf_tensor("wqrow", [128, I3], BF16))
            b_stg = [B() for _ in range(KI)]
            b_wrow = B(); b_wqrow = B()

            def prep_weight(src, dst, rows, cols, b_dst):
                nrt = rows // 128
                nct = cols // 128
                for rt in range(nrt):
                    def fn_ld(e, sem, rt=rt):
                        e.dma_start(out=wrow[:, 0:cols],
                                    in_=src[rt * 128:(rt + 1) * 128, :]).then_inc(sem, 16)
                    t_ld = S.emit("sync", fn_ld, dma_n=1, writes=[b_wrow])
                    toks = []
                    for off in range(0, cols, QF):
                        w = min(QF, cols - off)
                        tq = _emit_quant(S, wrow[:, off:off + w], w, w // 16,
                                         wqrow[:, off:off + w], q,
                                         deps=[t_ld], reads=[b_wrow], writes=[b_wqrow])
                        toks.append(tq)
                    for cc in range(nct):
                        pslot = 6 + (cc % 2)
                        def fn_tr(t, cc=cc, pslot=pslot):
                            return t.matmul(P[pslot][:, 0:128],
                                            wqrow[:, cc * 128:(cc + 1) * 128],
                                            ident_sb[:], start=True, stop=True)
                        t_tr = S.emit("pe", fn_tr, deps=toks + [t_c0],
                                      reads=[b_wqrow, b_consts], writes=[PB[pslot]])
                        def fn_ev(a, cc=cc, rt=rt, pslot=pslot, rows=rows):
                            return a.activation(stg[:, cc * rows + rt * 128:cc * rows + (rt + 1) * 128],
                                                P[pslot][:, 0:128], AF.Copy)
                        S.emit("act", fn_ev, deps=[t_tr], reads=[PB[pslot]], writes=[b_stg[cc]])
                for cc in range(nct):
                    def fn_st(e, sem, cc=cc, rows=rows):
                        e.dma_start(out=dst[cc * 128:(cc + 1) * 128, :],
                                    in_=stg[:, cc * rows:(cc + 1) * rows]).then_inc(sem, 16)
                    S.emit("sync", fn_st, dma_n=1, reads=[b_stg[cc]], writes=[b_dst])

            prep_weight(wg, wqgT, I3, D, b_wqg)
            prep_weight(wu, wquT, I3, D, b_wqu)
            prep_weight(wd, wqdT, D, I3, b_wqd)

            if debug:
                def fn_dwq(e, sem):
                    e.dma_start(out=dbg["wqg"], in_=wqgT).then_inc(sem, 16)
                S.emit("sync", fn_dwq, dma_n=1, reads=[b_wqg])

        bar0 = S.barrier_tokens()
        p1 = {}
        with contextlib.ExitStack() as p1ctx:
            sb1 = lambda name, shape, dt: p1ctx.enter_context(nc.sbuf_tensor(name, shape, dt))
            wiu_sb = sb1("wiu_sb", [128, KD, D], BF16)
            wst_sb = sb1("wst_sb", [128, KD, NST], BF16)
            wdt_sb = sb1("wdt_sb", [128, KD, NST], BF16)
            b_wiu = B(); b_wst = B(); b_wdt = B()

            def fn_w1(e, sem):
                e.dma_start(out=wiu_sb[:], in_=WiuT.rearrange("(k p) e -> p k e", p=128)).then_inc(sem, 16)
                e.dma_start(out=wst_sb[:], in_=WstT.rearrange("(k p) n -> p k n", p=128)).then_inc(sem, 16)
                e.dma_start(out=wdt_sb[:], in_=WdtT.rearrange("(k p) n -> p k n", p=128)).then_inc(sem, 16)
            t_w1 = S.emit("pool", fn_w1, dma_n=3, deps=bar0,
                          writes=[b_wiu, b_wst, b_wdt])
            xfm = [sb1(f"xfm{i}", [128, KD, M], BF16) for i in range(2)]
            x2fm = sb1("x2fm", [128, KD, M], BF16)
            xnfm = sb1("xnfm", [128, KD, M], BF16)
            ufm = sb1("ufm", [128, KD, M], BF16)
            tscr = sb1("tscr", [128, M], F32)
            mrow = sb1("mrow", [1, M], F32)
            irow = sb1("irow", [1, M], F32)
            mbs = sb1("mbs", [128, M], F32)
            ibs = sb1("ibs", [128, M], F32)
            dtc = sb1("dtc", [128, M], F32)
            dtc2 = sb1("dtc2", [128, M], F32)
            dec = sb1("dec", [128, M], F32)
            ef = sb1("ef", [128, M], F32)
            lscr = sb1("lscr", [128, M], F32)
            carry = sb1("carry", [NST, 1], F32)
            b_xfm = [B(), B()]
            b_x2 = B(); b_xn1 = B(); b_u = B(); b_stat = B(); b_bc = B()
            b_dt = B(); b_dec = B(); b_ef = B(); b_l1 = B(); b_carry = B()
            b_tscr = B()

            def fn_carry0(v):
                return v.memset(carry[:], 0.0)
            t_cy = S.emit("dve", fn_carry0, deps=bar0, writes=[b_carry])

            for s in range(ST1):
                xb = xfm[s % 2]
                bxb = b_xfm[s % 2]

                def fn_lx(e, sem, s=s, xb=xb):
                    e.dma_start(out=xb[:],
                                in_=xT.rearrange("(c p) t -> p c t", p=128)[:, :, s * M:(s + 1) * M]
                                ).then_inc(sem, 16)
                t_lx = S.emit("pool", fn_lx, dma_n=1, deps=bar0, writes=[bxb])

                def fn_sq(a, xb=xb):
                    return a.activation(x2fm[:], xb[:], AF.Square)
                t_sq = S.emit("act", fn_sq, deps=[t_lx, t_tab1], reads=[bxb], writes=[b_x2])

                def fn_cs(t, xb=xb):
                    for c in range(KD):
                        t.matmul(P[4][0:1, :], ones_col[:], xb[:, c, :],
                                 start=(c == 0), stop=(c == KD - 1))
                    for c in range(KD):
                        last = t.matmul(P[5][0:1, :], ones_col[:], x2fm[:, c, :],
                                        start=(c == 0), stop=(c == KD - 1))
                    return last
                t_cs = S.emit("pe", fn_cs, deps=[t_lx, t_sq, t_ms], reads=[bxb, b_x2],
                              writes=[PB[4], PB[5]])

                def fn_stat(v):
                    v.tensor_scalar(mrow[:], P[4][0:1, :], 1.0 / D, None, ALU.mult)
                    v.tensor_scalar(irow[:], P[5][0:1, :], 1.0 / D, None, ALU.mult)
                    v.tensor_tensor(tscr[0:1, :], mrow[:], mrow[:], ALU.mult)
                    v.tensor_tensor(irow[:], irow[:], tscr[0:1, :], ALU.subtract)
                    return v.tensor_scalar(irow[:], irow[:], 1e-5, None, ALU.add)
                t_st = S.emit("dve", fn_stat, deps=[t_cs], reads=[PB[4], PB[5]],
                              writes=[b_stat, b_tscr])

                def fn_inv(a):
                    a.activation(irow[:], irow[:], AF.Ln)
                    return a.activation(irow[:], irow[:], AF.Exp, scale=-0.5)
                t_inv = S.emit("act", fn_inv, deps=[t_st], writes=[b_stat])

                def fn_bc(t):
                    t.matmul(P[4][:, :], ones_row[:], mrow[:], start=True, stop=True)
                    return t.matmul(P[5][:, :], ones_row[:], irow[:], start=True, stop=True)
                t_bc = S.emit("pe", fn_bc, deps=[t_inv, t_ms], reads=[b_stat],
                              writes=[PB[4], PB[5]])

                def fn_bc_ev(a):
                    a.activation(mbs[:], P[4][:, :], AF.Copy)
                    return a.activation(ibs[:], P[5][:, :], AF.Copy)
                t_bce = S.emit("act", fn_bc_ev, deps=[t_bc], reads=[PB[4], PB[5]], writes=[b_bc])

                if s >= 8:
                    S2i = s - 8

                    def fn_strm(t, S2i=S2i):
                        for cc in range(4):
                            t.matmul(P[6][:, cc:cc + 1], mrow[0:1, cc * 128:(cc + 1) * 128],
                                     ident1[:], start=True, stop=True)
                        for cc in range(4):
                            last = t.matmul(P[6][:, 4 + cc:5 + cc],
                                            irow[0:1, cc * 128:(cc + 1) * 128],
                                            ident1[:], start=True, stop=True)
                        return last
                    t_str = S.emit("pe", fn_strm, deps=[t_inv, t_ms], reads=[b_stat],
                                   writes=[PB[6]])

                    def fn_stre(a, S2i=S2i):
                        a.activation(mcols[:, S2i * 4:(S2i + 1) * 4], P[6][:, 0:4], AF.Copy)
                        return a.activation(icols[:, S2i * 4:(S2i + 1) * 4], P[6][:, 4:8], AF.Copy)
                    S.emit("act", fn_stre, deps=[t_str], reads=[PB[6]],
                           writes=[b_mcols, b_icols])

                def fn_xn1(v, xb=xb):
                    for c in range(KD):
                        last = v.tensor_tensor(xnfm[:, c, :], xb[:, c, :], mbs[:], ALU.subtract)
                    return last
                t_x1 = S.emit("dve", fn_xn1, deps=[t_bce], reads=[bxb, b_bc], writes=[b_xn1])

                def fn_xn2(g):
                    for c in range(KD):
                        last = g.tensor_tensor(xnfm[:, c, :], xnfm[:, c, :], ibs[:], ALU.mult)
                    return last
                t_x2 = S.emit("pool", fn_xn2, deps=[t_x1], reads=[b_bc], writes=[b_xn1])

                t_uev = []
                for e8 in range(KD):
                    pslot = e8 % 2
                    def fn_mm(t, e8=e8, pslot=pslot):
                        for k in range(KD):
                            last = t.matmul(P[pslot][:, :],
                                            wiu_sb[:, k, e8 * 128:(e8 + 1) * 128],
                                            xnfm[:, k, :], start=(k == 0), stop=(k == KD - 1))
                        return last
                    t_mm = S.emit("pe", fn_mm, deps=[t_x2, t_w1], reads=[b_xn1, b_wiu],
                                  writes=[PB[pslot]])

                    def fn_ev(a, e8=e8, pslot=pslot):
                        return a.activation(ufm[:, e8, :], P[pslot][:, :], AF.Copy)
                    t_uev.append(S.emit("act", fn_ev, deps=[t_mm], reads=[PB[pslot]],
                                        writes=[b_u]))

                def fn_bt2(t):
                    for k in range(KD):
                        t.matmul(P[2][:, :], wst_sb[:, k, :], ufm[:, k, :],
                                 start=(k == 0), stop=(k == KD - 1))
                    for k in range(KD):
                        lb = t.matmul(P[3][:, :], wdt_sb[:, k, :], ufm[:, k, :],
                                      start=(k == 0), stop=(k == KD - 1))
                    return lb
                t_bt = S.emit("pe", fn_bt2, deps=t_uev + [t_w1], reads=[b_u, b_wst, b_wdt],
                              writes=[PB[2], PB[3]])

                def fn_dta(a):
                    a.activation(dtc[:], P[3][:, :], AF.Abs, bias=bdt_sb[:])
                    a.activation(dtc2[:], P[3][:, :], AF.Relu, bias=bdt_sb[:])
                    a.activation(dtc[:], dtc[:], AF.Exp, scale=-1.0)
                    return a.activation(dtc[:], dtc[:], AF.Ln, bias=1.0)
                t_dta = S.emit("act", fn_dta, deps=[t_bt, t_rate, t_c0], reads=[PB[3], b_consts],
                               writes=[b_dt])

                def fn_dtb(v):
                    v.tensor_tensor(dtc[:], dtc[:], dtc2[:], ALU.add)
                    v.tensor_scalar(dtc[:], dtc[:], 10.0, None, ALU.min)
                    return v.tensor_scalar(dtc[:], dtc[:], rate_sb[:], None, ALU.mult)
                t_dtb = S.emit("dve", fn_dtb, deps=[t_dta], reads=[b_rate], writes=[b_dt])

                def fn_dec(a):
                    return a.activation(dec[:], dtc[:], AF.Exp, scale=-1.0)
                t_dec = S.emit("act", fn_dec, deps=[t_dtb], reads=[b_dt], writes=[b_dec])

                def fn_ef(v):
                    v.tensor_scalar(ef[:], dec[:], -1.0, 1.0, ALU.mult, ALU.add)
                    return v.tensor_tensor(ef[:], ef[:], P[2][:, :], ALU.mult)
                t_ef = S.emit("dve", fn_ef, deps=[t_dec, t_bt], reads=[b_dec, PB[2]],
                              writes=[b_ef])

                def fn_scan(v, s=s):
                    return v.tensor_tensor_scan(lscr[:], dec[:], ef[:], carry[:, 0:1],
                                                ALU.mult, ALU.add)
                t_sc = S.emit("dve", fn_scan, deps=[t_ef, t_c0], reads=[b_dec, b_ef, b_carry],
                              writes=[b_l1])

                def fn_cup(g, s=s):
                    return g.tensor_tensor(carry[:], lscr[:, M - 1:M], cmask_sb[:, s:s + 1],
                                           ALU.mult)
                t_cup = S.emit("pool", fn_cup, deps=[t_sc], reads=[b_l1, b_consts],
                               writes=[b_carry])

                if s >= 8:
                    def fn_lst(v, s=s):
                        return v.tensor_copy(l_b[:, (s - 8) * M:(s - 7) * M], lscr[:])
                    S.emit("dve", fn_lst, deps=[t_sc], reads=[b_l1], writes=[b_l])

                if debug and s == 15:
                    bt_d = sb1("bt_d", [NST, M], F32)
                    b_btd = B()
                    def fn_btd(v):
                        return v.tensor_copy(bt_d[:], P[2][:, :])
                    t_btd = S.emit("dve", fn_btd, deps=[t_bt], reads=[PB[2]], writes=[b_btd])
                    def fn_d1(e, sem):
                        e.dma_start(out=dbg["xn1"], in_=xnfm[:]).then_inc(sem, 16)
                        e.dma_start(out=dbg["u"], in_=ufm[:]).then_inc(sem, 16)
                        e.dma_start(out=dbg["dec"], in_=dec[:]).then_inc(sem, 16)
                        e.dma_start(out=dbg["ef"], in_=ef[:]).then_inc(sem, 16)
                        e.dma_start(out=dbg["dt"], in_=dtc[:]).then_inc(sem, 16)
                        e.dma_start(out=dbg["bt"], in_=bt_d[:]).then_inc(sem, 16)
                        e.dma_start(out=dbg["rate"], in_=rate_sb[:]).then_inc(sem, 16)
                    S.emit("sync", fn_d1, dma_n=7,
                           reads=[b_xn1, b_u, b_dec, b_ef, b_dt, b_btd, b_rate])

            if debug:
                def fn_dl(e, sem):
                    e.dma_start(out=dbg["l"], in_=l_b[:]).then_inc(sem, 16)
                S.emit("sync", fn_dl, dma_n=1, reads=[b_l])
                def fn_dm(e, sem):
                    e.dma_start(out=dbg["mcols"], in_=mcols[:]).then_inc(sem, 16)
                    e.dma_start(out=dbg["icols"], in_=icols[:]).then_inc(sem, 16)
                S.emit("sync", fn_dm, dma_n=2, reads=[b_mcols, b_icols])

        bar1 = S.barrier_tokens()

        def fn_tab_silu(a):
            return a.add_instruction(mybir.InstLoadActFuncSet(
                name=nc.get_next_instruction_name(), act_func_set_id=_siluset,
                ins=[], outs=[]))
        t_tab2 = S.emit("act", fn_tab_silu, deps=bar1)

        with contextlib.ExitStack() as p2ctx:
            sb2 = lambda name, shape, dt: p2ctx.enter_context(nc.sbuf_tensor(name, shape, dt))
            xtm_t = [sb2(f"xtm{i}", [128, D], F32) for i in range(2)]
            xntm = sb2("xntm", [128, 4, D], F32)
            xnb_t = sb2("xnb_t", [128, D], BF16)
            xq_tm = sb2("xq_tm", [128, D], BF16)
            xnfm2 = sb2("xnfm2", [128, KD, M], BF16)
            xqfm = sb2("xqfm", [128, KD, M], BF16)
            sgtm = sb2("sgtm", [128, D], BF16)
            stm = sb2("stm", [128, 4, D], BF16)
            ztmp = sb2("ztmp", [128, D], F32)
            xnd = sb2("xnd", [128, D], F32)
            zb_t = xnb_t
            zfm = sb2("zfm", [128, KD, 128], BF16)
            h_all = [sb2(f"h_all{i}", [128, 4 * 512], F32) for i in range(2)]
            sG = [sb2(f"sG{i}", [128, 512], F32) for i in range(2)]
            hq_tm = sb2("hq_tm", [128, 4 * 512], BF16)
            hq_fm = sb2("hq_fm", [128, KI, M], BF16)
            wqg6 = sb2("wqg6", [128, KD, 512], BF16)
            wqu6 = sb2("wqu6", [128, KD, 512], BF16)
            wqd_sb = [sb2(f"wqd{i}", [128, 512], BF16) for i in range(2)]
            gsc = sG[0]
            xre = [sb2(f"xre{i}", [128, 512], F32) for i in range(2)]
            outb = [sb2(f"outb{i}", [128, 512], F32) for i in range(2)]
            st12 = sb2("st12", [128, 4, 12], F32)
            st12c = sb2("st12c", [128, 4, 12], F32)
            mvagg = sb2("mvagg", [128, 4, 2], F32)
            mvc = sb2("mvc", [128, 4, 2], F32)
            vpad = sb2("vpad", [128, 64], F32)
            ipad = sb2("ipad", [128, 64], F32)
            ipc = sb2("ipc", [128, 64], F32)
            b_st12 = B(); b_st12c = B(); b_mv = B(); b_mvc = B()
            b_vpad = B(); b_ipad = B(); b_ipc = B()
            fdbg = xre[0]

            b_xtm = [B(), B()]
            b_xntm = B(); b_xnb = B(); b_xqtm = B(); b_xnfm2 = B(); b_xqfm = B()
            b_z = b_xnb
            b_sg = B(); b_stm = B(); b_zfm = B(); b_hall = [B(), B()]; b_sG = [B(), B()]
            b_hqtm = B(); b_hqfm = B(); b_wg6 = B(); b_wu6 = B()
            b_wqdb = [B(), B()]
            b_gsc = b_sG[0]; b_xre = [B(), B()]; b_outb = [B(), B()]; b_ztmp = B(); b_xnd = B()
            b_fdbg = b_xre[0]
            b_out = B()

            SIX = [(0, 512), (512, 512), (1024, 512), (1536, 512), (2048, 512), (2560, 256)]

            t_out_last = None
            for S2 in range(st2_limit):
                T0 = S2 * M
                if serialize:
                    bar1 = S.barrier_tokens()
                t_sub = []
                for j in range(4):
                    xb = xtm_t[j % 2]; bxb = b_xtm[j % 2]

                    def fn_lx(e, sem, j=j, xb=xb, T0=T0):
                        e.dma_start(out=xb[:], in_=x_tm[T0 + j * 128:T0 + (j + 1) * 128, :]
                                    ).then_inc(sem, 16)
                    t_lx = S.emit("sync", fn_lx, dma_n=1, deps=bar1, writes=[bxb])

                    def fn_bn(v, j=j, xb=xb):
                        v.bn_stats(st12[:, j, 0:6], xb[:, 0:512])
                        return v.bn_stats(st12[:, j, 6:12], xb[:, 512:1024])
                    t_bn = S.emit("dve", fn_bn, deps=[t_lx] + bar1, reads=[bxb],
                                  writes=[b_st12])

                    def fn_stc(a, j=j):
                        return a.activation(st12c[:, j, :], st12[:, j, :], AF.Copy)
                    t_stc = S.emit("act", fn_stc, deps=[t_bn], reads=[b_st12],
                                   writes=[b_st12c])

                    def fn_ag(v, j=j):
                        return v.bn_aggr(mvagg[:, j, :], st12c[:, j, :])
                    t_ag = S.emit("dve", fn_ag, deps=[t_stc], reads=[b_st12c],
                                  writes=[b_mv])

                    def fn_mvc(a, j=j):
                        return a.activation(mvc[:, j, :], mvagg[:, j, :], AF.Copy)
                    t_mvc = S.emit("act", fn_mvc, deps=[t_ag], reads=[b_mv], writes=[b_mvc])

                    def fn_sub(v, j=j, xb=xb):
                        return v.tensor_scalar(xntm[:, j, :], xb[:], mvc[:, j, 0:1], None,
                                               ALU.subtract)
                    t_sub.append(S.emit("dve", fn_sub, deps=[t_mvc, t_lx],
                                        reads=[bxb, b_mvc], writes=[b_xntm]))

                def fn_gather(v):
                    return v.tensor_scalar(vpad[:, 0:4],
                                           mvc[:].rearrange("p a b -> p (a b)")[:, 1:8:2],
                                           1e-5, None, ALU.add)
                t_ga = S.emit("dve", fn_gather, reads=[b_mvc], writes=[b_vpad])

                def fn_sqrtw(a):
                    a.add_instruction(mybir.InstLoadActFuncSet(
                        name=nc.get_next_instruction_name(), act_func_set_id=_sqrtset,
                        ins=[], outs=[]))
                    a.activation(vpad[:], vpad[:], AF.Sqrt)
                    return a.add_instruction(mybir.InstLoadActFuncSet(
                        name=nc.get_next_instruction_name(), act_func_set_id=_siluset,
                        ins=[], outs=[]))
                t_sq2 = S.emit("act", fn_sqrtw, deps=[t_ga], reads=[b_vpad], writes=[b_vpad])

                def fn_rcp(v):
                    return v.reciprocal(ipad[:], vpad[:])
                t_rcp = S.emit("dve", fn_rcp, deps=[t_sq2], reads=[b_vpad], writes=[b_ipad])

                def fn_ipc(a):
                    return a.activation(ipc[:], ipad[:], AF.Copy)
                t_ipc = S.emit("act", fn_ipc, deps=[t_rcp], reads=[b_ipad], writes=[b_ipc])

                t_xnA = []
                for j in range(4):
                    def fn_xn(v, j=j):
                        return v.tensor_scalar(xntm[:, j, :], xntm[:, j, :], ipc[:, j:j + 1],
                                               None, ALU.mult)
                    t_xn = S.emit("dve", fn_xn, deps=[t_ipc, t_sub[j]] + bar1,
                                  reads=[b_ipc], writes=[b_xntm])
                    t_xnA.append(t_xn)

                    def fn_cast(v, j=j):
                        return v.tensor_copy(xnb_t[:], xntm[:, j, :])
                    t_cb = S.emit("dve", fn_cast, deps=[t_xn], reads=[b_xntm], writes=[b_xnb])

                    def fn_trx(t, j=j):
                        for c in range(KD):
                            pp = P[6] if c < 4 else P[7]
                            last = t.matmul(pp[:, (c % 4) * 128:(c % 4 + 1) * 128],
                                            xnb_t[:, c * 128:(c + 1) * 128], ident_sb[:],
                                            start=True, stop=True)
                        return last
                    t_trx = S.emit("pe", fn_trx, deps=[t_cb], reads=[b_xnb, b_consts],
                                   writes=[PB[6], PB[7]])

                    def fn_evx(a, j=j):
                        a.activation(xnfm2[:, 0:4, j * 128:(j + 1) * 128],
                                     P[6][:, :].rearrange("p (c t) -> p c t", t=128), AF.Copy)
                        return a.activation(xnfm2[:, 4:8, j * 128:(j + 1) * 128],
                                            P[7][:, :].rearrange("p (c t) -> p c t", t=128),
                                            AF.Copy)
                    S.emit("act", fn_evx, deps=[t_trx, t_tab2], reads=[PB[6], PB[7]],
                           writes=[b_xnfm2])

                for j in range(4):
                    tq = _emit_quant(S, xntm[:, j, :], 1024, 64,
                                     xq_tm[:], q,
                                     deps=[t_xnA[j]],
                                     reads=[b_xntm], writes=[b_xqtm])

                    def fn_trq(t, j=j):
                        for c in range(KD):
                            pp = P[6] if c < 4 else P[7]
                            last = t.matmul(pp[:, (c % 4) * 128:(c % 4 + 1) * 128],
                                            xq_tm[:, c * 128:(c + 1) * 128],
                                            ident_sb[:], start=True, stop=True)
                        return last
                    t_trq = S.emit("pe", fn_trq, deps=[tq], reads=[b_xqtm, b_consts],
                                   writes=[PB[6], PB[7]])

                    def fn_evq(a, j=j):
                        a.activation(xqfm[:, 0:4, j * 128:(j + 1) * 128],
                                     P[6][:, :].rearrange("p (c t) -> p c t", t=128), AF.Copy)
                        return a.activation(xqfm[:, 4:8, j * 128:(j + 1) * 128],
                                            P[7][:, :].rearrange("p (c t) -> p c t", t=128),
                                            AF.Copy)
                    S.emit("act", fn_evq, deps=[t_trq], reads=[PB[6], PB[7]], writes=[b_xqfm])

                for j in range(4):
                    def fn_g(t, j=j):
                        for k in range(KD):
                            t.matmul(P[0][:, :], xnfm2[:, k, j * 128:(j + 1) * 128],
                                     wing_sb[:, k, 0:512], start=(k == 0), stop=(k == KD - 1))
                        for k in range(KD):
                            last = t.matmul(P[1][:, :], xnfm2[:, k, j * 128:(j + 1) * 128],
                                            wing_sb[:, k, 512:1024], start=(k == 0),
                                            stop=(k == KD - 1))
                        return last
                    t_g = S.emit("pe", fn_g, deps=[t_w2], reads=[b_xnfm2, b_wing],
                                 writes=[PB[0], PB[1]])

                    def fn_sg(a, j=j):
                        a.activation(sgtm[:, 0:512], P[0][:, :], AF.Silu)
                        return a.activation(sgtm[:, 512:1024], P[1][:, :], AF.Silu)
                    t_sg = S.emit("act", fn_sg, deps=[t_g, t_tab2], reads=[PB[0], PB[1]],
                                  writes=[b_sg])

                    def fn_y(t, j=j, T0=T0):
                        t.matmul(P[2][:, :], l_b[:, T0 + j * 128:T0 + (j + 1) * 128],
                                 wct_sb[:, 0:512], start=True, stop=True)
                        return t.matmul(P[3][:, :], l_b[:, T0 + j * 128:T0 + (j + 1) * 128],
                                        wct_sb[:, 512:1024], start=True, stop=True)
                    t_y = S.emit("pe", fn_y, deps=[t_w2] + bar1, reads=[b_l, b_wct],
                                 writes=[PB[2], PB[3]])

                    def fn_xnd(g, j=j):
                        return g.tensor_tensor(xnd[:], xntm[:, j, :], dbc[:], ALU.mult)
                    t_xd = S.emit("pool", fn_xnd, deps=[t_dbc] + bar1, reads=[b_xntm, b_dbc],
                                  writes=[b_xnd])

                    def fn_z1(v, j=j):
                        v.tensor_tensor(ztmp[:, 0:512], sgtm[:, 0:512], P[2][:, :], ALU.mult)
                        v.tensor_tensor(ztmp[:, 512:1024], sgtm[:, 512:1024], P[3][:, :],
                                        ALU.mult)
                        return v.tensor_tensor(zb_t[:], ztmp[:], xnd[:], ALU.add)
                    t_z = S.emit("dve", fn_z1, deps=[t_sg, t_y, t_xd],
                                 reads=[b_sg, PB[2], PB[3], b_xnd], writes=[b_z, b_ztmp])

                    def fn_trz(t, j=j):
                        for c in range(KD):
                            pp = P[6] if c < 4 else P[7]
                            last = t.matmul(pp[:, (c % 4) * 128:(c % 4 + 1) * 128],
                                            zb_t[:, c * 128:(c + 1) * 128], ident_sb[:],
                                            start=True, stop=True)
                        return last
                    t_trz = S.emit("pe", fn_trz, deps=[t_z], reads=[b_z, b_consts],
                                   writes=[PB[6], PB[7]])

                    def fn_evz(a, j=j):
                        a.activation(zfm[:, 0:4, :],
                                     P[6][:, :].rearrange("p (c t) -> p c t", t=128), AF.Copy)
                        return a.activation(zfm[:, 4:8, :],
                                            P[7][:, :].rearrange("p (c t) -> p c t", t=128),
                                            AF.Copy)
                    t_evz = S.emit("act", fn_evz, deps=[t_trz], reads=[PB[6], PB[7]],
                                   writes=[b_zfm])

                    def fn_s(t, j=j):
                        for k in range(KD):
                            t.matmul(P[4][:, :], zfm[:, k, :], wout_sb[:, k, 0:512],
                                     start=(k == 0), stop=(k == KD - 1))
                        for k in range(KD):
                            last = t.matmul(P[5][:, :], zfm[:, k, :], wout_sb[:, k, 512:1024],
                                            start=(k == 0), stop=(k == KD - 1))
                        return last
                    t_s = S.emit("pe", fn_s, deps=[t_evz, t_w2], reads=[b_zfm, b_wout],
                                 writes=[PB[4], PB[5]])

                    def fn_evs(a, j=j):
                        a.activation(stm[:, j, 0:512], P[4][:, :], AF.Copy)
                        return a.activation(stm[:, j, 512:1024], P[5][:, :], AF.Copy)
                    S.emit("act", fn_evs, deps=[t_s], reads=[PB[4], PB[5]], writes=[b_stm])

                if debug:
                    if S2 == 0:
                        def fn_dxn2(e, sem):
                            e.dma_start(out=dbg["xn2"], in_=xntm[:]).then_inc(sem, 16)
                        S.emit("sync", fn_dxn2, dma_n=1, reads=[b_xntm])
                    def fn_ds(e, sem, T0=T0):
                        e.dma_start(out=dbg["s"][T0:T0 + M, :].rearrange("(a p) b -> p a b", p=128),
                                    in_=stm[:]).then_inc(sem, 16)
                    S.emit("sync", fn_ds, dma_n=1, reads=[b_stm])


                for si, (e0, ew) in enumerate(SIX):
                    hb = h_all[si % 2]; bhb = b_hall[si % 2]
                    sGb = sG[si % 2]; bsGb = b_sG[si % 2]
                    def fn_lw6(e, sem, e0=e0, ew=ew):
                        e.dma_start(out=wqg6[:, :, 0:ew],
                                    in_=wqgT.rearrange("(k p) e -> p k e", p=128)[:, :, e0:e0 + ew]
                                    ).then_inc(sem, 16)
                        e.dma_start(out=wqu6[:, :, 0:ew],
                                    in_=wquT.rearrange("(k p) e -> p k e", p=128)[:, :, e0:e0 + ew]
                                    ).then_inc(sem, 16)
                    t_lw = S.emit("sync", fn_lw6, dma_n=2,
                                  deps=[b_wqg.wtok, b_wqu.wtok] + bar1,
                                  writes=[b_wg6, b_wu6])

                    t_hj = []
                    for j in range(4):
                        pg, pu = (P[0], P[1]) if j % 2 == 0 else (P[2], P[3])
                        bg, bu = (PB[0], PB[1]) if j % 2 == 0 else (PB[2], PB[3])

                        def fn_gu(t, j=j, ew=ew, pg=pg, pu=pu):
                            for k in range(KD):
                                t.matmul(pg[:, 0:ew], xqfm[:, k, j * 128:(j + 1) * 128],
                                         wqg6[:, k, 0:ew], start=(k == 0), stop=(k == KD - 1))
                            for k in range(KD):
                                last = t.matmul(pu[:, 0:ew], xqfm[:, k, j * 128:(j + 1) * 128],
                                                wqu6[:, k, 0:ew], start=(k == 0),
                                                stop=(k == KD - 1))
                            return last
                        t_gu = S.emit("pe", fn_gu, deps=[t_lw], reads=[b_xqfm, b_wg6, b_wu6],
                                      writes=[bg, bu])

                        def fn_sG(a, j=j, ew=ew, pg=pg, sGb=sGb):
                            return a.activation(sGb[:, 0:ew], pg[:, 0:ew], AF.Silu)
                        t_sG = S.emit("act", fn_sG, deps=[t_gu, t_tab2], reads=[bg],
                                      writes=[bsGb])

                        def fn_h(v, j=j, ew=ew, pu=pu, hb=hb, sGb=sGb):
                            return v.tensor_tensor(hb[:, j * ew:(j + 1) * ew], sGb[:, 0:ew],
                                                   pu[:, 0:ew], ALU.mult)
                        t_h = S.emit("dve", fn_h, deps=[t_sG], reads=[bsGb, bu],
                                     writes=[bhb])
                        t_hj.append(t_h)

                    tqh = None
                    for off in range(0, 4 * ew, 1024):
                        w6 = min(1024, 4 * ew - off)
                        tqh = _emit_quant(S, hb[:, off:off + w6], w6, w6 // 16,
                                          hq_tm[:, off:off + w6], q, deps=t_hj,
                                          reads=[bhb], writes=[b_hqtm])

                    nct = ew // 128
                    for j in range(4):
                        def fn_trh(t, j=j, ew=ew, nct=nct):
                            for c in range(nct):
                                last = t.matmul(P[6][:, c * 128:(c + 1) * 128],
                                                hq_tm[:, j * ew + c * 128:j * ew + (c + 1) * 128],
                                                ident_sb[:], start=True, stop=True)
                            return last
                        t_trh = S.emit("pe", fn_trh, deps=[tqh], reads=[b_hqtm, b_consts],
                                       writes=[PB[6]])

                        def fn_evh(a, j=j, e0=e0, nct=nct):
                            return a.activation(
                                hq_fm[:, e0 // 128:e0 // 128 + nct, j * 128:(j + 1) * 128],
                                P[6][:, 0:nct * 128].rearrange("p (c t) -> p c t", t=128),
                                AF.Copy)
                        S.emit("act", fn_evh, deps=[t_trh], reads=[PB[6]], writes=[b_hqfm])

                for dh in range(2):
                    PF = [P[4 * dh + 0], P[4 * dh + 1], P[4 * dh + 2], P[4 * dh + 3]]
                    PBF = [PB[4 * dh + 0], PB[4 * dh + 1], PB[4 * dh + 2], PB[4 * dh + 3]]
                    t_fp = [None] * 4
                    for k in range(KI):
                        wb = wqd_sb[k % 2]; bwb = b_wqdb[k % 2]

                        def fn_lwd(e, sem, k=k, dh=dh, wb=wb):
                            e.dma_start(out=wb[:],
                                        in_=wqdT[k * 128:(k + 1) * 128, dh * 512:(dh + 1) * 512]
                                        ).then_inc(sem, 16)
                        t_lwd = S.emit("sync", fn_lwd, dma_n=1, deps=[b_wqd.wtok] + bar1,
                                       writes=[bwb])

                        for j in range(4):
                            def fn_f(t, k=k, j=j, wb=wb, PF=PF):
                                return t.matmul(PF[j][:, :], hq_fm[:, k, j * 128:(j + 1) * 128],
                                                wb[:], start=(k == 0), stop=(k == KI - 1))
                            t_fp[j] = S.emit("pe", fn_f, deps=[t_lwd], reads=[b_hqfm, bwb],
                                             writes=[PBF[j]])

                    for j in range(4):
                        tap = []
                        if debug and S2 == st2_limit - 1 and dh == 1 and j == 3:
                            def fn_fd(v, j=j, PF=PF):
                                return v.tensor_copy(fdbg[:], PF[j][:, :])
                            t_fd0 = S.emit("dve", fn_fd, deps=[t_fp[j]], reads=[PBF[j]],
                                           writes=[b_fdbg])
                            tap.append(t_fd0)
                            def fn_fdd(e, sem):
                                e.dma_start(out=dbg["fps"], in_=fdbg[:]).then_inc(sem, 16)
                            S.emit("sync", fn_fdd, dma_n=1, deps=[t_fd0], reads=[b_fdbg])
                        def fn_gsc(a, j=j, PF=PF):
                            return a.activation(gsc[:], PF[j][:, :], AF.Silu)
                        t_gs = S.emit("act", fn_gsc, deps=[t_fp[j], t_tab2] + tap,
                                      reads=[PBF[j]], writes=[b_gsc])

                        cpar = j % 2
                        xreb, bxreb = xre[cpar], b_xre[cpar]
                        outbb, boutbb = outb[cpar], b_outb[cpar]

                        def fn_lx2(e, sem, j=j, dh=dh, T0=T0, xreb=xreb):
                            e.dma_start(out=xreb[:],
                                        in_=x_tm[T0 + j * 128:T0 + (j + 1) * 128,
                                                 dh * 512:(dh + 1) * 512]).then_inc(sem, 16)
                        t_x2l = S.emit("sync", fn_lx2, dma_n=1, deps=bar1, writes=[bxreb])

                        def fn_comb(v, j=j, dh=dh, xreb=xreb, outbb=outbb):
                            v.tensor_tensor(outbb[:], gsc[:], stm[:, j, dh * 512:(dh + 1) * 512],
                                            ALU.mult)
                            return v.tensor_tensor(outbb[:], outbb[:], xreb[:], ALU.add)
                        t_cb2 = S.emit("dve", fn_comb, deps=[t_gs, t_x2l],
                                       reads=[b_gsc, b_stm, bxreb], writes=[boutbb])

                        def fn_so(e, sem, j=j, dh=dh, T0=T0, outbb=outbb):
                            e.dma_start(out=out[T0 + j * 128:T0 + (j + 1) * 128,
                                                dh * 512:(dh + 1) * 512],
                                        in_=outbb[:]).then_inc(sem, 16)
                        t_out_last = S.emit("sync", fn_so, dma_n=1, reads=[boutbb],
                                            writes=[b_out])

            if debug:
                bar_end = S.barrier_tokens()
                f2w = wqd_sb[0]
                t_prev = None
                for k in range(KI):
                    def fn_l2(e, sem, k=k):
                        e.dma_start(out=f2w[:],
                                    in_=wqdT[k * 128:(k + 1) * 128, 512:1024]).then_inc(sem, 16)
                    t_l2 = S.emit("sync", fn_l2, dma_n=1, deps=bar_end, writes=[b_wqdb[0]])
                    def fn_m2(t, k=k):
                        return t.matmul(P[5][:, :], hq_fm[:, k, 3 * 128:4 * 128], f2w[:],
                                        start=(k == 0), stop=(k == KI - 1))
                    t_prev = S.emit("pe", fn_m2, deps=[t_l2] + bar_end,
                                    reads=[b_hqfm, b_wqdb[0]], writes=[PB[5]])
                def fn_e2(v):
                    return v.tensor_copy(fdbg[:], P[5][:, :])
                t_e2 = S.emit("dve", fn_e2, deps=[t_prev], reads=[PB[5]], writes=[b_fdbg])
                def fn_d2(e, sem):
                    e.dma_start(out=dbg["f2"], in_=fdbg[:]).then_inc(sem, 16)
                    e.dma_start(out=dbg["wqd"], in_=wqdT).then_inc(sem, 16)
                S.emit("sync", fn_d2, dma_n=2, deps=[t_e2], reads=[b_fdbg, b_wqd])

                def fn_dfin(e, sem):
                    e.dma_start(out=dbg["hqfm"], in_=hq_fm[:]).then_inc(sem, 16)
                    e.dma_start(out=dbg["xqfm"], in_=xqfm[:]).then_inc(sem, 16)
                    e.dma_start(out=dbg["sgtm"], in_=sgtm[:]).then_inc(sem, 16)
                    e.dma_start(out=dbg["gsc"], in_=gsc[:]).then_inc(sem, 16)
                    e.dma_start(out=dbg["mvc"], in_=mvc[:].rearrange("p a b -> p (a b)")).then_inc(sem, 16)
                    e.dma_start(out=dbg["ipc"], in_=ipc[:]).then_inc(sem, 16)
                    e.dma_start(out=dbg["xnf"], in_=xntm[:]).then_inc(sem, 16)
                    e.dma_start(out=dbg["outb"], in_=outb[1][:]).then_inc(sem, 16)
                S.emit("sync", fn_dfin, dma_n=8,
                       reads=[b_hqfm, b_xqfm, b_sg, b_gsc, b_outb[1], b_mvc, b_ipc, b_xntm],
                       deps=S.barrier_tokens())

            def fn_fin(v):
                return v.engine_nop()
            S.emit("dve", fn_fin, deps=[t_out_last] + S.barrier_tokens())

        with nc.Block() as block:
            @block.tensor
            def _(eng):
                for st in S.steps["pe"]:
                    st(eng)

            @block.vector
            def _(eng):
                for st in S.steps["dve"]:
                    st(eng)

            @block.scalar
            def _(eng):
                for st in S.steps["act"]:
                    st(eng)

            @block.gpsimd
            def _(eng):
                for st in S.steps["pool"]:
                    st(eng)

            @block.sync
            def _(eng):
                for st in S.steps["sync"]:
                    st(eng)

    return nc


def _in_maps(inp):
    x = np.asarray(inp["L0_attn_x"], dtype=np.float32)
    W_in = np.asarray(inp["L0_attn_W_in"], dtype=np.float32)
    W_state = np.asarray(inp["L0_attn_W_state"], dtype=np.float32)
    W_dt = np.asarray(inp["L0_attn_W_dt"], dtype=np.float32)
    b_dt = np.asarray(inp["L0_attn_b_dt"], dtype=np.float32)
    A_log = np.asarray(inp["L0_attn_A_log"], dtype=np.float32)
    W_C = np.asarray(inp["L0_attn_W_C"], dtype=np.float32)
    Dv = np.asarray(inp["L0_attn_D"], dtype=np.float32)
    W_out = np.asarray(inp["L0_attn_W_out"], dtype=np.float32)
    w_gate = np.asarray(inp["L0_ffn_w_gate"], dtype=np.float32)
    w_up = np.asarray(inp["L0_ffn_w_up"], dtype=np.float32)
    w_down = np.asarray(inp["L0_ffn_w_down"], dtype=np.float32)

    import ml_dtypes
    WiuT = np.ascontiguousarray(W_in.T[:, :D])
    WigT = np.ascontiguousarray(W_in.T[:, D:])
    WstT = np.ascontiguousarray(W_state.T)
    WdtT = np.ascontiguousarray(W_dt.T)
    WcT = np.ascontiguousarray(W_C.T)
    WoT = np.ascontiguousarray(W_out.T)
    ident = np.eye(128, dtype=ml_dtypes.bfloat16)
    bdtc = np.ascontiguousarray(b_dt.reshape(NST, 1))
    alogc = np.ascontiguousarray(A_log.reshape(NST, 1))
    dvc = np.ascontiguousarray(Dv.reshape(1, D))

    maps = []
    for core in range(8):
        b, half = core // 2, core % 2
        xb = x[b]
        if half == 0:
            xroll = np.concatenate([xb[TOK:], xb[:TOK]], axis=0)
        else:
            xroll = xb
        cm = np.ones((128, ST1), np.float32)
        if half == 0:
            cm[:, 7] = 0.0
        maps.append({
            "xT": np.ascontiguousarray(xroll.T),
            "x_tm": np.ascontiguousarray(xb[half * TOK:(half + 1) * TOK]),
            "WiuT": WiuT, "WigT": WigT, "WstT": WstT, "WdtT": WdtT,
            "WcT": WcT, "WoT": WoT,
            "wg": w_gate, "wu": w_up, "wd": w_down,
            "bdt": bdtc, "alog": alogc, "dvec": dvc,
            "identI": ident, "cmask": cm,
        })
    return maps


def kernel(**inp):
    maps = _in_maps(inp)
    nc = build_nc(debug=False)
    res = run_bass_kernel_spmd(nc, maps, core_ids=list(range(8)))
    out = np.empty((4, TOKF, D), dtype=np.float32)
    for core in range(8):
        b, half = core // 2, core % 2
        out[b, half * TOK:(half + 1) * TOK] = res.results[core]["out"]
    return out
